# version 26
# baseline (speedup 1.0000x reference)
import numpy as np
import concourse.bass as bass
import concourse.mybir as mybir
from concourse.bass_utils import run_bass_kernel_spmd

F32 = mybir.dt.float32
BF16 = mybir.dt.bfloat16
I32 = mybir.dt.int32
AF = mybir.ActivationFunctionType
AX = mybir.AxisListType
ALU = mybir.AluOpType

NCORE = 8
SEQ = 4
N = 2048
D = 1024
NH = 16
E = 64
HF = 4096
TC = 512
DBG = False
import os
BI_SEQ = int(os.environ.get("BI_SEQ", "4"))
BI_PAIR = int(os.environ.get("BI_PAIR", "8"))
BI_STAGE = int(os.environ.get("BI_STAGE", "9"))
BI_PREP = int(os.environ.get("BI_PREP", "9"))
NOALT = int(os.environ.get("NOALT", "0"))
QPN = int(os.environ.get("QPN", "4"))
P1M = int(os.environ.get("P1M", "2"))
W1N = int(os.environ.get("W1N", "32"))
W2N = int(os.environ.get("W2N", "8"))
WPN = int(os.environ.get("WPN", "8"))
BI_PREP = int(os.environ.get("BI_PREP", "9"))
NOALT = int(os.environ.get("NOALT", "0"))
QPN = int(os.environ.get("QPN", "4"))
P1M = int(os.environ.get("P1M", "2"))


class Ch:

    def __init__(self, es, nc, name):
        self.sem = es.enter_context(nc.semaphore(name))
        self.n = 0

    def inc(self, inst, k=1):
        inst.then_inc(self.sem, k)
        self.n += k
        return self.n


def build_nc():
    import contextlib
    nc = bass.Bass("TRN2", target_bir_lowering=False)

    x_in = nc.dram_tensor("x", [SEQ, N, D], F32, kind="ExternalInput").ap()
    wq_in = nc.dram_tensor("wq", [NH, E, E], F32, kind="ExternalInput").ap()
    wp_in = nc.dram_tensor("wp", [D, D], F32, kind="ExternalInput").ap()
    bp_in = nc.dram_tensor("bp", [D], F32, kind="ExternalInput").ap()
    w1_in = nc.dram_tensor("w1", [HF, D], F32, kind="ExternalInput").ap()
    b1_in = nc.dram_tensor("b1", [HF], F32, kind="ExternalInput").ap()
    w2_in = nc.dram_tensor("w2", [D, HF], F32, kind="ExternalInput").ap()
    b2_in = nc.dram_tensor("b2", [D], F32, kind="ExternalInput").ap()
    out_d = nc.dram_tensor("out", [SEQ, N, D], F32, kind="ExternalOutput").ap()

    bw1t_d = nc.dram_tensor("bw1t_d", [HF // 128, 128, 1024], BF16, kind="Internal").ap()
    bw2t_d = nc.dram_tensor("bw2t_d", [D // 128, 128, 4096], BF16, kind="Internal").ap()
    wpt_d = nc.dram_tensor("wpt_d", [8, 128, 1024], BF16, kind="Internal").ap()

    dbg = {}
    if DBG:
        dbg["xnf"] = nc.dram_tensor("dbg_xnf", [128, 16384], BF16, kind="ExternalOutput").ap()
        dbg["resf"] = nc.dram_tensor("dbg_resf", [128, 16384], BF16, kind="ExternalOutput").ap()
        dbg["qA"] = nc.dram_tensor("dbg_qA", [128, 2048], BF16, kind="ExternalOutput").ap()
        dbg["qT"] = nc.dram_tensor("dbg_qT", [128, 2112], BF16, kind="ExternalOutput").ap()
        dbg["mxc"] = nc.dram_tensor("dbg_mxc", [128, 32], F32, kind="ExternalOutput").ap()
        dbg["pt"] = nc.dram_tensor("dbg_pt", [128, 1024], BF16, kind="ExternalOutput").ap()
        dbg["bw1t"] = nc.dram_tensor("dbg_bw1t", [128, 1024], BF16, kind="ExternalOutput").ap()
        dbg["hsb"] = nc.dram_tensor("dbg_hsb", [128, 16384], BF16, kind="ExternalOutput").ap()

    es = contextlib.ExitStack()
    with es:
        EC = es.enter_context
        xnf = EC(nc.sbuf_tensor("xnf", [128, 16384], BF16))
        resf = EC(nc.sbuf_tensor("resf", [128, 16384], BF16))
        gfm = EC(nc.sbuf_tensor("gfm", [128, 16384], BF16))
        wqt = EC(nc.sbuf_tensor("wqt", [128, 512], BF16))
        bpc = EC(nc.sbuf_tensor("bpc", [128, 8], F32))
        b1c = EC(nc.sbuf_tensor("b1c", [128, 32], F32))
        b2c = EC(nc.sbuf_tensor("b2c", [128, 8], F32))
        identb = EC(nc.sbuf_tensor("identb", [128, 128], BF16))
        identf = EC(nc.sbuf_tensor("identf", [128, 128], F32))
        PS0 = EC(nc.psum_tensor("PS0", [128, 1024], F32))
        PS1 = EC(nc.psum_tensor("PS1", [128, 1024], F32))
        PS2 = EC(nc.psum_tensor("PS2", [128, 1024], F32))
        PS3 = EC(nc.psum_tensor("PS3", [128, 1024], F32))
        PS2B = PS2.bitcast(BF16)
        PS3B = PS3.bitcast(BF16)
        PS0B = PS0.bitcast(BF16)
        PS1B = PS1.bitcast(BF16)

        block = EC(nc.Block())

        ch = {n: Ch(es, nc, n) for n in [
            "ldp", "bwout", "qp", "s1", "mx", "mxrow", "p2", "exp", "av",
            "divd", "rest", "resf", "lh", "lw1", "lw2", "lxr", "h", "hs",
            "gy", "gys", "sg", "sgt", "ost", "oout", "lnx", "lnst", "lnt",
            "lnf", "misc", "mrow", "vch", "ach", "ldpA", "ldpB", "lnxA", "lnxB",
            "lw1A", "lw1B", "lw2A", "lw2B", "ooutA", "ooutB",
            "yc", "sev", "gyh", "gact", "lwpA", "lwpB", "lxrA", "lxrB"]}

        def vedge(v, inst):
            inst.then_inc(ch["vch"].sem, 1)
            ch["vch"].n += 1
            v.wait_ge(ch["vch"].sem, ch["vch"].n)

        def aedge(a, inst):
            inst.then_inc(ch["ach"].sem, 1)
            ch["ach"].n += 1
            a.wait_ge(ch["ach"].sem, ch["ach"].n)

        with nc.sbuf_tensor("pstage0", [128, 4096], F32) as pstage0, \
             nc.sbuf_tensor("pstage1", [128, 4096], F32) as pstage1, \
             nc.sbuf_tensor("psgn", [128, 4096], BF16) as psgn, \
             nc.sbuf_tensor("psgn2", [128, 4096], BF16) as psgn2, \
             nc.sbuf_tensor("ptout", [128, 4096], BF16) as ptout, \
             nc.sbuf_tensor("palph", [128, 4], F32) as palph, \
             nc.sbuf_tensor("prow", [128, 128], F32) as prow, \
             nc.sbuf_tensor("iscr", [128, 128], I32) as iscr:

            pstages = [pstage0, pstage1]

            def _prep_gpsimd(g):
                g.iota(iscr[:, :], pattern=[[1, 128]], base=0,
                       channel_multiplier=-1).then_inc(ch["misc"].sem, 1)
            ch["misc"].n += 1
            block.gpsimd(_prep_gpsimd)

            m_iota = ch["misc"].n

            def _prep_vec0(v):
                v.wait_ge(ch["misc"].sem, m_iota)
                v.tensor_single_scalar(out=identb[:, :], in_=iscr[:, :], scalar=0,
                                       op=ALU.is_equal)
                v.tensor_single_scalar(out=identf[:, :], in_=iscr[:, :], scalar=0,
                                       op=ALU.is_equal).then_inc(ch["misc"].sem, 1)
            ch["misc"].n += 1
            block.vector(_prep_vec0)
            m_ident = ch["misc"].n
            m_biaspe_g = [m_ident]

            if BI_PREP >= 2:
                def _prep_bias_sync(s):
                    s.dma_start(out=prow[0:8, 0:128], in_=bp_in.rearrange("(a b) -> a b", b=128)).then_inc(ch["ldp"].sem, 16)
                    s.wait_ge(ch["misc"].sem, 1)
                    s.dma_start(out=prow[64:72, 0:128], in_=b2_in.rearrange("(a b) -> a b", b=128)).then_inc(ch["ldp"].sem, 16)
                    s.wait_ge(ch["misc"].sem, 1)
                    s.dma_start(out=prow[32:64, 0:128], in_=b1_in.rearrange("(a b) -> a b", b=128)).then_inc(ch["ldp"].sem, 16)
                    s.wait_ge(ch["misc"].sem, 1)
                ch["ldp"].n += 48
                block.sync(_prep_bias_sync)

                ldp_bias = ch["ldp"].n

                def _prep_bias_pe(t):
                    t.wait_ge(ch["ldp"].sem, ldp_bias)
                    t.wait_ge(ch["misc"].sem, m_ident)
                    t.matmul(out=PS0[:, 0:8], lhsT=prow[0:8, 0:128],
                             rhs=identf[0:8, 0:8], start=True, stop=True)
                    t.matmul(out=PS0[:, 8:16], lhsT=prow[64:72, 0:128],
                             rhs=identf[64:72, 64:72], start=True, stop=True)
                    t.matmul(out=PS1[:, 0:32], lhsT=prow[32:64, 0:128],
                             rhs=identf[32:64, 32:64], start=True,
                             stop=True).then_inc(ch["misc"].sem, 1)
                ch["misc"].n += 1
                block.tensor(_prep_bias_pe)

                m_biaspe = ch["misc"].n
                m_biaspe_g[0] = m_biaspe

                def _prep_bias_v(v):
                    v.wait_ge(ch["misc"].sem, m_biaspe)
                    v.tensor_copy(out=bpc[:, :], in_=PS0[:, 0:8])
                    v.tensor_copy(out=b2c[:, :], in_=PS0[:, 8:16])
                    v.tensor_copy(out=b1c[:, :],
                                  in_=PS1[:, 0:32]).then_inc(ch["misc"].sem, 1)
                ch["misc"].n += 1
                block.vector(_prep_bias_v)

            if BI_PREP >= 3:
                wq_bw0s = ch["bwout"].n

                def _prep_wq_sync(s):
                    s.wait_ge(ch["misc"].sem, m_biaspe_g[0])
                    for p in range(8):
                        if p >= 1:
                            s.wait_ge(ch["bwout"].sem, wq_bw0s + p * 16)
                        s.dma_start(out=prow[0:64, 0:64],
                                    in_=wq_in[2 * p]).then_inc(ch["ldp"].sem, 16)
                        s.wait_ge(ch["misc"].sem, 1)
                        s.dma_start(out=prow[0:64, 64:128],
                                    in_=wq_in[2 * p + 1]).then_inc(ch["ldp"].sem, 16)
                        s.wait_ge(ch["misc"].sem, 1)
                        s.wait_ge(ch["mrow"].sem, (p + 1))
                ch["ldp"].n += 8 * 32
                block.sync(_prep_wq_sync)

                wq_bw0 = ch["bwout"].n
                wq_s10 = ch["s1"].n

                def _prep_wq_v(v):
                    base = ch["ldp"].n - 8 * 32
                    for p in range(8):
                        v.wait_ge(ch["ldp"].sem, base + (p + 1) * 32)
                        v.tensor_copy(out=psgn[0:64, 0:128], in_=prow[0:64, 0:128]).then_inc(ch["misc"].sem, 1)
                        ch["misc"].n += 1
                        v.wait_ge(ch["s1"].sem, wq_s10 + 2 * (p + 1))
                        if p >= 1:
                            v.wait_ge(ch["bwout"].sem, wq_bw0 + p * 16)
                        v.tensor_copy(out=wqt[0:64, p * 64:(p + 1) * 64], in_=PS2B[0:64, 0:64])
                        v.tensor_copy(out=prow[0:64, 64:96].bitcast(BF16)[0:64, 0:64],
                                      in_=PS2B[0:64, 128:192]).then_inc(ch["mrow"].sem, 1)
                block.vector(_prep_wq_v)

                def _prep_wq_pe(t):
                    for p in range(8):
                        t.wait_ge(ch["misc"].sem, ch["misc"].n - 8 + (p + 1))
                        t.transpose(out=PS2B[0:64, 0:64], in_=psgn[0:64, 0:64],
                                    identity=identb[0:64, 0:64]).then_inc(ch["s1"].sem, 1)
                        ch["s1"].n += 1
                        t.transpose(out=PS2B[0:64, 128:192], in_=psgn[0:64, 64:128],
                                    identity=identb[0:64, 0:64]).then_inc(ch["s1"].sem, 1)
                        ch["s1"].n += 1
                block.tensor(_prep_wq_pe)

                def _prep_wq_g(g):
                    for p in range(8):
                        g.wait_ge(ch["mrow"].sem, p + 1)
                        g.dma_start(out=wqt[64:128, p * 64:(p + 1) * 64],
                                    in_=prow[0:64, 64:96].bitcast(BF16)[0:64, 0:64]
                                    ).then_inc(ch["bwout"].sem, 16)
                ch["bwout"].n += 8 * 16
                ch["mrow"].n += 8
                block.gpsimd(_prep_wq_g)

            if BI_PREP >= 4:
                mxb0 = ch["mx"].n
                expb0 = ch["exp"].n

                ldP = [ch["ldpA"], ch["ldpB"]]
                base_p = [ldP[0].n, ldP[1].n]

                def wload_sem(g):
                    return ldP[g % 2], base_p[g % 2] + (g // 2 + 1) * 16

                def _prep_w_sync(s):
                    s.wait_ge(ch["bwout"].sem, ch["bwout"].n)
                    for i in range(W1N):
                        if i >= 2:
                            s.wait_ge(ch["mx"].sem, mxb0 + i - 1)
                            s.wait_ge(ch["exp"].sem, expb0 + i - 1)
                        s.dma_start(out=pstages[i % 2][:, 0:1024],
                                    in_=w1_in[i * 128:(i + 1) * 128, :]).then_inc(
                                        ldP[i % 2].sem, 16)
                    for i in range(W2N):
                        g = W1N + i
                        s.wait_ge(ch["mx"].sem, mxb0 + W1N + max(0, i - 1))
                        s.wait_ge(ch["exp"].sem, expb0 + W1N + max(0, i - 1))
                        s.dma_start(out=pstages[g % 2][:, :],
                                    in_=w2_in[i * 128:(i + 1) * 128, :]).then_inc(
                                        ldP[g % 2].sem, 16)
                    for i in range(WPN):
                        g = W1N + W2N + i
                        s.wait_ge(ch["mx"].sem, mxb0 + W1N + W2N + max(0, i - 1))
                        s.dma_start(out=pstages[g % 2][:, 0:1024],
                                    in_=wp_in[i * 128:(i + 1) * 128, :]).then_inc(
                                        ldP[g % 2].sem, 16)
                nw = W1N + W2N + WPN
                ch["ldpA"].n += ((nw + 1) // 2) * 16
                ch["ldpB"].n += (nw // 2) * 16
                block.sync(_prep_w_sync)

                s1b0 = ch["s1"].n
                bwo0 = ch["bwout"].n
                sbase = ch["s1"].n
                p2b0 = ch["p2"].n

                def _prep_w_v(v):
                    n_b = W1N + W2N * 4 + WPN
                    for i in range(W1N):
                        sm, val = wload_sem(i)
                        v.wait_ge(sm.sem, val)
                        if i >= 1:
                            v.wait_ge(ch["mx"].sem, mxb0 + i)
                        vedge(v, v.reduce_sum(out=palph[:, 0:1],
                                               in_=pstages[i % 2][:, 0:1024],
                                               axis=AX.X, apply_absolute_value=True))
                        v.wait_ge(ch["exp"].sem, expb0 + i + 1)
                        if i >= 1:
                            v.wait_ge(ch["s1"].sem, s1b0 + i)
                        v.tensor_scalar(out=ptout[:, 0:1024], in0=psgn2[:, 0:1024],
                                        scalar1=palph[:, 0:1], scalar2=1.0 / 1024,
                                        op0=ALU.mult, op1=ALU.mult).then_inc(ch["mx"].sem, 1)
                        if i >= 1:
                            v.wait_ge(ch["bwout"].sem, bwo0 + (i - 1) * 16)
                            pb = PS2B if ((i - 1) % 2 == 0 or NOALT) else PS3B
                            v.tensor_copy(out=psgn[:, 0:1024],
                                          in_=pb[:, 0:1024]).then_inc(ch["p2"].sem, 1)
                    if W1N >= 1:
                        v.wait_ge(ch["s1"].sem, s1b0 + W1N)
                        v.wait_ge(ch["bwout"].sem, bwo0 + (W1N - 1) * 16)
                        pb = PS2B if ((W1N - 1) % 2 == 0 or NOALT) else PS3B
                        v.tensor_copy(out=psgn[:, 0:1024],
                                      in_=pb[:, 0:1024]).then_inc(ch["p2"].sem, 1)
                    for i in range(W2N):
                        sm, val = wload_sem(W1N + i)
                        v.wait_ge(sm.sem, val)
                        v.wait_ge(ch["mx"].sem, mxb0 + W1N + i)
                        vedge(v, v.reduce_sum(out=palph[:, 0:1],
                                               in_=pstages[i % 2][:, :],
                                               axis=AX.X, apply_absolute_value=True))
                        v.wait_ge(ch["exp"].sem, expb0 + W1N + i + 1)
                        v.tensor_scalar(out=ptout[:, :], in0=psgn2[:, :],
                                        scalar1=palph[:, 0:1], scalar2=1.0 / 4096,
                                        op0=ALU.mult, op1=ALU.mult).then_inc(ch["mx"].sem, 1)
                        v.wait_ge(ch["bwout"].sem, bwo0 + (W1N + i) * 16)
                        for b4 in range(4):
                            idx = W1N + i * 4 + b4
                            v.wait_ge(ch["s1"].sem, sbase + idx + 1)
                            pb = PS2B if idx % 2 == 0 else PS3B
                            v.tensor_copy(out=psgn[:, b4 * 1024:(b4 + 1) * 1024],
                                          in_=pb[:, 0:1024]).then_inc(ch["p2"].sem, 1)
                    for i in range(WPN):
                        sm, val = wload_sem(W1N + W2N + i)
                        v.wait_ge(sm.sem, val)
                        v.wait_ge(ch["s1"].sem, s1b0 + W1N + W2N * 4 + i)
                        v.tensor_copy(out=ptout[:, 0:1024],
                                      in_=pstages[i % 2][:, 0:1024]).then_inc(ch["mx"].sem, 1)
                        idx = W1N + W2N * 4 + i
                        v.wait_ge(ch["s1"].sem, sbase + idx + 1)
                        v.wait_ge(ch["bwout"].sem, bwo0 + (W1N + W2N + i) * 16)
                        pb = PS2B if idx % 2 == 0 else PS3B
                        v.tensor_copy(out=psgn[:, 0:1024],
                                      in_=pb[:, 0:1024]).then_inc(ch["p2"].sem, 1)
                ch["mx"].n += W1N + W2N + WPN
                ch["p2"].n += W1N + W2N * 4 + WPN
                block.vector(_prep_w_v)

                def _prep_w_act(sc):
                    for i in range(W1N):
                        sm, val = wload_sem(i)
                        sc.wait_ge(sm.sem, val)
                        if i >= 1:
                            sc.wait_ge(ch["mx"].sem, mxb0 + i)
                        sc.activation(out=psgn2[:, 0:1024], in_=pstages[i % 2][:, 0:1024],
                                      func=AF.Sign).then_inc(ch["exp"].sem, 1)
                    for i in range(W2N):
                        sm, val = wload_sem(W1N + i)
                        sc.wait_ge(sm.sem, val)
                        sc.wait_ge(ch["mx"].sem, mxb0 + W1N + i)
                        sc.activation(out=psgn2[:, :], in_=pstages[i % 2][:, :],
                                      func=AF.Sign).then_inc(ch["exp"].sem, 1)
                ch["exp"].n += W1N + W2N
                block.scalar(_prep_w_act)

                def _prep_w_pe(t):
                    cnt = 0
                    for i in range(W1N):
                        t.wait_ge(ch["mx"].sem, mxb0 + i + 1)
                        if i >= 2 or NOALT:
                            t.wait_ge(ch["p2"].sem, p2b0 + max(0, i - (1 if not NOALT else 0)))
                        pb = PS2B if (i % 2 == 0 or NOALT) else PS3B
                        for dc in range(8):
                            inst = t.transpose(out=pb[:, dc * 128:(dc + 1) * 128],
                                               in_=ptout[:, dc * 128:(dc + 1) * 128],
                                               identity=identb[:, :])
                        inst.then_inc(ch["s1"].sem, 1)
                        cnt += 1
                    for i in range(W2N):
                        t.wait_ge(ch["mx"].sem, mxb0 + W1N + i + 1)
                        for b4 in range(4):
                            idx = i * 4 + b4
                            if W1N + idx >= 2:
                                t.wait_ge(ch["p2"].sem, p2b0 + W1N + idx - 1)
                            pb = PS2B if (32 + idx) % 2 == 0 else PS3B
                            for k in range(8):
                                ht = b4 * 8 + k
                                inst = t.transpose(out=pb[:, k * 128:(k + 1) * 128],
                                                   in_=ptout[:, ht * 128:(ht + 1) * 128],
                                                   identity=identb[:, :])
                            inst.then_inc(ch["s1"].sem, 1)
                            cnt += 1
                    for i in range(WPN):
                        t.wait_ge(ch["mx"].sem, mxb0 + W1N + W2N + i + 1)
                        idx = W1N + W2N * 4 + i
                        t.wait_ge(ch["p2"].sem, p2b0 + idx - 1)
                        pb = PS2B if idx % 2 == 0 else PS3B
                        for dc in range(8):
                            inst = t.transpose(out=pb[:, dc * 128:(dc + 1) * 128],
                                               in_=ptout[:, dc * 128:(dc + 1) * 128],
                                               identity=identb[:, :])
                        inst.then_inc(ch["s1"].sem, 1)
                        cnt += 1
                n_batches = W1N + W2N * 4 + WPN
                ch["s1"].n += n_batches
                block.tensor(_prep_w_pe)

                def _prep_w_out(g):
                    pbase = ch["p2"].n - n_batches
                    for i in range(W1N):
                        g.wait_ge(ch["p2"].sem, pbase + i + 1)
                        g.dma_start(out=bw1t_d[i], in_=psgn[:, 0:1024]).then_inc(ch["bwout"].sem, 16)
                    for i in range(W2N):
                        g.wait_ge(ch["p2"].sem, pbase + W1N + (i + 1) * 4)
                        g.dma_start(out=bw2t_d[i], in_=psgn[:, :]).then_inc(ch["bwout"].sem, 16)
                    for i in range(WPN):
                        g.wait_ge(ch["p2"].sem, pbase + W1N + W2N * 4 + i + 1)
                        g.dma_start(out=wpt_d[i], in_=psgn[:, 0:1024]).then_inc(ch["bwout"].sem, 16)
                ch["bwout"].n += (W1N + W2N + WPN) * 16
                block.gpsimd(_prep_w_out)

            if DBG:
                def _dbg_bw(s):
                    s.wait_ge(ch["bwout"].sem, ch["bwout"].n)
                    s.dma_start(out=dbg["bw1t"], in_=bw1t_d[0]).then_inc(ch["oout"].sem, 16)
                ch["oout"].n += 16
                block.sync(_dbg_bw)

        qA = EC(nc.sbuf_tensor("qA", [128, 2048], BF16))
        qB = EC(nc.sbuf_tensor("qB", [128, 2048], BF16))
        qC = EC(nc.sbuf_tensor("qC", [128, 2048], BF16))
        qD = EC(nc.sbuf_tensor("qD", [128, 2048], BF16))
        qE = EC(nc.sbuf_tensor("qE", [128, 2048], BF16))
        qT = EC(nc.sbuf_tensor("qT", [128, 2112], BF16))
        qsub = EC(nc.sbuf_tensor("qsub", [128, 512], BF16))
        pt0 = EC(nc.sbuf_tensor("pt0", [128, 1024], BF16))
        pt1 = EC(nc.sbuf_tensor("pt1", [128, 1024], BF16))
        subm = EC(nc.sbuf_tensor("subm", [128, 32], F32))
        diag = EC(nc.sbuf_tensor("diag", [128, 32], F32))
        mxc = EC(nc.sbuf_tensor("mxc", [128, 32], F32))
        tmpc = EC(nc.sbuf_tensor("tmpc", [128, 8], F32))
        rcz = EC(nc.sbuf_tensor("rcz", [128, 16], F32))
        cneg = EC(nc.sbuf_tensor("cneg", [128, 1], F32))
        xst0 = EC(nc.sbuf_tensor("xst0", [128, 1024], F32))
        xst1 = EC(nc.sbuf_tensor("xst1", [128, 1024], F32))
        sqscr = EC(nc.sbuf_tensor("sqscr", [128, 1024], F32))
        lns = EC(nc.sbuf_tensor("lns", [128, 8], F32))
        xnstage = EC(nc.sbuf_tensor("xnstage", [128, 1024], BF16))
        hsb = EC(nc.sbuf_tensor("hsb", [128, 16384], BF16))
        w1s0 = EC(nc.sbuf_tensor("w1s0", [128, 1024], BF16))
        w1s1 = EC(nc.sbuf_tensor("w1s1", [128, 1024], BF16))
        w2s0 = EC(nc.sbuf_tensor("w2s0", [128, 2048], BF16))
        w2s1 = EC(nc.sbuf_tensor("w2s1", [128, 2048], BF16))
        rtm = EC(nc.sbuf_tensor("rtm", [128, 2048], BF16))
        gscr = EC(nc.sbuf_tensor("gscr", [128, 512], F32))
        sscr = EC(nc.sbuf_tensor("sscr", [128, 512], F32))
        sgscr = EC(nc.sbuf_tensor("sgscr", [128, 512], F32))
        xrj0 = EC(nc.sbuf_tensor("xrj0", [128, 512], F32))
        xrj1 = EC(nc.sbuf_tensor("xrj1", [128, 512], F32))
        wps0 = EC(nc.sbuf_tensor("wps0", [128, 1024], BF16))
        wps1 = EC(nc.sbuf_tensor("wps1", [128, 1024], BF16))
        sgst0 = EC(nc.sbuf_tensor("sgst0", [128, 512], F32))
        sgst1 = EC(nc.sbuf_tensor("sgst1", [128, 512], F32))
        mxst = sqscr.bitcast(BF16)
        restm = rtm[:, :]

        prep_done = {"bwout": ch["bwout"].n, "p2": ch["p2"].n, "mrow": ch["mrow"].n,
                     "misc": ch["misc"].n}

        def _barrier(e):
            e.wait_ge(ch["bwout"].sem, prep_done["bwout"])
            e.wait_ge(ch["p2"].sem, prep_done["p2"])
        for engname in ["tensor", "vector", "scalar", "gpsimd", "sync"]:
            getattr(block, engname)(_barrier)

        def _post_prep(v):
            if BI_PREP >= 5:
                v.memset(qA[64:96, :], 0.0)
                v.memset(qB[64:96, :], 0.0)
                vedge(v, v.memset(qD[64:96, :], 0.0))
                vedge(v, v.memset(qE[64:96, :], 0.0))
                v.memset(qA[64:65, :], 1.0)
                v.memset(qD[64:65, :], 1.0)
                v.memset(qT[:, :].rearrange("p (s c) -> p s c", c=66)[:, :, 64:65], 1.0)
                v.memset(qT[:, :].rearrange("p (s c) -> p s c", c=66)[:, :, 65:66], 0.0)
                v.memset(lns[:, 7:8], 1e-5)
                v.memset(cneg[:, :], -32.0)
            v.memset(tmpc[:, 0:8], 0.0).then_inc(ch["qp"].sem, 1)
        ch["qp"].n += 1
        block.vector(_post_prep)



        def emit_ln(s):
            xstg = [xst0, xst1]

            lnP = [ch["lnxA"], ch["lnxB"]]
            lnb = [lnP[0].n, lnP[1].n]

            def _sync(sy):
                for tt in range(16):
                    if tt >= 2:
                        sy.wait_ge(ch["lnst"].sem, ch["lnst"].n + tt - 1)
                    sy.dma_start(out=xstg[tt % 2][:, :],
                                 in_=x_in[s, tt * 128:(tt + 1) * 128, :]
                                 ).then_inc(lnP[tt % 2].sem, 16)
            ch["lnxA"].n += 8 * 16
            ch["lnxB"].n += 8 * 16
            block.sync(_sync)

            lnt0 = ch["lnt"].n
            lnf0 = ch["lnf"].n
            lnst0 = ch["lnst"].n
            s1b = ch["s1"].n
            h_prev = ch["h"].n

            def _act(sc):
                for tt in range(16):
                    sc.wait_ge(lnP[tt % 2].sem, lnb[tt % 2] + (tt // 2 + 1) * 16)
                    if tt >= 1:
                        sc.wait_ge(ch["lnst"].sem, lnst0 + tt)
                    sc.activation(out=sqscr[:, :], in_=xstg[tt % 2][:, :], func=AF.Square,
                                  accum_out=lns[:, 1:2]).then_inc(ch["lnt"].sem, 1)
                    sc.wait_ge(ch["lnf"].sem, lnf0 + 2 * tt + 1)
                    sc.activation(out=lns[:, 5:6], in_=lns[:, 4:5], func=AF.Sqrt,
                                  bias=lns[:, 7:8], scale=1.0).then_inc(ch["lnt"].sem, 1)
            ch["lnt"].n += 32
            block.scalar(_act)

            def _vec(v):
                for tt in range(16):
                    v.wait_ge(lnP[tt % 2].sem, lnb[tt % 2] + (tt // 2 + 1) * 16)
                    vedge(v, v.reduce_sum(out=lns[:, 0:1], in_=xstg[tt % 2][:, :],
                                           axis=AX.X))
                    vedge(v, v.tensor_scalar(out=lns[:, 2:3], in0=lns[:, 0:1],
                                             scalar1=-1.0 / 1024,
                                             scalar2=None, op0=ALU.mult))
                    vedge(v, v.tensor_mul(out=lns[:, 3:4], in0=lns[:, 2:3],
                                          in1=lns[:, 2:3]))
                    v.wait_ge(ch["lnt"].sem, lnt0 + 2 * tt + 1)
                    v.scalar_tensor_tensor(out=lns[:, 4:5], in0=lns[:, 1:2],
                                           scalar=1.0 / 1024, in1=lns[:, 3:4],
                                           op0=ALU.mult, op1=ALU.subtract
                                           ).then_inc(ch["lnf"].sem, 1)
                    v.wait_ge(ch["lnt"].sem, lnt0 + 2 * tt + 2)
                    vedge(v, v.reciprocal(out=lns[:, 6:7], in_=lns[:, 5:6]))
                    if tt >= 1:
                        v.wait_ge(ch["s1"].sem, s1b + tt)
                    v.tensor_scalar(out=xnstage[:, :], in0=xstg[tt % 2][:, :],
                                    scalar1=lns[:, 2:3], scalar2=lns[:, 6:7],
                                    op0=ALU.add, op1=ALU.mult).then_inc(ch["lnst"].sem, 1)
                    if tt == 0:
                        v.wait_ge(ch["h"].sem, h_prev)
                    v.wait_ge(ch["s1"].sem, s1b + tt + 1)
                    v.tensor_copy(
                        out=xnf[:, :].rearrange("p (dc t) -> p dc t", t=2048)
                        [:, :, tt * 128:(tt + 1) * 128],
                        in_=PS1B[:, 1024:2048].rearrange("p (dc t) -> p dc t", t=128)
                        ).then_inc(ch["lnf"].sem, 1)
            ch["lnf"].n += 32
            ch["lnst"].n += 16
            block.vector(_vec)

            hs_prev = ch["hs"].n
            gys_prev = ch["gys"].n

            def _pe(t):
                t.wait_ge(ch["hs"].sem, hs_prev)
                t.wait_ge(ch["gys"].sem, gys_prev)
                for tt in range(16):
                    t.wait_ge(ch["lnst"].sem, lnst0 + tt + 1)
                    if tt >= 1:
                        t.wait_ge(ch["lnf"].sem, lnf0 + 2 * tt)
                    for dc in range(8):
                        inst = t.transpose(out=PS1B[:, 1024 + dc * 128:1024 + (dc + 1) * 128],
                                           in_=xnstage[:, dc * 128:(dc + 1) * 128],
                                           identity=identb[:, :])
                    inst.then_inc(ch["s1"].sem, 1)
            ch["s1"].n += 16
            block.tensor(_pe)

        pair_ctx = {}

        def emit_qproj_fm(s, p):
            qp0 = ch["qp"].n
            sev_prev = ch["sev"].n
            pair_ctx[p] = qp0

            def _fm_pe(t):
                t.wait_ge(ch["lnf"].sem, ch["lnf"].n)
                if p == 0:
                    t.wait_ge(ch["sev"].sem, sev_prev)
                for half in range(2):
                    if half == 1:
                        t.wait_ge(ch["qp"].sem, qp0 + 5)
                    for c in range(2):
                        t.matmul(out=PS2[0:64, c * 512:(c + 1) * 512],
                                 lhsT=wqt[0:64, p * 64:(p + 1) * 64],
                                 rhs=xnf[0:64, p * 2048 + half * 1024 + c * 512:
                                         p * 2048 + half * 1024 + (c + 1) * 512],
                                 start=True, stop=True, tile_position=(0, 0))
                        t.matmul(out=PS2[64:128, c * 512:(c + 1) * 512],
                                 lhsT=wqt[64:128, p * 64:(p + 1) * 64],
                                 rhs=xnf[64:128, p * 2048 + half * 1024 + c * 512:
                                         p * 2048 + half * 1024 + (c + 1) * 512],
                                 start=True, stop=True, tile_position=(64, 64))
                        inst = t.matmul(out=PS3[0:64, c * 512:(c + 1) * 512],
                                        lhsT=wqt[64:128, p * 64:(p + 1) * 64],
                                        rhs=xnf[64:128, p * 2048 + half * 1024 + c * 512:
                                                p * 2048 + half * 1024 + (c + 1) * 512],
                                        start=True, stop=True, tile_position=(64, 0))
                    inst.then_inc(ch["p2"].sem, 1)
                    pef(t, 0.003)
            p2q = ch["p2"].n
            ch["p2"].n += 2
            block.tensor(_fm_pe)

            def _fm_ev(sc):
                for half in range(2):
                    sc.wait_ge(ch["p2"].sem, p2q + half + 1)
                    hs = slice(half * 1024, (half + 1) * 1024)
                    for dst in (qA, qB):
                        sc.activation(out=dst[0:64, hs], in_=PS2[0:64, 0:1024],
                                      func=AF.Identity, bias=0.0, scale=1.0)
                    sc.activation(out=qC[64:128, hs], in_=PS2[64:128, 0:1024],
                                  func=AF.Identity, bias=0.0, scale=1.0)
                    sc.activation(out=qD[0:64, hs], in_=PS3[0:64, 0:1024],
                                  func=AF.Identity, bias=0.0, scale=1.0)
                    sc.activation(out=qE[0:64, hs], in_=PS3[0:64, 0:1024],
                                  func=AF.Identity, bias=0.0,
                                  scale=1.0).then_inc(ch["qp"].sem, 5)
            ch["qp"].n += 10
            block.scalar(_fm_ev)

            def _fm_gather(v):
                v.wait_ge(ch["qp"].sem, qp0 + 10)
                v.tensor_copy(out=qsub[0:64, :],
                              in_=qA[0:64, :].rearrange("p (a b) -> p a b", b=4)
                              [:, :, 0])
                v.tensor_copy(out=qsub[64:128, :],
                              in_=qC[64:128, :].rearrange("p (a b) -> p a b", b=4)
                              [:, :, 0]).then_inc(ch["qp"].sem, 1)
                vef(v, 0.05)
            ch["qp"].n += 1
            block.vector(_fm_gather)

        def emit_attn_main(s, p):
            STG = BI_STAGE
            qp0 = pair_ctx[p]
            if fq["act"]:
                def _af(sc):
                    aef(sc, 1.0)
                block.scalar(_af)

            resf_prev = ch["resf"].n
            ost_prev = ch["ost"].n

            def _qproj_tm(t):
                for bb in range(2):
                    if bb == 0:
                        t.wait_ge(ch["resf"].sem, resf_prev)
                        if p == 0:
                            t.wait_ge(ch["ost"].sem, ost_prev)
                    t.wait_ge(ch["qp"].sem, qp0 + 10 + (0 if bb == 0 else 2))
                    for k in range(8):
                        nt = bb * 8 + k
                        t.transpose(out=PS0B[:, k * 64:(k + 1) * 64],
                                    in_=qA[0:64, nt * 128:(nt + 1) * 128],
                                    identity=identb[0:64, 0:64])
                        inst = t.transpose(out=PS1B[:, k * 64:(k + 1) * 64],
                                           in_=qC[64:128, nt * 128:(nt + 1) * 128],
                                           identity=identb[64:128, 64:128])
                    inst.then_inc(ch["p2"].sem, 1)
                    pef(t, 0.003)
            p2q = ch["p2"].n
            ch["p2"].n += 2
            block.tensor(_qproj_tm)

            def _qproj_ev(v):
                for bb in range(2 if QPN >= 3 else 0):
                    v.wait_ge(ch["p2"].sem, p2q + 1 + bb)
                    v.tensor_copy(
                        out=qT[:, :].rearrange("p (ss c) -> p ss c", c=66)
                        [:, bb * 8:(bb + 1) * 8, 0:64],
                        in_=PS0B[:, 0:512].rearrange("p (k e) -> p k e", e=64))
                    v.tensor_copy(
                        out=qT[:, :].rearrange("p (ss c) -> p ss c", c=66)
                        [:, 16 + bb * 8:16 + (bb + 1) * 8, 0:64],
                        in_=PS1B[:, 0:512].rearrange("p (k e) -> p k e", e=64)
                        ).then_inc(ch["qp"].sem, 1)
                if QPN >= 4:
                    v.wait_ge(ch["qp"].sem, qp0 + 13)
                    vedge(v, v.tensor_tensor(
                        out=sqscr[:, :].rearrange("p (ss e) -> p ss e", e=64),
                        in0=qT[:, :].rearrange("p (ss c) -> p ss c", c=66)[:, 0:16, 0:64],
                        in1=qT[:, :].rearrange("p (ss c) -> p ss c", c=66)[:, 0:16, 0:64],
                        op=ALU.mult))
                    vedge(v, v.reduce_sum(
                        out=diag[:, 0:16],
                        in_=sqscr[:, :].rearrange("p (ss e) -> p ss e", e=64), axis=AX.X))
                    vedge(v, v.tensor_tensor(
                        out=sqscr[:, :].rearrange("p (ss e) -> p ss e", e=64),
                        in0=qT[:, :].rearrange("p (ss c) -> p ss c", c=66)[:, 16:32, 0:64],
                        in1=qT[:, :].rearrange("p (ss c) -> p ss c", c=66)[:, 16:32, 0:64],
                        op=ALU.mult))
                    v.reduce_sum(out=diag[:, 16:32],
                                 in_=sqscr[:, :].rearrange("p (ss e) -> p ss e", e=64),
                                 axis=AX.X).then_inc(ch["qp"].sem, 1)
                    vef(v, 0.05)
            ch["qp"].n += 3
            block.vector(_qproj_ev)

            if STG < 3:
                return
            s10 = ch["s1"].n
            mx0 = ch["mx"].n

            qp_all = ch["qp"].n

            P1R = [PS0, PS2, PS3]

            def _p1_t(t, nt):
                if nt == 0:
                    t.wait_ge(ch["qp"].sem, qp_all - 3)
                if nt == 2:
                    t.wait_ge(ch["qp"].sem, qp_all - 1)
                PSe = P1R[(nt + 1) % 3]
                if nt >= 3:
                    t.wait_ge(ch["mx"].sem, mx0 + (nt - 2) * 2)
                t.matmul(out=PSe[0:64, 0:512],
                         lhsT=qA[0:64, nt * 128:nt * 128 + 64],
                         rhs=qsub[0:64, :], start=True, stop=True,
                         tile_position=(0, 0))
                t.matmul(out=PSe[64:128, 0:512],
                         lhsT=qA[0:64, nt * 128 + 64:nt * 128 + 128],
                         rhs=qsub[0:64, :], start=True, stop=True,
                         tile_position=(0, 64))
                t.matmul(out=PSe[0:64, 512:1024],
                         lhsT=qC[64:128, nt * 128:nt * 128 + 64],
                         rhs=qsub[64:128, :], start=True, stop=True,
                         tile_position=(64, 0))
                mm = t.matmul(out=PSe[64:128, 512:1024],
                              lhsT=qC[64:128, nt * 128 + 64:nt * 128 + 128],
                              rhs=qsub[64:128, :], start=True, stop=True,
                              tile_position=(64, 64))
                mm.then_inc(ch["s1"].sem, 1)
                pef(t, 0.003)

            def _p1_vv(v, nt):
                PSe = P1R[(nt + 1) % 3]
                v.wait_ge(ch["s1"].sem, s10 + nt + 1)
                v.reduce_max(out=subm[:, nt:nt + 1], in_=PSe[:, 0:512], axis=AX.X)
                inst = v.reduce_max(out=subm[:, 16 + nt:17 + nt],
                                    in_=PSe[:, 512:1024], axis=AX.X)
                inst.then_inc(ch["mx"].sem, 2)
                if nt == 15:
                    v.wait_ge(ch["mx"].sem, mx0 + 32)
                    v.wait_ge(ch["qp"].sem, qp_all)
                    v.tensor_max(out=mxc[:, :], in0=subm[:, :],
                                 in1=diag[:, :]).then_inc(ch["mx"].sem, 1)

            for nt in range(16):
                block.tensor(lambda t, nt=nt: _p1_t(t, nt))
                block.vector(lambda v, nt=nt: _p1_vv(v, nt))
            ch["s1"].n += 16
            ch["mx"].n += 33

            if STG < 4:
                return
            mrow0 = ch["mrow"].n

            def _mx_pe(t):
                t.wait_ge(ch["mx"].sem, ch["mx"].n)
                for h in range(2):
                    for half in range(2):
                        PSx = PS2 if h == 0 else PS3
                        r = 64 * half
                        for c in range(8):
                            nt = half * 8 + c
                            inst = t.matmul(out=PSx[r:r + 1, c * 128:(c + 1) * 128],
                                            lhsT=mxc[:, h * 16 + nt:h * 16 + nt + 1],
                                            rhs=identf[:, :], start=(c in (0, 4)),
                                            stop=(c in (3, 7)),
                                            tile_position=(0, r))
                        inst.then_inc(ch["p2"].sem, 1)
                        pef(t, 0.005)
            ch["p2"].n += 4
            block.tensor(_mx_pe)

            p2m = ch["p2"].n - 4

            mxrow_prev = ch["mxrow"].n

            def _mx_v(v):
                v.wait_ge(ch["p2"].sem, p2m + 4)
                for i in range(4):
                    h, half = divmod(i, 2)
                    PSx = PS2 if h == 0 else PS3
                    r = 64 * half
                    if i == 0:
                        v.wait_ge(ch["mxrow"].sem, mxrow_prev)
                    if i == 2:
                        v.wait_ge(ch["mxrow"].sem, mxrow_prev + 16)
                    v.tensor_scalar(out=mxst[0:1, half * 1024:(half + 1) * 1024],
                                    in0=PSx[r:r + 1, 0:1024], scalar1=-1.0, scalar2=None,
                                    op0=ALU.mult).then_inc(ch["mrow"].sem, 1)
                vef(v, 0.03)
            ch["mrow"].n += 4
            block.vector(_mx_v)

            def _mx_g(g):
                g.wait_ge(ch["mrow"].sem, mrow0 + 2)
                g.dma_start(out=qB[64:65, :], in_=mxst[0:1, :]).then_inc(ch["mxrow"].sem, 16)
                g.wait_ge(ch["mrow"].sem, mrow0 + 4)
                g.dma_start(out=qE[64:65, :], in_=mxst[0:1, :]).then_inc(ch["mxrow"].sem, 16)
            ch["mxrow"].n += 32
            block.sync(_mx_g)

            if STG < 5:
                return
            slot_map = []
            for nt in range(16):
                if nt < 7:
                    slot_map.append((PS0, nt * 66))
                elif nt < 14:
                    slot_map.append((PS0, 512 + (nt - 7) * 66))
                else:
                    slot_map.append((PS1, (nt - 14) * 66))

            for h in range(2):
                qlhs = qA if h == 0 else qD
                qrhs = qB if h == 0 else qE
                p20 = ch["p2"].n
                exp0 = ch["exp"].n
                av0 = ch["av"].n

                def _p2_grp(t, grp, h=h, qlhs=qlhs, qrhs=qrhs, p20=p20,
                            exp0=exp0, av0=av0):
                    if grp == 0:
                        t.wait_ge(ch["mxrow"].sem, ch["mxrow"].n - 16 * (1 - h))
                        if h == 0:
                            t.wait_ge(ch["mx"].sem, ch["mx"].n)
                    for it in range(grp * 4, grp * 4 + 4):
                        half, J = divmod(it, 16)
                        PSa = PS2 if it % 2 == 0 else PS3
                        if it >= 2:
                            t.wait_ge(ch["exp"].sem, exp0 + it - 1)
                        t.matmul(out=PSa[:, 0:512],
                                 lhsT=qlhs[0:96, J * 128:(J + 1) * 128],
                                 rhs=qrhs[0:96, half * 1024:half * 1024 + 512],
                                 start=True, stop=True)
                        t.matmul(out=PSa[:, 512:1024],
                                 lhsT=qlhs[0:96, J * 128:(J + 1) * 128],
                                 rhs=qrhs[0:96, half * 1024 + 512:half * 1024 + 1024],
                                 start=True, stop=True).then_inc(ch["p2"].sem, 1)
                        pef(t, 0.0145)
                        if it >= 1:
                            Jp = (it - 1) % 16
                            halfp = (it - 1) // 16
                            t.wait_ge(ch["exp"].sem, exp0 + it)
                            if h == 1 and it == 1 and BI_STAGE >= 6:
                                t.wait_ge(ch["divd"].sem, ch["divd"].n)
                            ptb = pt0 if (it - 1) % 2 == 0 else pt1
                            for k in range(8):
                                nt = halfp * 8 + k
                                pst, col = slot_map[nt]
                                inst = t.matmul(
                                    out=pst[:, col:col + 65],
                                    lhsT=ptb[:, k * 128:(k + 1) * 128],
                                    rhs=qT[:, (h * 16 + Jp) * 66:(h * 16 + Jp) * 66 + 65],
                                    start=(Jp == 0 and nt in (0, 7, 8, 14)),
                                    stop=(Jp == 15 and nt in (6, 7, 13, 15)))
                            if Jp == 15:
                                inst.then_inc(ch["av"].sem, 1)
                    if grp == 7:
                        t.wait_ge(ch["exp"].sem, exp0 + 32)
                        ptb = pt1
                        for k in range(8):
                            nt = 8 + k
                            pst, col = slot_map[nt]
                            inst = t.matmul(
                                out=pst[:, col:col + 65],
                                lhsT=ptb[:, k * 128:(k + 1) * 128],
                                rhs=qT[:, (h * 16 + 15) * 66:(h * 16 + 15) * 66 + 65],
                                start=False, stop=(nt in (13, 15)))
                        inst.then_inc(ch["av"].sem, 1)
                        pef(t, 0.0145)

                for grp in range(8):
                    block.tensor(lambda t, grp=grp: _p2_grp(t, grp))
                    if fq["vec"]:
                        block.vector(lambda v: vef(v, 0.03))
                ch["p2"].n += 32
                ch["av"].n += 2

                def _p2_act(sc, p20=p20, exp0=exp0, av0=av0):
                    for half in range(2):
                        for J in range(16):
                            it = half * 16 + J
                            sc.wait_ge(ch["p2"].sem, p20 + it + 1)
                            if it >= 2:
                                sc.wait_ge(ch["av"].sem, av0)
                            PSa = PS2 if it % 2 == 0 else PS3
                            ptb = pt0 if it % 2 == 0 else pt1
                            sc.activation(out=ptb[:, :], in_=PSa[:, :], func=AF.Exp,
                                          bias=cneg[:, :], scale=8.0).then_inc(
                                              ch["exp"].sem, 1)
                ch["exp"].n += 32
                block.scalar(_p2_act)

                div0 = ch["divd"].n

                if STG < 6:
                    continue_div = False
                else:
                    continue_div = True

                def _div_v(v, h=h, av0=av0):
                    for half in range(2):
                        v.wait_ge(ch["av"].sem, av0 + 2)
                        for k in range(8):
                            nt = half * 8 + k
                            pst, col = slot_map[nt]
                            inst = v.reciprocal(out=rcz[:, nt:nt + 1],
                                                in_=pst[:, col + 64:col + 65])
                        vedge(v, inst)
                        for k in range(8):
                            nt = half * 8 + k
                            pst, col = slot_map[nt]
                            inst = v.tensor_scalar(
                                out=restm[:, nt * 128 + h * 64:nt * 128 + h * 64 + 64],
                                in0=pst[:, col:col + 64],
                                scalar1=rcz[:, nt:nt + 1], scalar2=None,
                                op0=ALU.mult)
                        inst.then_inc(ch["divd"].sem, 1)
                        vef(v, 0.1)
                if continue_div:
                    ch["divd"].n += 2
                    block.vector(_div_v)

        def emit_attn_res(s, p):
            rst0 = ch["rest"].n
            divn = ch["divd"].n

            def _res_pe(t):
                t.wait_ge(ch["divd"].sem, divn)
                for bb in range(4):
                    if bb >= 2:
                        t.wait_ge(ch["resf"].sem, ch["resf"].n + bb - 1)
                    pb = PS0B[:, (bb % 2) * 1024:(bb % 2) * 1024 + 512]
                    for k in range(4):
                        nt = bb * 4 + k
                        inst = t.transpose(out=pb[:, k * 128:(k + 1) * 128],
                                           in_=restm[:, nt * 128:(nt + 1) * 128],
                                           identity=identb[:, :])
                    inst.then_inc(ch["rest"].sem, 1)
                    pef(t, 0.003)
            ch["rest"].n += 4
            block.tensor(_res_pe)

            def _res_v(v):
                for bb in range(4):
                    v.wait_ge(ch["rest"].sem, rst0 + bb + 1)
                    pb = PS0B[:, (bb % 2) * 1024:(bb % 2) * 1024 + 512]
                    v.tensor_copy(out=resf[:, p * 2048 + bb * 512:p * 2048 + (bb + 1) * 512],
                                  in_=pb[:, 0:512]).then_inc(ch["resf"].sem, 1)
                vef(v, 0.08)
            ch["resf"].n += 4
            block.vector(_res_v)

        seq_silu = {}
        cnt = {"wps": 0}

        from collections import deque
        fq = {"pe": deque(), "vec": deque(), "act": deque()}
        pace_d = {"pe": 0.0, "vec": 0.0, "act": 0.0}
        pace_t = {"pe": 0.0, "vec": 0.0, "act": 0.0}
        pair_budget = {"pe": 0.0, "vec": 0.0, "act": 0.0}

        emt = {"h": 0, "gy": 0, "hs": 0, "gys": 0, "gact": 0}

        def _fill(qn, eng, frac):
            pace_t[qn] += frac * pair_budget[qn]
            q = fq[qn]
            eager = qn == "vec"
            while q and (eager or pace_d[qn] + q[0][0] * 0.5 <= pace_t[qn]):
                c, fn, reqs, prov = q[0]
                if any(emt[k] < v for k, v in reqs):
                    break
                q.popleft()
                fn(eng)
                emt[prov] += 1
                pace_d[qn] += c

        def pef(t, frac):
            _fill("pe", t, frac)

        def vef(v, frac):
            _fill("vec", v, frac)

        def aef(sc, frac):
            _fill("act", sc, frac)

        def pace_reset():
            for qn in fq:
                pair_budget[qn] = sum(e[0] for e in fq[qn]) / 8.5
                pace_d[qn] = 0.0
                pace_t[qn] = 0.0

        def drain_all():
            qs = {"pe": block.tensor, "vec": block.vector, "act": block.scalar}
            while any(fq[qn] for qn in qs):
                progress = False
                for qn, emit in qs.items():
                    runnable = []
                    while fq[qn]:
                        c, fn, reqs, prov = fq[qn][0]
                        if any(emt[k] < v for k, v in reqs):
                            break
                        fq[qn].popleft()
                        runnable.append(fn)
                        emt[prov] += 1
                    if runnable:
                        progress = True

                        def _sec(e, runnable=runnable):
                            for fn in runnable:
                                fn(e)
                        emit(_sec)
                assert progress, "ffn queue drain deadlock"

        def plan_ffn_chunk(s, ci, defer_dma=None):
            tc, half = divmod(ci, 2)
            fb = PS1[:, 512:1024]
            w1P = [ch["lw1A"], ch["lw1B"]]
            w2P = [ch["lw2A"], ch["lw2B"]]
            M0 = ch["h"].n
            G0 = ch["gy"].n
            gy_now = ch["gy"].n
            gys_now = ch["gys"].n
            sg_now = ch["sg"].n
            gact0 = ch["gact"].n
            lnf_now = ch["lnf"].n

            w1_wait = []
            for k in range(16):
                par = (M0 + k) % 2
                w1P[par].n += 16
                w1_wait.append(w1P[par].n)
            w2_wait = []
            if half == 1:
                for dt in range(8):
                    w2P[0].n += 16
                    a = w2P[0].n
                    w2P[1].n += 16
                    w2_wait.append((a, w2P[1].n))

            yc_now = ch["yc"].n

            def _dma(g, M0=M0, G0=G0, tc=tc, half=half, ci=ci, yc_now=yc_now):
                for k in range(16):
                    M = M0 + k
                    if M >= 2:
                        g.wait_ge(ch["h"].sem, M - 1)
                    g.dma_start(out=(w1s0 if M % 2 == 0 else w1s1)[:, :],
                                in_=bw1t_d[half * 16 + k]).then_inc(w1P[M % 2].sem, 16)
                if half == 1:
                    for dt in range(8):
                        if G0 + dt >= 1:
                            g.wait_ge(ch["gyh"].sem, G0 + dt)
                        g.dma_start(out=w2s0[:, :],
                                    in_=bw2t_d[dt][:, 0:2048]).then_inc(w2P[0].sem, 16)
                        if G0 + dt >= 1:
                            g.wait_ge(ch["gy"].sem, G0 + dt)
                        g.dma_start(out=w2s1[:, :],
                                    in_=bw2t_d[dt][:, 2048:4096]).then_inc(w2P[1].sem, 16)
            if defer_dma is None:
                block.gpsimd(_dma)
            else:
                defer_dma.append(_dma)

            for k in range(16):
                def _h(t, k=k, M0=M0, tc=tc, half=half, ci=ci,
                       w1_wait=w1_wait, gys_now=gys_now, lnf_now=lnf_now):
                    M = M0 + k
                    par = M % 2
                    t.wait_ge(w1P[par].sem, w1_wait[k])
                    if ci == 0 and k == 0:
                        t.wait_ge(ch["lnf"].sem, lnf_now)
                    if half == 0 and k == 0:
                        t.wait_ge(ch["gys"].sem, gys_now)
                    if M >= 1:
                        t.wait_ge(ch["hs"].sem, M)
                    w1b = w1s0 if par == 0 else w1s1
                    for dc in range(8):
                        inst = t.matmul(
                            out=fb, lhsT=w1b[:, dc * 128:(dc + 1) * 128],
                            rhs=xnf[:, dc * 2048 + tc * 512:dc * 2048 + (tc + 1) * 512],
                            start=(dc == 0), stop=(dc == 7))
                    inst.then_inc(ch["h"].sem, 1)
                fq["pe"].append((1700, _h, [("hs", M0 + k)], "h"))
            if half == 1:
                for dt in range(8):
                    def _g(t, dt=dt, M0=M0, G0=G0, tc=tc,
                           w2_wait=w2_wait, gact0=gact0):
                        G = G0 + dt
                        t.wait_ge(w2P[0].sem, w2_wait[dt][0])
                        if dt == 0:
                            t.wait_ge(ch["gact"].sem, gact0 + 1)
                            t.wait_ge(ch["hs"].sem, M0 + 16)
                        else:
                            t.wait_ge(ch["gys"].sem, G)
                        for kk in range(32):
                            if kk == 16:
                                t.wait_ge(w2P[1].sem, w2_wait[dt][1])
                            w2b = w2s0 if kk < 16 else w2s1
                            inst = t.matmul(out=fb,
                                            lhsT=w2b[:, (kk % 16) * 128:(kk % 16 + 1) * 128],
                                            rhs=hsb[:, kk * 512:(kk + 1) * 512],
                                            start=(kk == 0), stop=(kk == 31))
                            if kk == 15:
                                inst.then_inc(ch["gyh"].sem, 1)
                        inst.then_inc(ch["gy"].sem, 1)
                    fq["pe"].append(
                        (6800, _g,
                         [("gact", gact0 + 1), ("hs", M0 + 16)] if dt == 0
                         else [("gys", G0 + dt)], "gy"))
            ch["h"].n += 16
            if half == 1:
                ch["gy"].n += 8
                ch["gyh"].n += 8

            for k in range(16):
                def _e(v, k=k, M0=M0, half=half, gy_now=gy_now):
                    M = M0 + k
                    v.wait_ge(ch["h"].sem, M + 1)
                    if half == 0 and k == 0:
                        v.wait_ge(ch["gy"].sem, gy_now)
                    ht = half * 16 + k
                    v.tensor_scalar(out=hsb[:, ht * 512:(ht + 1) * 512], in0=fb,
                                    scalar1=b1c[:, ht:ht + 1], scalar2=None,
                                    op0=ALU.add).then_inc(ch["hs"].sem, 1)
                fq["vec"].append((250, _e, [("h", M0 + k + 1)], "hs"))
            if half == 1:
                for dt in range(8):
                    def _ge(v, dt=dt, G0=G0, tc=tc, ci=ci, s=s, sg_now=sg_now):
                        G = G0 + dt
                        v.wait_ge(ch["gy"].sem, G + 1)
                        if s > 0 and ci == 1 and dt == 0:
                            v.wait_ge(ch["sg"].sem, sg_now)
                        v.tensor_scalar(out=gfm[:, tc * 4096 + dt * 512:
                                                 tc * 4096 + (dt + 1) * 512],
                                        in0=fb, scalar1=b2c[:, dt:dt + 1], scalar2=None,
                                        op0=ALU.add).then_inc(ch["gys"].sem, 1)
                    fq["vec"].append((250, _ge, [("gy", G0 + dt + 1)], "gys"))
            ch["hs"].n += 16
            if half == 1:
                ch["gys"].n += 8

            def _gelu(sc, M0=M0, half=half):
                sc.wait_ge(ch["hs"].sem, M0 + 16)
                sc.activation(out=hsb[:, half * 8192:(half + 1) * 8192],
                              in_=hsb[:, half * 8192:(half + 1) * 8192],
                              func=AF.Gelu, bias=0.0, scale=1.0).then_inc(
                    ch["gact"].sem, 1)
            fq["act"].append((7100, _gelu, [("hs", M0 + 16)], "gact"))
            ch["gact"].n += 1
            if half == 1:
                def _silu(sc, G0=G0, tc=tc):
                    sc.wait_ge(ch["gys"].sem, G0 + 8)
                    sc.activation(out=gfm[:, tc * 4096:(tc + 1) * 4096],
                                  in_=gfm[:, tc * 4096:(tc + 1) * 4096],
                                  func=AF.Silu, bias=0.0, scale=1.0).then_inc(
                        ch["gact"].sem, 1)
                fq["act"].append((3700, _silu, [("gys", G0 + 8)], "gact"))
                ch["gact"].n += 1
                seq_silu[(s, tc)] = ch["gact"].n

        def emit_combine_a(s):
            lwpP = [ch["lwpA"], ch["lwpB"]]
            ouP = [ch["ooutA"], ch["ooutB"]]
            resf_end = ch["resf"].n
            divd_end = ch["divd"].n
            exp_end = ch["exp"].n
            c = {"s": s, "Y0": ch["yc"].n, "W0": cnt["wps"], "sg0": ch["sg"].n,
                 "sev0": ch["sev"].n, "ost0": ch["ost"].n,
                 "silu_all": seq_silu[(s, 3)], "ou0": [ouP[0].n, ouP[1].n]}
            Y0, W0, sg0, ost0 = c["Y0"], c["W0"], c["sg0"], c["ost0"]
            h_now = ch["h"].n

            wps_wait = []
            for j in range(32):
                par = (W0 + j) % 2
                lwpP[par].n += 16
                wps_wait.append(lwpP[par].n)
            cnt["wps"] += 32
            lxrP = [ch["lxrA"], ch["lxrB"]]
            xres_ready = []
            for j in range(32):
                lxrP[j % 2].n += 16
                xres_ready.append(lxrP[j % 2].n)
            c["xres_ready"] = xres_ready
            c["ou_abs"] = [c["ou0"][j % 2] + 64 * (j // 2 + 1) for j in range(32)]
            YPS = [PS2[:, 0:512], PS2[:, 512:1024], PS3[:, 0:512], PS3[:, 512:1024]]
            c["YPS"] = YPS

            def _dma(g):
                for j in range(32):
                    dt = j % 8
                    Wg = W0 + j
                    if Wg >= 2:
                        g.wait_ge(ch["yc"].sem, Wg - 1)
                    g.dma_start(out=(wps0 if Wg % 2 == 0 else wps1)[:, :],
                                in_=wpt_d[dt]).then_inc(lwpP[Wg % 2].sem, 16)
            block.gpsimd(_dma)

            def _xld(sy, j):
                tc, dt = divmod(j, 8)
                sy.dma_start(
                    out=(xrj0 if j % 2 == 0 else xrj1)[:, :]
                    .rearrange("p (k d) -> p k d", d=128),
                    in_=x_in[s, tc * 512:(tc + 1) * 512, dt * 128:(dt + 1) * 128]
                    .rearrange("(k p) d -> p k d", p=128)
                    ).then_inc(lxrP[j % 2].sem, 16)
            c["xld"] = _xld

            def _sgT(t, j):
                Yg = Y0 + j
                t.wait_ge(ch["sg"].sem, sg0 + j + 1)
                if Yg >= 2:
                    t.wait_ge(ch["ost"].sem, Yg - 1)
                for k in range(4):
                    inst = t.transpose(
                        out=PS0[:, (j % 2) * 512 + k * 128:(j % 2) * 512 + (k + 1) * 128],
                        in_=sgscr[:, k * 128:(k + 1) * 128], identity=identf[:, :])
                inst.then_inc(ch["sgt"].sem, 1)

            def _pe(t):
                t.wait_ge(ch["divd"].sem, divd_end)
                t.wait_ge(ch["exp"].sem, exp_end)
                t.wait_ge(ch["resf"].sem, resf_end)
                for j in range(32):
                    tc, dt = divmod(j, 8)
                    Yg = Y0 + j
                    par = (W0 + j) % 2
                    t.wait_ge(lwpP[par].sem, wps_wait[j])
                    if Yg >= 4:
                        t.wait_ge(ch["sev"].sem, Yg - 3)
                    buf = wps0 if par == 0 else wps1
                    for dc in range(8):
                        inst = t.matmul(
                            out=YPS[j % 4],
                            lhsT=buf[:, dc * 128:(dc + 1) * 128],
                            rhs=resf[:, dc * 2048 + tc * 512:dc * 2048 + (tc + 1) * 512],
                            start=(dc == 0), stop=(dc == 7))
                    inst.then_inc(ch["yc"].sem, 1)
                    if j >= 1:
                        _sgT(t, j - 1)
                _sgT(t, 31)
            ch["yc"].n += 32
            ch["sgt"].n += 32
            block.tensor(_pe)

            def _act(sc):
                for j in range(32):
                    dt = j % 8
                    Yg = Y0 + j
                    sc.wait_ge(ch["yc"].sem, Yg + 1)
                    if Yg >= 2:
                        sc.wait_ge(ch["sg"].sem, Yg - 1)
                    sc.activation(out=(sscr if j % 2 == 0 else gscr)[:, :],
                                  in_=YPS[j % 4],
                                  func=AF.Identity, bias=bpc[:, dt:dt + 1],
                                  scale=1.0).then_inc(ch["sev"].sem, 1)
            ch["sev"].n += 32
            block.scalar(_act)
            return c

        def emit_combine_b(c):
            s = c["s"]
            ouP = [ch["ooutA"], ch["ooutB"]]
            lxrP = [ch["lxrA"], ch["lxrB"]]
            Y0, sg0, sev0, ost0 = c["Y0"], c["sg0"], c["sev0"], c["ost0"]

            def _vec(v):
                for j in range(32):
                    tc, dt = divmod(j, 8)
                    Yg = Y0 + j
                    v.wait_ge(ch["sev"].sem, sev0 + j + 1)
                    if j == 0:
                        v.wait_ge(ch["gact"].sem, c["silu_all"])
                    if Yg >= 1:
                        v.wait_ge(ch["sgt"].sem, Yg)
                    v.tensor_tensor(out=sgscr[:, :],
                                    in0=gfm[:, tc * 4096 + dt * 512:
                                            tc * 4096 + (dt + 1) * 512],
                                    in1=(sscr if j % 2 == 0 else gscr)[:, :],
                                    op=ALU.mult).then_inc(ch["sg"].sem, 1)
                    v.wait_ge(ch["sgt"].sem, Yg + 1)
                    v.wait_ge(lxrP[j % 2].sem, c["xres_ready"][j])
                    if j >= 2:
                        v.wait_ge(ouP[j % 2].sem, c["ou_abs"][j - 2])
                    else:
                        v.wait_ge(ouP[j % 2].sem, c["ou0"][j % 2])
                    stg = sgst0 if j % 2 == 0 else sgst1
                    v.tensor_tensor(
                        out=stg[:, :].rearrange("p (k d) -> p k d", d=128),
                        in0=PS0[:, (j % 2) * 512:(j % 2) * 512 + 512]
                        .rearrange("p (k d) -> p k d", d=128),
                        in1=(xrj0 if j % 2 == 0 else xrj1)[:, :]
                        .rearrange("p (k d) -> p k d", d=128),
                        op=ALU.add).then_inc(ch["ost"].sem, 1)
            ch["sg"].n += 32
            ch["ost"].n += 32
            block.vector(_vec)

            def _out(sy):
                sy.wait_ge(ch["ost"].sem, ost0)
                c["xld"](sy, 0)
                c["xld"](sy, 1)
                for j in range(32):
                    tc, dt = divmod(j, 8)
                    sy.wait_ge(ch["ost"].sem, ost0 + j + 1)
                    if j + 2 < 32:
                        c["xld"](sy, j + 2)
                    stg = sgst0 if j % 2 == 0 else sgst1
                    for k in range(4):
                        sy.dma_start(
                            out=out_d[s, tc * 512 + k * 128:tc * 512 + (k + 1) * 128,
                                      dt * 128:(dt + 1) * 128],
                            in_=stg[:, k * 128:(k + 1) * 128]
                            ).then_inc(ouP[j % 2].sem, 16)
            ch["ooutA"].n += 16 * 64
            ch["ooutB"].n += 16 * 64
            block.sync(_out)

        ln_done = set()
        planned = set()
        for s in range(min(SEQ, BI_SEQ)):
            if BI_STAGE >= 1 and s not in ln_done:
                emit_ln(s)
            full = BI_STAGE >= 8 and BI_PAIR >= 8
            if full and s not in planned:
                for ci in range(8):
                    plan_ffn_chunk(s, ci)
                pace_reset()
                planned.add(s)
            if BI_STAGE >= 2:
                emit_qproj_fm(s, 0)
                for p in range(min(8, BI_PAIR)):
                    emit_attn_main(s, p)
                    if p + 1 < min(8, BI_PAIR):
                        emit_qproj_fm(s, p + 1)
                    emit_attn_res(s, p)
            if DBG and s == 0:
                def _dbg0(sy):
                    sy.wait_ge(ch["resf"].sem, ch["resf"].n)
                    sy.dma_start(out=dbg["xnf"], in_=xnf[:, :]).then_inc(ch["oout"].sem, 16)
                    sy.dma_start(out=dbg["resf"], in_=resf[:, :]).then_inc(ch["oout"].sem, 16)
                    sy.dma_start(out=dbg["qA"], in_=qA[:, :]).then_inc(ch["oout"].sem, 16)
                    sy.dma_start(out=dbg["qT"], in_=qT[:, :]).then_inc(ch["oout"].sem, 16)
                    sy.dma_start(out=dbg["mxc"], in_=mxc[:, :]).then_inc(ch["oout"].sem, 16)
                    sy.dma_start(out=dbg["pt"], in_=pt1[:, :]).then_inc(ch["oout"].sem, 16)
                ch["oout"].n += 96
                block.sync(_dbg0)
            if full:
                drain_all()
                boundary = BI_SEQ > SEQ - 1 and s + 1 < min(SEQ, BI_SEQ)
                if boundary:
                    emit_ln(s + 1)
                    ln_done.add(s + 1)
                    dma_defer = []
                    for ci in range(8):
                        plan_ffn_chunk(s + 1, ci, defer_dma=dma_defer)
                    pace_reset()
                    planned.add(s + 1)
                cctx = emit_combine_a(s)
                if boundary:
                    for f in dma_defer:
                        block.gpsimd(f)
                emit_combine_b(cctx)
                if boundary:
                    for _ in range(24):
                        block.tensor(lambda t: pef(t, 0.08))
                        block.vector(lambda v: vef(v, 0.07))
            if DBG and s == 0:
                def _dbg1(sy):
                    sy.wait_ge(ch["oout"].sem, ch["oout"].n)
                    sy.dma_start(out=dbg["hsb"], in_=hsb[:, :]).then_inc(ch["oout"].sem, 16)
                ch["oout"].n += 16
                block.sync(_dbg1)

        def _final(sy):
            sy.wait_ge(ch["ooutA"].sem, ch["ooutA"].n)
            sy.wait_ge(ch["ooutB"].sem, ch["ooutB"].n)
        block.sync(_final)

    return nc


def kernel(L0_attn_x, L0_attn_Wq, L0_attn_Wp, L0_attn_bp,
           L0_ffn_w1, L0_ffn_b1, L0_ffn_w2, L0_ffn_b2):
    x = np.ascontiguousarray(np.asarray(L0_attn_x, dtype=np.float32))
    wq = np.ascontiguousarray(np.asarray(L0_attn_Wq, dtype=np.float32))
    wp = np.ascontiguousarray(np.asarray(L0_attn_Wp, dtype=np.float32))
    bp = np.ascontiguousarray(np.asarray(L0_attn_bp, dtype=np.float32))
    w1 = np.ascontiguousarray(np.asarray(L0_ffn_w1, dtype=np.float32))
    b1 = np.ascontiguousarray(np.asarray(L0_ffn_b1, dtype=np.float32))
    w2 = np.ascontiguousarray(np.asarray(L0_ffn_w2, dtype=np.float32))
    b2 = np.ascontiguousarray(np.asarray(L0_ffn_b2, dtype=np.float32))

    nc = build_nc()
    in_maps = []
    for c in range(NCORE):
        in_maps.append({
            "x": np.ascontiguousarray(x[c * SEQ:(c + 1) * SEQ]),
            "wq": wq, "wp": wp, "bp": bp, "w1": w1, "b1": b1, "w2": w2, "b2": b2,
        })
    res = run_bass_kernel_spmd(nc, in_maps, core_ids=list(range(NCORE)))
    out = np.concatenate([np.asarray(r["out"]) for r in res.results], axis=0)
    if DBG:
        kernel.dbg = res.results
    return out


# version 27
# speedup vs baseline: 1.0030x; 1.0030x over previous
import numpy as np
import concourse.bass as bass
import concourse.mybir as mybir
from concourse.bass_utils import run_bass_kernel_spmd

F32 = mybir.dt.float32
BF16 = mybir.dt.bfloat16
I32 = mybir.dt.int32
AF = mybir.ActivationFunctionType
AX = mybir.AxisListType
ALU = mybir.AluOpType

NCORE = 8
SEQ = 4
N = 2048
D = 1024
NH = 16
E = 64
HF = 4096
TC = 512
DBG = False
import os
BI_SEQ = int(os.environ.get("BI_SEQ", "4"))
BI_PAIR = int(os.environ.get("BI_PAIR", "8"))
BI_STAGE = int(os.environ.get("BI_STAGE", "9"))
BI_PREP = int(os.environ.get("BI_PREP", "9"))
NOALT = int(os.environ.get("NOALT", "0"))
QPN = int(os.environ.get("QPN", "4"))
P1M = int(os.environ.get("P1M", "2"))
W1N = int(os.environ.get("W1N", "32"))
W2N = int(os.environ.get("W2N", "8"))
WPN = int(os.environ.get("WPN", "8"))
BI_PREP = int(os.environ.get("BI_PREP", "9"))
NOALT = int(os.environ.get("NOALT", "0"))
QPN = int(os.environ.get("QPN", "4"))
P1M = int(os.environ.get("P1M", "2"))


class Ch:

    def __init__(self, es, nc, name):
        self.sem = es.enter_context(nc.semaphore(name))
        self.n = 0

    def inc(self, inst, k=1):
        inst.then_inc(self.sem, k)
        self.n += k
        return self.n


def build_nc():
    import contextlib
    nc = bass.Bass("TRN2", target_bir_lowering=False)

    x_in = nc.dram_tensor("x", [SEQ, N, D], F32, kind="ExternalInput").ap()
    wq_in = nc.dram_tensor("wq", [NH, E, E], F32, kind="ExternalInput").ap()
    wp_in = nc.dram_tensor("wp", [D, D], F32, kind="ExternalInput").ap()
    bp_in = nc.dram_tensor("bp", [D], F32, kind="ExternalInput").ap()
    w1_in = nc.dram_tensor("w1", [HF, D], F32, kind="ExternalInput").ap()
    b1_in = nc.dram_tensor("b1", [HF], F32, kind="ExternalInput").ap()
    w2_in = nc.dram_tensor("w2", [D, HF], F32, kind="ExternalInput").ap()
    b2_in = nc.dram_tensor("b2", [D], F32, kind="ExternalInput").ap()
    out_d = nc.dram_tensor("out", [SEQ, N, D], F32, kind="ExternalOutput").ap()

    bw1t_d = nc.dram_tensor("bw1t_d", [HF // 128, 128, 1024], BF16, kind="Internal").ap()
    bw2t_d = nc.dram_tensor("bw2t_d", [D // 128, 128, 4096], BF16, kind="Internal").ap()
    wpt_d = nc.dram_tensor("wpt_d", [8, 128, 1024], BF16, kind="Internal").ap()

    dbg = {}
    if DBG:
        dbg["xnf"] = nc.dram_tensor("dbg_xnf", [128, 16384], BF16, kind="ExternalOutput").ap()
        dbg["resf"] = nc.dram_tensor("dbg_resf", [128, 16384], BF16, kind="ExternalOutput").ap()
        dbg["qA"] = nc.dram_tensor("dbg_qA", [128, 2048], BF16, kind="ExternalOutput").ap()
        dbg["qT"] = nc.dram_tensor("dbg_qT", [128, 2112], BF16, kind="ExternalOutput").ap()
        dbg["mxc"] = nc.dram_tensor("dbg_mxc", [128, 32], F32, kind="ExternalOutput").ap()
        dbg["pt"] = nc.dram_tensor("dbg_pt", [128, 1024], BF16, kind="ExternalOutput").ap()
        dbg["bw1t"] = nc.dram_tensor("dbg_bw1t", [128, 1024], BF16, kind="ExternalOutput").ap()
        dbg["hsb"] = nc.dram_tensor("dbg_hsb", [128, 16384], BF16, kind="ExternalOutput").ap()

    es = contextlib.ExitStack()
    with es:
        EC = es.enter_context
        xnf = EC(nc.sbuf_tensor("xnf", [128, 16384], BF16))
        resf = EC(nc.sbuf_tensor("resf", [128, 16384], BF16))
        gfm = EC(nc.sbuf_tensor("gfm", [128, 16384], BF16))
        wqt = EC(nc.sbuf_tensor("wqt", [128, 512], BF16))
        bpc = EC(nc.sbuf_tensor("bpc", [128, 8], F32))
        b1c = EC(nc.sbuf_tensor("b1c", [128, 32], F32))
        b2c = EC(nc.sbuf_tensor("b2c", [128, 8], F32))
        identb = EC(nc.sbuf_tensor("identb", [128, 128], BF16))
        identf = EC(nc.sbuf_tensor("identf", [128, 128], F32))
        PS0 = EC(nc.psum_tensor("PS0", [128, 1024], F32))
        PS1 = EC(nc.psum_tensor("PS1", [128, 1024], F32))
        PS2 = EC(nc.psum_tensor("PS2", [128, 1024], F32))
        PS3 = EC(nc.psum_tensor("PS3", [128, 1024], F32))
        PS2B = PS2.bitcast(BF16)
        PS3B = PS3.bitcast(BF16)
        PS0B = PS0.bitcast(BF16)
        PS1B = PS1.bitcast(BF16)

        block = EC(nc.Block())

        ch = {n: Ch(es, nc, n) for n in [
            "ldp", "bwout", "qp", "s1", "mx", "mxrow", "p2", "exp", "av",
            "divd", "rest", "resf", "lh", "lw1", "lw2", "lxr", "h", "hs",
            "gy", "gys", "sg", "sgt", "ost", "oout", "lnx", "lnst", "lnt",
            "lnf", "misc", "mrow", "vch", "ach", "ldpA", "ldpB", "lnxA", "lnxB",
            "lw1A", "lw1B", "lw2A", "lw2B", "ooutA", "ooutB",
            "yc", "sev", "gyh", "gact", "lwpA", "lwpB", "lxrA", "lxrB", "lw1C"]}

        def vedge(v, inst):
            inst.then_inc(ch["vch"].sem, 1)
            ch["vch"].n += 1
            v.wait_ge(ch["vch"].sem, ch["vch"].n)

        def aedge(a, inst):
            inst.then_inc(ch["ach"].sem, 1)
            ch["ach"].n += 1
            a.wait_ge(ch["ach"].sem, ch["ach"].n)

        with nc.sbuf_tensor("pstage0", [128, 4096], F32) as pstage0, \
             nc.sbuf_tensor("pstage1", [128, 4096], F32) as pstage1, \
             nc.sbuf_tensor("psgn", [128, 4096], BF16) as psgn, \
             nc.sbuf_tensor("psgn2", [128, 4096], BF16) as psgn2, \
             nc.sbuf_tensor("ptout", [128, 4096], BF16) as ptout, \
             nc.sbuf_tensor("palph", [128, 4], F32) as palph, \
             nc.sbuf_tensor("prow", [128, 128], F32) as prow, \
             nc.sbuf_tensor("iscr", [128, 128], I32) as iscr:

            pstages = [pstage0, pstage1]

            def _prep_gpsimd(g):
                g.iota(iscr[:, :], pattern=[[1, 128]], base=0,
                       channel_multiplier=-1).then_inc(ch["misc"].sem, 1)
            ch["misc"].n += 1
            block.gpsimd(_prep_gpsimd)

            m_iota = ch["misc"].n

            def _prep_vec0(v):
                v.wait_ge(ch["misc"].sem, m_iota)
                v.tensor_single_scalar(out=identb[:, :], in_=iscr[:, :], scalar=0,
                                       op=ALU.is_equal)
                v.tensor_single_scalar(out=identf[:, :], in_=iscr[:, :], scalar=0,
                                       op=ALU.is_equal).then_inc(ch["misc"].sem, 1)
            ch["misc"].n += 1
            block.vector(_prep_vec0)
            m_ident = ch["misc"].n
            m_biaspe_g = [m_ident]

            if BI_PREP >= 2:
                def _prep_bias_sync(s):
                    s.dma_start(out=prow[0:8, 0:128], in_=bp_in.rearrange("(a b) -> a b", b=128)).then_inc(ch["ldp"].sem, 16)
                    s.wait_ge(ch["misc"].sem, 1)
                    s.dma_start(out=prow[64:72, 0:128], in_=b2_in.rearrange("(a b) -> a b", b=128)).then_inc(ch["ldp"].sem, 16)
                    s.wait_ge(ch["misc"].sem, 1)
                    s.dma_start(out=prow[32:64, 0:128], in_=b1_in.rearrange("(a b) -> a b", b=128)).then_inc(ch["ldp"].sem, 16)
                    s.wait_ge(ch["misc"].sem, 1)
                ch["ldp"].n += 48
                block.sync(_prep_bias_sync)

                ldp_bias = ch["ldp"].n

                def _prep_bias_pe(t):
                    t.wait_ge(ch["ldp"].sem, ldp_bias)
                    t.wait_ge(ch["misc"].sem, m_ident)
                    t.matmul(out=PS0[:, 0:8], lhsT=prow[0:8, 0:128],
                             rhs=identf[0:8, 0:8], start=True, stop=True)
                    t.matmul(out=PS0[:, 8:16], lhsT=prow[64:72, 0:128],
                             rhs=identf[64:72, 64:72], start=True, stop=True)
                    t.matmul(out=PS1[:, 0:32], lhsT=prow[32:64, 0:128],
                             rhs=identf[32:64, 32:64], start=True,
                             stop=True).then_inc(ch["misc"].sem, 1)
                ch["misc"].n += 1
                block.tensor(_prep_bias_pe)

                m_biaspe = ch["misc"].n
                m_biaspe_g[0] = m_biaspe

                def _prep_bias_v(v):
                    v.wait_ge(ch["misc"].sem, m_biaspe)
                    v.tensor_copy(out=bpc[:, :], in_=PS0[:, 0:8])
                    v.tensor_copy(out=b2c[:, :], in_=PS0[:, 8:16])
                    v.tensor_copy(out=b1c[:, :],
                                  in_=PS1[:, 0:32]).then_inc(ch["misc"].sem, 1)
                ch["misc"].n += 1
                block.vector(_prep_bias_v)

            if BI_PREP >= 3:
                wq_bw0s = ch["bwout"].n

                def _prep_wq_sync(s):
                    s.wait_ge(ch["misc"].sem, m_biaspe_g[0])
                    for p in range(8):
                        if p >= 1:
                            s.wait_ge(ch["bwout"].sem, wq_bw0s + p * 16)
                        s.dma_start(out=prow[0:64, 0:64],
                                    in_=wq_in[2 * p]).then_inc(ch["ldp"].sem, 16)
                        s.wait_ge(ch["misc"].sem, 1)
                        s.dma_start(out=prow[0:64, 64:128],
                                    in_=wq_in[2 * p + 1]).then_inc(ch["ldp"].sem, 16)
                        s.wait_ge(ch["misc"].sem, 1)
                        s.wait_ge(ch["mrow"].sem, (p + 1))
                ch["ldp"].n += 8 * 32
                block.sync(_prep_wq_sync)

                wq_bw0 = ch["bwout"].n
                wq_s10 = ch["s1"].n

                def _prep_wq_v(v):
                    base = ch["ldp"].n - 8 * 32
                    for p in range(8):
                        v.wait_ge(ch["ldp"].sem, base + (p + 1) * 32)
                        v.tensor_copy(out=psgn[0:64, 0:128], in_=prow[0:64, 0:128]).then_inc(ch["misc"].sem, 1)
                        ch["misc"].n += 1
                        v.wait_ge(ch["s1"].sem, wq_s10 + 2 * (p + 1))
                        if p >= 1:
                            v.wait_ge(ch["bwout"].sem, wq_bw0 + p * 16)
                        v.tensor_copy(out=wqt[0:64, p * 64:(p + 1) * 64], in_=PS2B[0:64, 0:64])
                        v.tensor_copy(out=prow[0:64, 64:96].bitcast(BF16)[0:64, 0:64],
                                      in_=PS2B[0:64, 128:192]).then_inc(ch["mrow"].sem, 1)
                block.vector(_prep_wq_v)

                def _prep_wq_pe(t):
                    for p in range(8):
                        t.wait_ge(ch["misc"].sem, ch["misc"].n - 8 + (p + 1))
                        t.transpose(out=PS2B[0:64, 0:64], in_=psgn[0:64, 0:64],
                                    identity=identb[0:64, 0:64]).then_inc(ch["s1"].sem, 1)
                        ch["s1"].n += 1
                        t.transpose(out=PS2B[0:64, 128:192], in_=psgn[0:64, 64:128],
                                    identity=identb[0:64, 0:64]).then_inc(ch["s1"].sem, 1)
                        ch["s1"].n += 1
                block.tensor(_prep_wq_pe)

                def _prep_wq_g(g):
                    for p in range(8):
                        g.wait_ge(ch["mrow"].sem, p + 1)
                        g.dma_start(out=wqt[64:128, p * 64:(p + 1) * 64],
                                    in_=prow[0:64, 64:96].bitcast(BF16)[0:64, 0:64]
                                    ).then_inc(ch["bwout"].sem, 16)
                ch["bwout"].n += 8 * 16
                ch["mrow"].n += 8
                block.gpsimd(_prep_wq_g)

            if BI_PREP >= 4:
                mxb0 = ch["mx"].n
                expb0 = ch["exp"].n

                ldP = [ch["ldpA"], ch["ldpB"]]
                base_p = [ldP[0].n, ldP[1].n]

                def wload_sem(g):
                    return ldP[g % 2], base_p[g % 2] + (g // 2 + 1) * 16

                def _prep_w_sync(s):
                    s.wait_ge(ch["bwout"].sem, ch["bwout"].n)
                    for i in range(W1N):
                        if i >= 2:
                            s.wait_ge(ch["mx"].sem, mxb0 + i - 1)
                            s.wait_ge(ch["exp"].sem, expb0 + i - 1)
                        s.dma_start(out=pstages[i % 2][:, 0:1024],
                                    in_=w1_in[i * 128:(i + 1) * 128, :]).then_inc(
                                        ldP[i % 2].sem, 16)
                    for i in range(W2N):
                        g = W1N + i
                        s.wait_ge(ch["mx"].sem, mxb0 + W1N + max(0, i - 1))
                        s.wait_ge(ch["exp"].sem, expb0 + W1N + max(0, i - 1))
                        s.dma_start(out=pstages[g % 2][:, :],
                                    in_=w2_in[i * 128:(i + 1) * 128, :]).then_inc(
                                        ldP[g % 2].sem, 16)
                    for i in range(WPN):
                        g = W1N + W2N + i
                        s.wait_ge(ch["mx"].sem, mxb0 + W1N + W2N + max(0, i - 1))
                        s.dma_start(out=pstages[g % 2][:, 0:1024],
                                    in_=wp_in[i * 128:(i + 1) * 128, :]).then_inc(
                                        ldP[g % 2].sem, 16)
                nw = W1N + W2N + WPN
                ch["ldpA"].n += ((nw + 1) // 2) * 16
                ch["ldpB"].n += (nw // 2) * 16
                block.sync(_prep_w_sync)

                s1b0 = ch["s1"].n
                bwo0 = ch["bwout"].n
                sbase = ch["s1"].n
                p2b0 = ch["p2"].n

                def _prep_w_v(v):
                    n_b = W1N + W2N * 4 + WPN
                    for i in range(W1N):
                        sm, val = wload_sem(i)
                        v.wait_ge(sm.sem, val)
                        if i >= 1:
                            v.wait_ge(ch["mx"].sem, mxb0 + i)
                        vedge(v, v.reduce_sum(out=palph[:, 0:1],
                                               in_=pstages[i % 2][:, 0:1024],
                                               axis=AX.X, apply_absolute_value=True))
                        v.wait_ge(ch["exp"].sem, expb0 + i + 1)
                        if i >= 1:
                            v.wait_ge(ch["s1"].sem, s1b0 + i)
                        v.tensor_scalar(out=ptout[:, 0:1024], in0=psgn2[:, 0:1024],
                                        scalar1=palph[:, 0:1], scalar2=1.0 / 1024,
                                        op0=ALU.mult, op1=ALU.mult).then_inc(ch["mx"].sem, 1)
                        if i >= 1:
                            v.wait_ge(ch["bwout"].sem, bwo0 + (i - 1) * 16)
                            pb = PS2B if ((i - 1) % 2 == 0 or NOALT) else PS3B
                            v.tensor_copy(out=psgn[:, 0:1024],
                                          in_=pb[:, 0:1024]).then_inc(ch["p2"].sem, 1)
                    if W1N >= 1:
                        v.wait_ge(ch["s1"].sem, s1b0 + W1N)
                        v.wait_ge(ch["bwout"].sem, bwo0 + (W1N - 1) * 16)
                        pb = PS2B if ((W1N - 1) % 2 == 0 or NOALT) else PS3B
                        v.tensor_copy(out=psgn[:, 0:1024],
                                      in_=pb[:, 0:1024]).then_inc(ch["p2"].sem, 1)
                    for i in range(W2N):
                        sm, val = wload_sem(W1N + i)
                        v.wait_ge(sm.sem, val)
                        v.wait_ge(ch["mx"].sem, mxb0 + W1N + i)
                        vedge(v, v.reduce_sum(out=palph[:, 0:1],
                                               in_=pstages[i % 2][:, :],
                                               axis=AX.X, apply_absolute_value=True))
                        v.wait_ge(ch["exp"].sem, expb0 + W1N + i + 1)
                        v.tensor_scalar(out=ptout[:, :], in0=psgn2[:, :],
                                        scalar1=palph[:, 0:1], scalar2=1.0 / 4096,
                                        op0=ALU.mult, op1=ALU.mult).then_inc(ch["mx"].sem, 1)
                        v.wait_ge(ch["bwout"].sem, bwo0 + (W1N + i) * 16)
                        for b4 in range(4):
                            idx = W1N + i * 4 + b4
                            v.wait_ge(ch["s1"].sem, sbase + idx + 1)
                            pb = PS2B if idx % 2 == 0 else PS3B
                            v.tensor_copy(out=psgn[:, b4 * 1024:(b4 + 1) * 1024],
                                          in_=pb[:, 0:1024]).then_inc(ch["p2"].sem, 1)
                    for i in range(WPN):
                        sm, val = wload_sem(W1N + W2N + i)
                        v.wait_ge(sm.sem, val)
                        v.wait_ge(ch["s1"].sem, s1b0 + W1N + W2N * 4 + i)
                        v.tensor_copy(out=ptout[:, 0:1024],
                                      in_=pstages[i % 2][:, 0:1024]).then_inc(ch["mx"].sem, 1)
                        idx = W1N + W2N * 4 + i
                        v.wait_ge(ch["s1"].sem, sbase + idx + 1)
                        v.wait_ge(ch["bwout"].sem, bwo0 + (W1N + W2N + i) * 16)
                        pb = PS2B if idx % 2 == 0 else PS3B
                        v.tensor_copy(out=psgn[:, 0:1024],
                                      in_=pb[:, 0:1024]).then_inc(ch["p2"].sem, 1)
                ch["mx"].n += W1N + W2N + WPN
                ch["p2"].n += W1N + W2N * 4 + WPN
                block.vector(_prep_w_v)

                def _prep_w_act(sc):
                    for i in range(W1N):
                        sm, val = wload_sem(i)
                        sc.wait_ge(sm.sem, val)
                        if i >= 1:
                            sc.wait_ge(ch["mx"].sem, mxb0 + i)
                        sc.activation(out=psgn2[:, 0:1024], in_=pstages[i % 2][:, 0:1024],
                                      func=AF.Sign).then_inc(ch["exp"].sem, 1)
                    for i in range(W2N):
                        sm, val = wload_sem(W1N + i)
                        sc.wait_ge(sm.sem, val)
                        sc.wait_ge(ch["mx"].sem, mxb0 + W1N + i)
                        sc.activation(out=psgn2[:, :], in_=pstages[i % 2][:, :],
                                      func=AF.Sign).then_inc(ch["exp"].sem, 1)
                ch["exp"].n += W1N + W2N
                block.scalar(_prep_w_act)

                def _prep_w_pe(t):
                    cnt = 0
                    for i in range(W1N):
                        t.wait_ge(ch["mx"].sem, mxb0 + i + 1)
                        if i >= 2 or NOALT:
                            t.wait_ge(ch["p2"].sem, p2b0 + max(0, i - (1 if not NOALT else 0)))
                        pb = PS2B if (i % 2 == 0 or NOALT) else PS3B
                        for dc in range(8):
                            inst = t.transpose(out=pb[:, dc * 128:(dc + 1) * 128],
                                               in_=ptout[:, dc * 128:(dc + 1) * 128],
                                               identity=identb[:, :])
                        inst.then_inc(ch["s1"].sem, 1)
                        cnt += 1
                    for i in range(W2N):
                        t.wait_ge(ch["mx"].sem, mxb0 + W1N + i + 1)
                        for b4 in range(4):
                            idx = i * 4 + b4
                            if W1N + idx >= 2:
                                t.wait_ge(ch["p2"].sem, p2b0 + W1N + idx - 1)
                            pb = PS2B if (32 + idx) % 2 == 0 else PS3B
                            for k in range(8):
                                ht = b4 * 8 + k
                                inst = t.transpose(out=pb[:, k * 128:(k + 1) * 128],
                                                   in_=ptout[:, ht * 128:(ht + 1) * 128],
                                                   identity=identb[:, :])
                            inst.then_inc(ch["s1"].sem, 1)
                            cnt += 1
                    for i in range(WPN):
                        t.wait_ge(ch["mx"].sem, mxb0 + W1N + W2N + i + 1)
                        idx = W1N + W2N * 4 + i
                        t.wait_ge(ch["p2"].sem, p2b0 + idx - 1)
                        pb = PS2B if idx % 2 == 0 else PS3B
                        for dc in range(8):
                            inst = t.transpose(out=pb[:, dc * 128:(dc + 1) * 128],
                                               in_=ptout[:, dc * 128:(dc + 1) * 128],
                                               identity=identb[:, :])
                        inst.then_inc(ch["s1"].sem, 1)
                        cnt += 1
                n_batches = W1N + W2N * 4 + WPN
                ch["s1"].n += n_batches
                block.tensor(_prep_w_pe)

                def _prep_w_out(g):
                    pbase = ch["p2"].n - n_batches
                    for i in range(W1N):
                        g.wait_ge(ch["p2"].sem, pbase + i + 1)
                        g.dma_start(out=bw1t_d[i], in_=psgn[:, 0:1024]).then_inc(ch["bwout"].sem, 16)
                    for i in range(W2N):
                        g.wait_ge(ch["p2"].sem, pbase + W1N + (i + 1) * 4)
                        g.dma_start(out=bw2t_d[i], in_=psgn[:, :]).then_inc(ch["bwout"].sem, 16)
                    for i in range(WPN):
                        g.wait_ge(ch["p2"].sem, pbase + W1N + W2N * 4 + i + 1)
                        g.dma_start(out=wpt_d[i], in_=psgn[:, 0:1024]).then_inc(ch["bwout"].sem, 16)
                ch["bwout"].n += (W1N + W2N + WPN) * 16
                block.gpsimd(_prep_w_out)

            if DBG:
                def _dbg_bw(s):
                    s.wait_ge(ch["bwout"].sem, ch["bwout"].n)
                    s.dma_start(out=dbg["bw1t"], in_=bw1t_d[0]).then_inc(ch["oout"].sem, 16)
                ch["oout"].n += 16
                block.sync(_dbg_bw)

        qA = EC(nc.sbuf_tensor("qA", [128, 2048], BF16))
        qB = EC(nc.sbuf_tensor("qB", [128, 2048], BF16))
        qC = EC(nc.sbuf_tensor("qC", [128, 2048], BF16))
        qD = EC(nc.sbuf_tensor("qD", [128, 2048], BF16))
        qE = EC(nc.sbuf_tensor("qE", [128, 2048], BF16))
        qT = EC(nc.sbuf_tensor("qT", [128, 2112], BF16))
        qsub = EC(nc.sbuf_tensor("qsub", [128, 512], BF16))
        pt0 = EC(nc.sbuf_tensor("pt0", [128, 1024], BF16))
        pt1 = EC(nc.sbuf_tensor("pt1", [128, 1024], BF16))
        subm = EC(nc.sbuf_tensor("subm", [128, 32], F32))
        diag = EC(nc.sbuf_tensor("diag", [128, 32], F32))
        mxc = EC(nc.sbuf_tensor("mxc", [128, 32], F32))
        tmpc = EC(nc.sbuf_tensor("tmpc", [128, 8], F32))
        rcz = EC(nc.sbuf_tensor("rcz", [128, 16], F32))
        cneg = EC(nc.sbuf_tensor("cneg", [128, 1], F32))
        xst0 = EC(nc.sbuf_tensor("xst0", [128, 1024], F32))
        xst1 = EC(nc.sbuf_tensor("xst1", [128, 1024], F32))
        lns = EC(nc.sbuf_tensor("lns", [128, 8], F32))
        xnstage = EC(nc.sbuf_tensor("xnstage", [128, 1024], BF16))
        hsb = EC(nc.sbuf_tensor("hsb", [128, 16384], BF16))
        w1s0 = EC(nc.sbuf_tensor("w1s0", [128, 1024], BF16))
        w1s1 = EC(nc.sbuf_tensor("w1s1", [128, 1024], BF16))
        w1s2 = EC(nc.sbuf_tensor("w1s2", [128, 1024], BF16))
        w2s0 = EC(nc.sbuf_tensor("w2s0", [128, 2048], BF16))
        w2s1 = EC(nc.sbuf_tensor("w2s1", [128, 2048], BF16))
        rtm = EC(nc.sbuf_tensor("rtm", [128, 2048], BF16))
        gscr = EC(nc.sbuf_tensor("gscr", [128, 512], F32))
        sscr = EC(nc.sbuf_tensor("sscr", [128, 512], F32))
        sgscr = EC(nc.sbuf_tensor("sgscr", [128, 512], F32))
        xrj0 = EC(nc.sbuf_tensor("xrj0", [128, 512], F32))
        xrj1 = EC(nc.sbuf_tensor("xrj1", [128, 512], F32))
        wps0 = EC(nc.sbuf_tensor("wps0", [128, 1024], BF16))
        wps1 = EC(nc.sbuf_tensor("wps1", [128, 1024], BF16))
        sgst0 = EC(nc.sbuf_tensor("sgst0", [128, 512], F32))
        sgst1 = EC(nc.sbuf_tensor("sgst1", [128, 512], F32))
        mxst = rtm
        hsbF32 = hsb.bitcast(F32)
        restm = rtm[:, :]

        prep_done = {"bwout": ch["bwout"].n, "p2": ch["p2"].n, "mrow": ch["mrow"].n,
                     "misc": ch["misc"].n}

        def _barrier(e):
            e.wait_ge(ch["bwout"].sem, prep_done["bwout"])
            e.wait_ge(ch["p2"].sem, prep_done["p2"])
        for engname in ["tensor", "vector", "scalar", "gpsimd", "sync"]:
            getattr(block, engname)(_barrier)

        def _post_prep(v):
            if BI_PREP >= 5:
                v.memset(qA[64:96, :], 0.0)
                v.memset(qB[64:96, :], 0.0)
                vedge(v, v.memset(qD[64:96, :], 0.0))
                vedge(v, v.memset(qE[64:96, :], 0.0))
                v.memset(qA[64:65, :], 1.0)
                v.memset(qD[64:65, :], 1.0)
                v.memset(qT[:, :].rearrange("p (s c) -> p s c", c=66)[:, :, 64:65], 1.0)
                v.memset(qT[:, :].rearrange("p (s c) -> p s c", c=66)[:, :, 65:66], 0.0)
                v.memset(lns[:, 7:8], 1e-5)
                v.memset(cneg[:, :], -32.0)
            v.memset(tmpc[:, 0:8], 0.0).then_inc(ch["qp"].sem, 1)
        ch["qp"].n += 1
        block.vector(_post_prep)



        def emit_ln(s):
            xstg = [xst0, xst1]

            lnP = [ch["lnxA"], ch["lnxB"]]
            lnb = [lnP[0].n, lnP[1].n]

            def _sync(sy):
                for tt in range(16):
                    if tt >= 2:
                        sy.wait_ge(ch["lnst"].sem, ch["lnst"].n + tt - 1)
                    sy.dma_start(out=xstg[tt % 2][:, :],
                                 in_=x_in[s, tt * 128:(tt + 1) * 128, :]
                                 ).then_inc(lnP[tt % 2].sem, 16)
            ch["lnxA"].n += 8 * 16
            ch["lnxB"].n += 8 * 16
            block.sync(_sync)

            lnt0 = ch["lnt"].n
            lnf0 = ch["lnf"].n
            lnst0 = ch["lnst"].n
            s1b = ch["s1"].n
            h_prev = ch["h"].n

            def _act(sc):
                for tt in range(16):
                    sc.wait_ge(lnP[tt % 2].sem, lnb[tt % 2] + (tt // 2 + 1) * 16)
                    if tt >= 1:
                        sc.wait_ge(ch["lnst"].sem, lnst0 + tt)
                    sc.activation(out=hsbF32[:, 7168:8192], in_=xstg[tt % 2][:, :],
                                  func=AF.Square,
                                  accum_out=lns[:, 1:2]).then_inc(ch["lnt"].sem, 1)
                    sc.wait_ge(ch["lnf"].sem, lnf0 + 2 * tt + 1)
                    sc.activation(out=lns[:, 5:6], in_=lns[:, 4:5], func=AF.Sqrt,
                                  bias=lns[:, 7:8], scale=1.0).then_inc(ch["lnt"].sem, 1)
            ch["lnt"].n += 32
            block.scalar(_act)

            def _vec(v):
                for tt in range(16):
                    v.wait_ge(lnP[tt % 2].sem, lnb[tt % 2] + (tt // 2 + 1) * 16)
                    vedge(v, v.reduce_sum(out=lns[:, 0:1], in_=xstg[tt % 2][:, :],
                                           axis=AX.X))
                    vedge(v, v.tensor_scalar(out=lns[:, 2:3], in0=lns[:, 0:1],
                                             scalar1=-1.0 / 1024,
                                             scalar2=None, op0=ALU.mult))
                    vedge(v, v.tensor_mul(out=lns[:, 3:4], in0=lns[:, 2:3],
                                          in1=lns[:, 2:3]))
                    v.wait_ge(ch["lnt"].sem, lnt0 + 2 * tt + 1)
                    v.scalar_tensor_tensor(out=lns[:, 4:5], in0=lns[:, 1:2],
                                           scalar=1.0 / 1024, in1=lns[:, 3:4],
                                           op0=ALU.mult, op1=ALU.subtract
                                           ).then_inc(ch["lnf"].sem, 1)
                    v.wait_ge(ch["lnt"].sem, lnt0 + 2 * tt + 2)
                    vedge(v, v.reciprocal(out=lns[:, 6:7], in_=lns[:, 5:6]))
                    if tt >= 1:
                        v.wait_ge(ch["s1"].sem, s1b + tt)
                    v.tensor_scalar(out=xnstage[:, :], in0=xstg[tt % 2][:, :],
                                    scalar1=lns[:, 2:3], scalar2=lns[:, 6:7],
                                    op0=ALU.add, op1=ALU.mult).then_inc(ch["lnst"].sem, 1)
                    if tt == 0:
                        v.wait_ge(ch["h"].sem, h_prev)
                    v.wait_ge(ch["s1"].sem, s1b + tt + 1)
                    v.tensor_copy(
                        out=xnf[:, :].rearrange("p (dc t) -> p dc t", t=2048)
                        [:, :, tt * 128:(tt + 1) * 128],
                        in_=PS1B[:, 1024:2048].rearrange("p (dc t) -> p dc t", t=128)
                        ).then_inc(ch["lnf"].sem, 1)
            ch["lnf"].n += 32
            ch["lnst"].n += 16
            block.vector(_vec)

            hs_prev = ch["hs"].n
            gys_prev = ch["gys"].n

            def _pe(t):
                t.wait_ge(ch["hs"].sem, hs_prev)
                t.wait_ge(ch["gys"].sem, gys_prev)
                for tt in range(16):
                    t.wait_ge(ch["lnst"].sem, lnst0 + tt + 1)
                    if tt >= 1:
                        t.wait_ge(ch["lnf"].sem, lnf0 + 2 * tt)
                    for dc in range(8):
                        inst = t.transpose(out=PS1B[:, 1024 + dc * 128:1024 + (dc + 1) * 128],
                                           in_=xnstage[:, dc * 128:(dc + 1) * 128],
                                           identity=identb[:, :])
                    inst.then_inc(ch["s1"].sem, 1)
            ch["s1"].n += 16
            block.tensor(_pe)

        pair_ctx = {}

        def emit_qproj_fm(s, p):
            qp0 = ch["qp"].n
            sev_prev = ch["sev"].n
            pair_ctx[p] = qp0

            def _fm_pe(t):
                t.wait_ge(ch["lnf"].sem, ch["lnf"].n)
                if p == 0:
                    t.wait_ge(ch["sev"].sem, sev_prev)
                for half in range(2):
                    if half == 1:
                        t.wait_ge(ch["qp"].sem, qp0 + 5)
                    for c in range(2):
                        t.matmul(out=PS2[0:64, c * 512:(c + 1) * 512],
                                 lhsT=wqt[0:64, p * 64:(p + 1) * 64],
                                 rhs=xnf[0:64, p * 2048 + half * 1024 + c * 512:
                                         p * 2048 + half * 1024 + (c + 1) * 512],
                                 start=True, stop=True, tile_position=(0, 0))
                        t.matmul(out=PS2[64:128, c * 512:(c + 1) * 512],
                                 lhsT=wqt[64:128, p * 64:(p + 1) * 64],
                                 rhs=xnf[64:128, p * 2048 + half * 1024 + c * 512:
                                         p * 2048 + half * 1024 + (c + 1) * 512],
                                 start=True, stop=True, tile_position=(64, 64))
                        inst = t.matmul(out=PS3[0:64, c * 512:(c + 1) * 512],
                                        lhsT=wqt[64:128, p * 64:(p + 1) * 64],
                                        rhs=xnf[64:128, p * 2048 + half * 1024 + c * 512:
                                                p * 2048 + half * 1024 + (c + 1) * 512],
                                        start=True, stop=True, tile_position=(64, 0))
                    inst.then_inc(ch["p2"].sem, 1)
                    pef(t, 0.003)
            p2q = ch["p2"].n
            ch["p2"].n += 2
            block.tensor(_fm_pe)

            def _fm_ev(sc):
                for half in range(2):
                    sc.wait_ge(ch["p2"].sem, p2q + half + 1)
                    hs = slice(half * 1024, (half + 1) * 1024)
                    for dst in (qA, qB):
                        sc.activation(out=dst[0:64, hs], in_=PS2[0:64, 0:1024],
                                      func=AF.Identity, bias=0.0, scale=1.0)
                    sc.activation(out=qC[64:128, hs], in_=PS2[64:128, 0:1024],
                                  func=AF.Identity, bias=0.0, scale=1.0)
                    sc.activation(out=qD[0:64, hs], in_=PS3[0:64, 0:1024],
                                  func=AF.Identity, bias=0.0, scale=1.0)
                    sc.activation(out=qE[0:64, hs], in_=PS3[0:64, 0:1024],
                                  func=AF.Identity, bias=0.0,
                                  scale=1.0).then_inc(ch["qp"].sem, 5)
            ch["qp"].n += 10
            block.scalar(_fm_ev)

            def _fm_gather(v):
                v.wait_ge(ch["qp"].sem, qp0 + 10)
                v.tensor_copy(out=qsub[0:64, :],
                              in_=qA[0:64, :].rearrange("p (a b) -> p a b", b=4)
                              [:, :, 0])
                v.tensor_copy(out=qsub[64:128, :],
                              in_=qC[64:128, :].rearrange("p (a b) -> p a b", b=4)
                              [:, :, 0]).then_inc(ch["qp"].sem, 1)
                vef(v, 0.05)
            ch["qp"].n += 1
            block.vector(_fm_gather)

        def emit_attn_main(s, p):
            STG = BI_STAGE
            qp0 = pair_ctx[p]
            if fq["act"]:
                def _af(sc):
                    aef(sc, 1.0)
                block.scalar(_af)

            resf_prev = ch["resf"].n
            ost_prev = ch["ost"].n

            def _qproj_tm(t):
                for bb in range(2):
                    if bb == 0:
                        t.wait_ge(ch["resf"].sem, resf_prev)
                        if p == 0:
                            t.wait_ge(ch["ost"].sem, ost_prev)
                    t.wait_ge(ch["qp"].sem, qp0 + 10 + (0 if bb == 0 else 2))
                    for k in range(8):
                        nt = bb * 8 + k
                        t.transpose(out=PS0B[:, k * 64:(k + 1) * 64],
                                    in_=qA[0:64, nt * 128:(nt + 1) * 128],
                                    identity=identb[0:64, 0:64])
                        inst = t.transpose(out=PS1B[:, k * 64:(k + 1) * 64],
                                           in_=qC[64:128, nt * 128:(nt + 1) * 128],
                                           identity=identb[64:128, 64:128])
                    inst.then_inc(ch["p2"].sem, 1)
                    pef(t, 0.003)
            p2q = ch["p2"].n
            ch["p2"].n += 2
            block.tensor(_qproj_tm)

            def _qproj_ev(v):
                for bb in range(2 if QPN >= 3 else 0):
                    v.wait_ge(ch["p2"].sem, p2q + 1 + bb)
                    v.tensor_copy(
                        out=qT[:, :].rearrange("p (ss c) -> p ss c", c=66)
                        [:, bb * 8:(bb + 1) * 8, 0:64],
                        in_=PS0B[:, 0:512].rearrange("p (k e) -> p k e", e=64))
                    v.tensor_copy(
                        out=qT[:, :].rearrange("p (ss c) -> p ss c", c=66)
                        [:, 16 + bb * 8:16 + (bb + 1) * 8, 0:64],
                        in_=PS1B[:, 0:512].rearrange("p (k e) -> p k e", e=64)
                        ).then_inc(ch["qp"].sem, 1)
                if QPN >= 4:
                    v.wait_ge(ch["qp"].sem, qp0 + 13)
                    qTv = qT[:, :].rearrange("p (ss c) -> p ss c", c=66)
                    for hh in range(4):
                        scr = gscr if hh % 2 == 0 else sscr
                        sl = slice(hh * 8, (hh + 1) * 8)
                        vedge(v, v.tensor_tensor(
                            out=scr[:, :].rearrange("p (ss e) -> p ss e", e=64),
                            in0=qTv[:, sl, 0:64], in1=qTv[:, sl, 0:64],
                            op=ALU.mult))
                        inst = v.reduce_sum(
                            out=diag[:, hh * 8:(hh + 1) * 8],
                            in_=scr[:, :].rearrange("p (ss e) -> p ss e", e=64),
                            axis=AX.X)
                        if hh < 3:
                            vedge(v, inst)
                    inst.then_inc(ch["qp"].sem, 1)
                    vef(v, 0.05)
            ch["qp"].n += 3
            block.vector(_qproj_ev)

            if STG < 3:
                return
            s10 = ch["s1"].n
            mx0 = ch["mx"].n

            qp_all = ch["qp"].n

            P1R = [PS0, PS2, PS3]

            def _p1_t(t, nt):
                if nt == 0:
                    t.wait_ge(ch["qp"].sem, qp_all - 3)
                if nt == 2:
                    t.wait_ge(ch["qp"].sem, qp_all - 1)
                PSe = P1R[(nt + 1) % 3]
                if nt >= 3:
                    t.wait_ge(ch["mx"].sem, mx0 + (nt - 2) * 2)
                t.matmul(out=PSe[0:64, 0:512],
                         lhsT=qA[0:64, nt * 128:nt * 128 + 64],
                         rhs=qsub[0:64, :], start=True, stop=True,
                         tile_position=(0, 0))
                t.matmul(out=PSe[64:128, 0:512],
                         lhsT=qA[0:64, nt * 128 + 64:nt * 128 + 128],
                         rhs=qsub[0:64, :], start=True, stop=True,
                         tile_position=(0, 64))
                t.matmul(out=PSe[0:64, 512:1024],
                         lhsT=qC[64:128, nt * 128:nt * 128 + 64],
                         rhs=qsub[64:128, :], start=True, stop=True,
                         tile_position=(64, 0))
                mm = t.matmul(out=PSe[64:128, 512:1024],
                              lhsT=qC[64:128, nt * 128 + 64:nt * 128 + 128],
                              rhs=qsub[64:128, :], start=True, stop=True,
                              tile_position=(64, 64))
                mm.then_inc(ch["s1"].sem, 1)
                pef(t, 0.003)

            def _p1_vv(v, nt):
                PSe = P1R[(nt + 1) % 3]
                v.wait_ge(ch["s1"].sem, s10 + nt + 1)
                v.reduce_max(out=subm[:, nt:nt + 1], in_=PSe[:, 0:512], axis=AX.X)
                inst = v.reduce_max(out=subm[:, 16 + nt:17 + nt],
                                    in_=PSe[:, 512:1024], axis=AX.X)
                inst.then_inc(ch["mx"].sem, 2)
                if nt == 15:
                    v.wait_ge(ch["mx"].sem, mx0 + 32)
                    v.wait_ge(ch["qp"].sem, qp_all)
                    v.tensor_max(out=mxc[:, :], in0=subm[:, :],
                                 in1=diag[:, :]).then_inc(ch["mx"].sem, 1)

            for nt in range(16):
                block.tensor(lambda t, nt=nt: _p1_t(t, nt))
                block.vector(lambda v, nt=nt: _p1_vv(v, nt))
            ch["s1"].n += 16
            ch["mx"].n += 33

            if STG < 4:
                return
            mrow0 = ch["mrow"].n

            def _mx_pe(t):
                t.wait_ge(ch["mx"].sem, ch["mx"].n)
                for h in range(2):
                    for half in range(2):
                        PSx = PS2 if h == 0 else PS3
                        r = 64 * half
                        for c in range(8):
                            nt = half * 8 + c
                            inst = t.matmul(out=PSx[r:r + 1, c * 128:(c + 1) * 128],
                                            lhsT=mxc[:, h * 16 + nt:h * 16 + nt + 1],
                                            rhs=identf[:, :], start=(c in (0, 4)),
                                            stop=(c in (3, 7)),
                                            tile_position=(0, r))
                        inst.then_inc(ch["p2"].sem, 1)
                        pef(t, 0.005)
            ch["p2"].n += 4
            block.tensor(_mx_pe)

            p2m = ch["p2"].n - 4

            mxrow_prev = ch["mxrow"].n

            def _mx_v(v):
                v.wait_ge(ch["p2"].sem, p2m + 4)
                for i in range(4):
                    h, half = divmod(i, 2)
                    PSx = PS2 if h == 0 else PS3
                    r = 64 * half
                    if i == 0:
                        v.wait_ge(ch["mxrow"].sem, mxrow_prev)
                    if i == 2:
                        v.wait_ge(ch["mxrow"].sem, mxrow_prev + 16)
                    v.tensor_scalar(out=mxst[0:1, half * 1024:(half + 1) * 1024],
                                    in0=PSx[r:r + 1, 0:1024], scalar1=-1.0, scalar2=None,
                                    op0=ALU.mult).then_inc(ch["mrow"].sem, 1)
                vef(v, 0.03)
            ch["mrow"].n += 4
            block.vector(_mx_v)

            def _mx_g(g):
                g.wait_ge(ch["mrow"].sem, mrow0 + 2)
                g.dma_start(out=qB[64:65, :], in_=mxst[0:1, :]).then_inc(ch["mxrow"].sem, 16)
                g.wait_ge(ch["mrow"].sem, mrow0 + 4)
                g.dma_start(out=qE[64:65, :], in_=mxst[0:1, :]).then_inc(ch["mxrow"].sem, 16)
            ch["mxrow"].n += 32
            block.sync(_mx_g)

            if STG < 5:
                return
            slot_map = []
            for nt in range(16):
                if nt < 7:
                    slot_map.append((PS0, nt * 66))
                elif nt < 14:
                    slot_map.append((PS0, 512 + (nt - 7) * 66))
                else:
                    slot_map.append((PS1, (nt - 14) * 66))

            for h in range(2):
                qlhs = qA if h == 0 else qD
                qrhs = qB if h == 0 else qE
                p20 = ch["p2"].n
                exp0 = ch["exp"].n
                av0 = ch["av"].n

                def _p2_grp(t, grp, h=h, qlhs=qlhs, qrhs=qrhs, p20=p20,
                            exp0=exp0, av0=av0):
                    if grp == 0:
                        t.wait_ge(ch["mxrow"].sem, ch["mxrow"].n - 16 * (1 - h))
                        if h == 0:
                            t.wait_ge(ch["mx"].sem, ch["mx"].n)
                    for it in range(grp * 4, grp * 4 + 4):
                        half, J = divmod(it, 16)
                        PSa = PS2 if it % 2 == 0 else PS3
                        if it >= 2:
                            t.wait_ge(ch["exp"].sem, exp0 + it - 1)
                        t.matmul(out=PSa[:, 0:512],
                                 lhsT=qlhs[0:96, J * 128:(J + 1) * 128],
                                 rhs=qrhs[0:96, half * 1024:half * 1024 + 512],
                                 start=True, stop=True)
                        t.matmul(out=PSa[:, 512:1024],
                                 lhsT=qlhs[0:96, J * 128:(J + 1) * 128],
                                 rhs=qrhs[0:96, half * 1024 + 512:half * 1024 + 1024],
                                 start=True, stop=True).then_inc(ch["p2"].sem, 1)
                        pef(t, 0.0145)
                        if it >= 1:
                            Jp = (it - 1) % 16
                            halfp = (it - 1) // 16
                            t.wait_ge(ch["exp"].sem, exp0 + it)
                            if h == 1 and it == 1 and BI_STAGE >= 6:
                                t.wait_ge(ch["divd"].sem, ch["divd"].n)
                            ptb = pt0 if (it - 1) % 2 == 0 else pt1
                            for k in range(8):
                                nt = halfp * 8 + k
                                pst, col = slot_map[nt]
                                inst = t.matmul(
                                    out=pst[:, col:col + 65],
                                    lhsT=ptb[:, k * 128:(k + 1) * 128],
                                    rhs=qT[:, (h * 16 + Jp) * 66:(h * 16 + Jp) * 66 + 65],
                                    start=(Jp == 0 and nt in (0, 7, 8, 14)),
                                    stop=(Jp == 15 and nt in (6, 7, 13, 15)))
                            if Jp == 15:
                                inst.then_inc(ch["av"].sem, 1)
                    if grp == 7:
                        t.wait_ge(ch["exp"].sem, exp0 + 32)
                        ptb = pt1
                        for k in range(8):
                            nt = 8 + k
                            pst, col = slot_map[nt]
                            inst = t.matmul(
                                out=pst[:, col:col + 65],
                                lhsT=ptb[:, k * 128:(k + 1) * 128],
                                rhs=qT[:, (h * 16 + 15) * 66:(h * 16 + 15) * 66 + 65],
                                start=False, stop=(nt in (13, 15)))
                        inst.then_inc(ch["av"].sem, 1)
                        pef(t, 0.0145)

                for grp in range(8):
                    block.tensor(lambda t, grp=grp: _p2_grp(t, grp))
                    if fq["vec"]:
                        block.vector(lambda v: vef(v, 0.03))
                ch["p2"].n += 32
                ch["av"].n += 2

                def _p2_act(sc, p20=p20, exp0=exp0, av0=av0):
                    for half in range(2):
                        for J in range(16):
                            it = half * 16 + J
                            sc.wait_ge(ch["p2"].sem, p20 + it + 1)
                            if it >= 2:
                                sc.wait_ge(ch["av"].sem, av0)
                            PSa = PS2 if it % 2 == 0 else PS3
                            ptb = pt0 if it % 2 == 0 else pt1
                            sc.activation(out=ptb[:, :], in_=PSa[:, :], func=AF.Exp,
                                          bias=cneg[:, :], scale=8.0).then_inc(
                                              ch["exp"].sem, 1)
                ch["exp"].n += 32
                block.scalar(_p2_act)

                div0 = ch["divd"].n

                if STG < 6:
                    continue_div = False
                else:
                    continue_div = True

                mxrow_now = ch["mxrow"].n

                def _div_v(v, h=h, av0=av0, mxrow_now=mxrow_now):
                    v.wait_ge(ch["mxrow"].sem, mxrow_now)
                    for half in range(2):
                        v.wait_ge(ch["av"].sem, av0 + 2)
                        for k in range(8):
                            nt = half * 8 + k
                            pst, col = slot_map[nt]
                            inst = v.reciprocal(out=rcz[:, nt:nt + 1],
                                                in_=pst[:, col + 64:col + 65])
                        vedge(v, inst)
                        for k in range(8):
                            nt = half * 8 + k
                            pst, col = slot_map[nt]
                            inst = v.tensor_scalar(
                                out=restm[:, nt * 128 + h * 64:nt * 128 + h * 64 + 64],
                                in0=pst[:, col:col + 64],
                                scalar1=rcz[:, nt:nt + 1], scalar2=None,
                                op0=ALU.mult)
                        inst.then_inc(ch["divd"].sem, 1)
                        vef(v, 0.1)
                if continue_div:
                    ch["divd"].n += 2
                    block.vector(_div_v)

        def emit_attn_res(s, p):
            rst0 = ch["rest"].n
            divn = ch["divd"].n

            def _res_pe(t):
                t.wait_ge(ch["divd"].sem, divn)
                for bb in range(4):
                    if bb >= 2:
                        t.wait_ge(ch["resf"].sem, ch["resf"].n + bb - 1)
                    pb = PS0B[:, (bb % 2) * 1024:(bb % 2) * 1024 + 512]
                    for k in range(4):
                        nt = bb * 4 + k
                        inst = t.transpose(out=pb[:, k * 128:(k + 1) * 128],
                                           in_=restm[:, nt * 128:(nt + 1) * 128],
                                           identity=identb[:, :])
                    inst.then_inc(ch["rest"].sem, 1)
                    pef(t, 0.003)
            ch["rest"].n += 4
            block.tensor(_res_pe)

            def _res_v(v):
                for bb in range(4):
                    v.wait_ge(ch["rest"].sem, rst0 + bb + 1)
                    pb = PS0B[:, (bb % 2) * 1024:(bb % 2) * 1024 + 512]
                    v.tensor_copy(out=resf[:, p * 2048 + bb * 512:p * 2048 + (bb + 1) * 512],
                                  in_=pb[:, 0:512]).then_inc(ch["resf"].sem, 1)
                vef(v, 0.08)
            ch["resf"].n += 4
            block.vector(_res_v)

        seq_silu = {}
        cnt = {"wps": 0}

        from collections import deque
        fq = {"pe": deque(), "vec": deque(), "act": deque()}
        pace_d = {"pe": 0.0, "vec": 0.0, "act": 0.0}
        pace_t = {"pe": 0.0, "vec": 0.0, "act": 0.0}
        pair_budget = {"pe": 0.0, "vec": 0.0, "act": 0.0}

        emt = {"h": 0, "gy": 0, "hs": 0, "gys": 0, "gact": 0}

        def _fill(qn, eng, frac):
            pace_t[qn] += frac * pair_budget[qn]
            q = fq[qn]
            eager = qn == "vec"
            while q and (eager or pace_d[qn] + q[0][0] * 0.5 <= pace_t[qn]):
                c, fn, reqs, prov = q[0]
                if any(emt[k] < v for k, v in reqs):
                    break
                q.popleft()
                fn(eng)
                emt[prov] += 1
                pace_d[qn] += c

        def pef(t, frac):
            _fill("pe", t, frac)

        def vef(v, frac):
            _fill("vec", v, frac)

        def aef(sc, frac):
            _fill("act", sc, frac)

        def pace_reset():
            for qn in fq:
                pair_budget[qn] = sum(e[0] for e in fq[qn]) / 8.5
                pace_d[qn] = 0.0
                pace_t[qn] = 0.0

        def drain_all():
            qs = {"pe": block.tensor, "vec": block.vector, "act": block.scalar}
            while any(fq[qn] for qn in qs):
                progress = False
                for qn, emit in qs.items():
                    runnable = []
                    while fq[qn]:
                        c, fn, reqs, prov = fq[qn][0]
                        if any(emt[k] < v for k, v in reqs):
                            break
                        fq[qn].popleft()
                        runnable.append(fn)
                        emt[prov] += 1
                    if runnable:
                        progress = True

                        def _sec(e, runnable=runnable):
                            for fn in runnable:
                                fn(e)
                        emit(_sec)
                assert progress, "ffn queue drain deadlock"

        def plan_ffn_chunk(s, ci, defer_dma=None):
            tc, half = divmod(ci, 2)
            fb = PS1[:, 512:1024]
            w1P = [ch["lw1A"], ch["lw1B"], ch["lw1C"]]
            w1bufs = [w1s0, w1s1, w1s2]
            w2P = [ch["lw2A"], ch["lw2B"]]
            M0 = ch["h"].n
            G0 = ch["gy"].n
            gy_now = ch["gy"].n
            gys_now = ch["gys"].n
            sg_now = ch["sg"].n
            gact0 = ch["gact"].n
            lnf_now = ch["lnf"].n

            w1_wait = []
            for k in range(16):
                par = (M0 + k) % 3
                w1P[par].n += 16
                w1_wait.append(w1P[par].n)
            w2_wait = []
            if half == 1:
                for dt in range(8):
                    w2P[0].n += 16
                    a = w2P[0].n
                    w2P[1].n += 16
                    w2_wait.append((a, w2P[1].n))

            yc_now = ch["yc"].n

            def _dma(g, M0=M0, G0=G0, tc=tc, half=half, ci=ci, yc_now=yc_now):
                for k in range(16):
                    M = M0 + k
                    if M >= 3:
                        g.wait_ge(ch["h"].sem, M - 2)
                    g.dma_start(out=w1bufs[M % 3][:, :],
                                in_=bw1t_d[half * 16 + k]).then_inc(w1P[M % 3].sem, 16)
                if half == 1:
                    for dt in range(8):
                        if G0 + dt >= 1:
                            g.wait_ge(ch["gyh"].sem, G0 + dt)
                        g.dma_start(out=w2s0[:, :],
                                    in_=bw2t_d[dt][:, 0:2048]).then_inc(w2P[0].sem, 16)
                        if G0 + dt >= 1:
                            g.wait_ge(ch["gy"].sem, G0 + dt)
                        g.dma_start(out=w2s1[:, :],
                                    in_=bw2t_d[dt][:, 2048:4096]).then_inc(w2P[1].sem, 16)
            if defer_dma is None:
                block.gpsimd(_dma)
            else:
                defer_dma.append(_dma)

            for k in range(16):
                def _h(t, k=k, M0=M0, tc=tc, half=half, ci=ci,
                       w1_wait=w1_wait, gys_now=gys_now, lnf_now=lnf_now):
                    M = M0 + k
                    par = M % 3
                    t.wait_ge(w1P[par].sem, w1_wait[k])
                    if ci == 0 and k == 0:
                        t.wait_ge(ch["lnf"].sem, lnf_now)
                    if half == 0 and k == 0:
                        t.wait_ge(ch["gys"].sem, gys_now)
                    if M >= 1:
                        t.wait_ge(ch["hs"].sem, M)
                    w1b = w1bufs[par]
                    for dc in range(8):
                        inst = t.matmul(
                            out=fb, lhsT=w1b[:, dc * 128:(dc + 1) * 128],
                            rhs=xnf[:, dc * 2048 + tc * 512:dc * 2048 + (tc + 1) * 512],
                            start=(dc == 0), stop=(dc == 7))
                    inst.then_inc(ch["h"].sem, 1)
                fq["pe"].append((1700, _h, [("hs", M0 + k)], "h"))
            if half == 1:
                for dt in range(8):
                    def _g(t, dt=dt, M0=M0, G0=G0, tc=tc,
                           w2_wait=w2_wait, gact0=gact0):
                        G = G0 + dt
                        t.wait_ge(w2P[0].sem, w2_wait[dt][0])
                        if dt == 0:
                            t.wait_ge(ch["gact"].sem, gact0 + 1)
                            t.wait_ge(ch["hs"].sem, M0 + 16)
                        else:
                            t.wait_ge(ch["gys"].sem, G)
                        for kk in range(32):
                            if kk == 16:
                                t.wait_ge(w2P[1].sem, w2_wait[dt][1])
                            w2b = w2s0 if kk < 16 else w2s1
                            inst = t.matmul(out=fb,
                                            lhsT=w2b[:, (kk % 16) * 128:(kk % 16 + 1) * 128],
                                            rhs=hsb[:, kk * 512:(kk + 1) * 512],
                                            start=(kk == 0), stop=(kk == 31))
                            if kk == 15:
                                inst.then_inc(ch["gyh"].sem, 1)
                        inst.then_inc(ch["gy"].sem, 1)
                    fq["pe"].append(
                        (6800, _g,
                         [("gact", gact0 + 1), ("hs", M0 + 16)] if dt == 0
                         else [("gys", G0 + dt)], "gy"))
            ch["h"].n += 16
            if half == 1:
                ch["gy"].n += 8
                ch["gyh"].n += 8

            for k in range(16):
                def _e(v, k=k, M0=M0, half=half, gy_now=gy_now):
                    M = M0 + k
                    v.wait_ge(ch["h"].sem, M + 1)
                    if half == 0 and k == 0:
                        v.wait_ge(ch["gy"].sem, gy_now)
                    ht = half * 16 + k
                    v.tensor_scalar(out=hsb[:, ht * 512:(ht + 1) * 512], in0=fb,
                                    scalar1=b1c[:, ht:ht + 1], scalar2=None,
                                    op0=ALU.add).then_inc(ch["hs"].sem, 1)
                fq["vec"].append((250, _e, [("h", M0 + k + 1)], "hs"))
            if half == 1:
                for dt in range(8):
                    def _ge(v, dt=dt, G0=G0, tc=tc, ci=ci, s=s, sg_now=sg_now):
                        G = G0 + dt
                        v.wait_ge(ch["gy"].sem, G + 1)
                        if s > 0 and ci == 1 and dt == 0:
                            v.wait_ge(ch["sg"].sem, sg_now)
                        v.tensor_scalar(out=gfm[:, tc * 4096 + dt * 512:
                                                 tc * 4096 + (dt + 1) * 512],
                                        in0=fb, scalar1=b2c[:, dt:dt + 1], scalar2=None,
                                        op0=ALU.add).then_inc(ch["gys"].sem, 1)
                    fq["vec"].append((250, _ge, [("gy", G0 + dt + 1)], "gys"))
            ch["hs"].n += 16
            if half == 1:
                ch["gys"].n += 8

            def _gelu(sc, M0=M0, half=half):
                sc.wait_ge(ch["hs"].sem, M0 + 16)
                sc.activation(out=hsb[:, half * 8192:(half + 1) * 8192],
                              in_=hsb[:, half * 8192:(half + 1) * 8192],
                              func=AF.Gelu, bias=0.0, scale=1.0).then_inc(
                    ch["gact"].sem, 1)
            fq["act"].append((7100, _gelu, [("hs", M0 + 16)], "gact"))
            ch["gact"].n += 1
            if half == 1:
                def _silu(sc, G0=G0, tc=tc):
                    sc.wait_ge(ch["gys"].sem, G0 + 8)
                    sc.activation(out=gfm[:, tc * 4096:(tc + 1) * 4096],
                                  in_=gfm[:, tc * 4096:(tc + 1) * 4096],
                                  func=AF.Silu, bias=0.0, scale=1.0).then_inc(
                        ch["gact"].sem, 1)
                fq["act"].append((3700, _silu, [("gys", G0 + 8)], "gact"))
                ch["gact"].n += 1
                seq_silu[(s, tc)] = ch["gact"].n

        def emit_combine_a(s):
            lwpP = [ch["lwpA"], ch["lwpB"]]
            ouP = [ch["ooutA"], ch["ooutB"]]
            resf_end = ch["resf"].n
            divd_end = ch["divd"].n
            exp_end = ch["exp"].n
            c = {"s": s, "Y0": ch["yc"].n, "W0": cnt["wps"], "sg0": ch["sg"].n,
                 "sev0": ch["sev"].n, "ost0": ch["ost"].n,
                 "silu_all": seq_silu[(s, 3)], "ou0": [ouP[0].n, ouP[1].n]}
            Y0, W0, sg0, ost0 = c["Y0"], c["W0"], c["sg0"], c["ost0"]
            h_now = ch["h"].n

            wps_wait = []
            for j in range(32):
                par = (W0 + j) % 2
                lwpP[par].n += 16
                wps_wait.append(lwpP[par].n)
            cnt["wps"] += 32
            lxrP = [ch["lxrA"], ch["lxrB"]]
            xres_ready = []
            for j in range(32):
                lxrP[j % 2].n += 16
                xres_ready.append(lxrP[j % 2].n)
            c["xres_ready"] = xres_ready
            c["ou_abs"] = [c["ou0"][j % 2] + 64 * (j // 2 + 1) for j in range(32)]
            YPS = [PS2[:, 0:512], PS2[:, 512:1024], PS3[:, 0:512], PS3[:, 512:1024]]
            c["YPS"] = YPS

            def _dma(g):
                for j in range(32):
                    dt = j % 8
                    Wg = W0 + j
                    if Wg >= 2:
                        g.wait_ge(ch["yc"].sem, Wg - 1)
                    g.dma_start(out=(wps0 if Wg % 2 == 0 else wps1)[:, :],
                                in_=wpt_d[dt]).then_inc(lwpP[Wg % 2].sem, 16)
            block.gpsimd(_dma)

            def _xld(sy, j):
                tc, dt = divmod(j, 8)
                sy.dma_start(
                    out=(xrj0 if j % 2 == 0 else xrj1)[:, :]
                    .rearrange("p (k d) -> p k d", d=128),
                    in_=x_in[s, tc * 512:(tc + 1) * 512, dt * 128:(dt + 1) * 128]
                    .rearrange("(k p) d -> p k d", p=128)
                    ).then_inc(lxrP[j % 2].sem, 16)
            c["xld"] = _xld

            def _sgT(t, j):
                Yg = Y0 + j
                t.wait_ge(ch["sg"].sem, sg0 + j + 1)
                if Yg >= 2:
                    t.wait_ge(ch["ost"].sem, Yg - 1)
                for k in range(4):
                    inst = t.transpose(
                        out=PS0[:, (j % 2) * 512 + k * 128:(j % 2) * 512 + (k + 1) * 128],
                        in_=sgscr[:, k * 128:(k + 1) * 128], identity=identf[:, :])
                inst.then_inc(ch["sgt"].sem, 1)

            def _pe(t):
                t.wait_ge(ch["divd"].sem, divd_end)
                t.wait_ge(ch["exp"].sem, exp_end)
                t.wait_ge(ch["resf"].sem, resf_end)
                for j in range(32):
                    tc, dt = divmod(j, 8)
                    Yg = Y0 + j
                    par = (W0 + j) % 2
                    t.wait_ge(lwpP[par].sem, wps_wait[j])
                    if Yg >= 4:
                        t.wait_ge(ch["sev"].sem, Yg - 3)
                    buf = wps0 if par == 0 else wps1
                    for dc in range(8):
                        inst = t.matmul(
                            out=YPS[j % 4],
                            lhsT=buf[:, dc * 128:(dc + 1) * 128],
                            rhs=resf[:, dc * 2048 + tc * 512:dc * 2048 + (tc + 1) * 512],
                            start=(dc == 0), stop=(dc == 7))
                    inst.then_inc(ch["yc"].sem, 1)
                    if j >= 1:
                        _sgT(t, j - 1)
                _sgT(t, 31)
            ch["yc"].n += 32
            ch["sgt"].n += 32
            block.tensor(_pe)

            def _act(sc):
                for j in range(32):
                    dt = j % 8
                    Yg = Y0 + j
                    sc.wait_ge(ch["yc"].sem, Yg + 1)
                    if Yg >= 2:
                        sc.wait_ge(ch["sg"].sem, Yg - 1)
                    sc.activation(out=(sscr if j % 2 == 0 else gscr)[:, :],
                                  in_=YPS[j % 4],
                                  func=AF.Identity, bias=bpc[:, dt:dt + 1],
                                  scale=1.0).then_inc(ch["sev"].sem, 1)
            ch["sev"].n += 32
            block.scalar(_act)
            return c

        def emit_combine_b(c):
            s = c["s"]
            ouP = [ch["ooutA"], ch["ooutB"]]
            lxrP = [ch["lxrA"], ch["lxrB"]]
            Y0, sg0, sev0, ost0 = c["Y0"], c["sg0"], c["sev0"], c["ost0"]

            def _vec(v):
                for j in range(32):
                    tc, dt = divmod(j, 8)
                    Yg = Y0 + j
                    v.wait_ge(ch["sev"].sem, sev0 + j + 1)
                    if j == 0:
                        v.wait_ge(ch["gact"].sem, c["silu_all"])
                    if Yg >= 1:
                        v.wait_ge(ch["sgt"].sem, Yg)
                    v.tensor_tensor(out=sgscr[:, :],
                                    in0=gfm[:, tc * 4096 + dt * 512:
                                            tc * 4096 + (dt + 1) * 512],
                                    in1=(sscr if j % 2 == 0 else gscr)[:, :],
                                    op=ALU.mult).then_inc(ch["sg"].sem, 1)
                    v.wait_ge(ch["sgt"].sem, Yg + 1)
                    v.wait_ge(lxrP[j % 2].sem, c["xres_ready"][j])
                    if j >= 2:
                        v.wait_ge(ouP[j % 2].sem, c["ou_abs"][j - 2])
                    else:
                        v.wait_ge(ouP[j % 2].sem, c["ou0"][j % 2])
                    stg = sgst0 if j % 2 == 0 else sgst1
                    v.tensor_tensor(
                        out=stg[:, :].rearrange("p (k d) -> p k d", d=128),
                        in0=PS0[:, (j % 2) * 512:(j % 2) * 512 + 512]
                        .rearrange("p (k d) -> p k d", d=128),
                        in1=(xrj0 if j % 2 == 0 else xrj1)[:, :]
                        .rearrange("p (k d) -> p k d", d=128),
                        op=ALU.add).then_inc(ch["ost"].sem, 1)
            ch["sg"].n += 32
            ch["ost"].n += 32
            block.vector(_vec)

            def _out(sy):
                sy.wait_ge(ch["ost"].sem, ost0)
                c["xld"](sy, 0)
                c["xld"](sy, 1)
                for j in range(32):
                    tc, dt = divmod(j, 8)
                    sy.wait_ge(ch["ost"].sem, ost0 + j + 1)
                    if j + 2 < 32:
                        c["xld"](sy, j + 2)
                    stg = sgst0 if j % 2 == 0 else sgst1
                    for k in range(4):
                        sy.dma_start(
                            out=out_d[s, tc * 512 + k * 128:tc * 512 + (k + 1) * 128,
                                      dt * 128:(dt + 1) * 128],
                            in_=stg[:, k * 128:(k + 1) * 128]
                            ).then_inc(ouP[j % 2].sem, 16)
            ch["ooutA"].n += 16 * 64
            ch["ooutB"].n += 16 * 64
            block.sync(_out)

        ln_done = set()
        planned = set()
        for s in range(min(SEQ, BI_SEQ)):
            if BI_STAGE >= 1 and s not in ln_done:
                emit_ln(s)
            full = BI_STAGE >= 8 and BI_PAIR >= 8
            if full and s not in planned:
                for ci in range(8):
                    plan_ffn_chunk(s, ci)
                pace_reset()
                planned.add(s)
            if BI_STAGE >= 2:
                emit_qproj_fm(s, 0)
                for p in range(min(8, BI_PAIR)):
                    emit_attn_main(s, p)
                    if p + 1 < min(8, BI_PAIR):
                        emit_qproj_fm(s, p + 1)
                    emit_attn_res(s, p)
            if DBG and s == 0:
                def _dbg0(sy):
                    sy.wait_ge(ch["resf"].sem, ch["resf"].n)
                    sy.dma_start(out=dbg["xnf"], in_=xnf[:, :]).then_inc(ch["oout"].sem, 16)
                    sy.dma_start(out=dbg["resf"], in_=resf[:, :]).then_inc(ch["oout"].sem, 16)
                    sy.dma_start(out=dbg["qA"], in_=qA[:, :]).then_inc(ch["oout"].sem, 16)
                    sy.dma_start(out=dbg["qT"], in_=qT[:, :]).then_inc(ch["oout"].sem, 16)
                    sy.dma_start(out=dbg["mxc"], in_=mxc[:, :]).then_inc(ch["oout"].sem, 16)
                    sy.dma_start(out=dbg["pt"], in_=pt1[:, :]).then_inc(ch["oout"].sem, 16)
                ch["oout"].n += 96
                block.sync(_dbg0)
            if full:
                drain_all()
                boundary = BI_SEQ > SEQ - 1 and s + 1 < min(SEQ, BI_SEQ)
                if boundary:
                    emit_ln(s + 1)
                    ln_done.add(s + 1)
                    dma_defer = []
                    for ci in range(8):
                        plan_ffn_chunk(s + 1, ci, defer_dma=dma_defer)
                    pace_reset()
                    planned.add(s + 1)
                cctx = emit_combine_a(s)
                if boundary:
                    for f in dma_defer:
                        block.gpsimd(f)
                emit_combine_b(cctx)
                if boundary:
                    for _ in range(24):
                        block.tensor(lambda t: pef(t, 0.08))
                        block.vector(lambda v: vef(v, 0.07))
            if DBG and s == 0:
                def _dbg1(sy):
                    sy.wait_ge(ch["oout"].sem, ch["oout"].n)
                    sy.dma_start(out=dbg["hsb"], in_=hsb[:, :]).then_inc(ch["oout"].sem, 16)
                ch["oout"].n += 16
                block.sync(_dbg1)

        def _final(sy):
            sy.wait_ge(ch["ooutA"].sem, ch["ooutA"].n)
            sy.wait_ge(ch["ooutB"].sem, ch["ooutB"].n)
        block.sync(_final)

    return nc


def kernel(L0_attn_x, L0_attn_Wq, L0_attn_Wp, L0_attn_bp,
           L0_ffn_w1, L0_ffn_b1, L0_ffn_w2, L0_ffn_b2):
    x = np.ascontiguousarray(np.asarray(L0_attn_x, dtype=np.float32))
    wq = np.ascontiguousarray(np.asarray(L0_attn_Wq, dtype=np.float32))
    wp = np.ascontiguousarray(np.asarray(L0_attn_Wp, dtype=np.float32))
    bp = np.ascontiguousarray(np.asarray(L0_attn_bp, dtype=np.float32))
    w1 = np.ascontiguousarray(np.asarray(L0_ffn_w1, dtype=np.float32))
    b1 = np.ascontiguousarray(np.asarray(L0_ffn_b1, dtype=np.float32))
    w2 = np.ascontiguousarray(np.asarray(L0_ffn_w2, dtype=np.float32))
    b2 = np.ascontiguousarray(np.asarray(L0_ffn_b2, dtype=np.float32))

    nc = build_nc()
    in_maps = []
    for c in range(NCORE):
        in_maps.append({
            "x": np.ascontiguousarray(x[c * SEQ:(c + 1) * SEQ]),
            "wq": wq, "wp": wp, "bp": bp, "w1": w1, "b1": b1, "w2": w2, "b2": b2,
        })
    res = run_bass_kernel_spmd(nc, in_maps, core_ids=list(range(NCORE)))
    out = np.concatenate([np.asarray(r["out"]) for r in res.results], axis=0)
    if DBG:
        kernel.dbg = res.results
    return out


# version 29
# speedup vs baseline: 1.0083x; 1.0053x over previous
import numpy as np
import concourse.bass as bass
import concourse.mybir as mybir
from concourse.bass_utils import run_bass_kernel_spmd

F32 = mybir.dt.float32
BF16 = mybir.dt.bfloat16
I32 = mybir.dt.int32
AF = mybir.ActivationFunctionType
AX = mybir.AxisListType
ALU = mybir.AluOpType

NCORE = 8
SEQ = 4
N = 2048
D = 1024
NH = 16
E = 64
HF = 4096
TC = 512
DBG = False
import os
BI_SEQ = int(os.environ.get("BI_SEQ", "4"))
BI_PAIR = int(os.environ.get("BI_PAIR", "8"))
BI_STAGE = int(os.environ.get("BI_STAGE", "9"))
BI_PREP = int(os.environ.get("BI_PREP", "9"))
NOALT = int(os.environ.get("NOALT", "0"))
QPN = int(os.environ.get("QPN", "4"))
P1M = int(os.environ.get("P1M", "2"))
W1N = int(os.environ.get("W1N", "32"))
W2N = int(os.environ.get("W2N", "8"))
WPN = int(os.environ.get("WPN", "8"))
BI_PREP = int(os.environ.get("BI_PREP", "9"))
NOALT = int(os.environ.get("NOALT", "0"))
QPN = int(os.environ.get("QPN", "4"))
P1M = int(os.environ.get("P1M", "2"))


class Ch:

    def __init__(self, es, nc, name):
        self.sem = es.enter_context(nc.semaphore(name))
        self.n = 0

    def inc(self, inst, k=1):
        inst.then_inc(self.sem, k)
        self.n += k
        return self.n


def build_nc():
    import contextlib
    nc = bass.Bass("TRN2", target_bir_lowering=False)

    x_in = nc.dram_tensor("x", [SEQ, N, D], F32, kind="ExternalInput").ap()
    wq_in = nc.dram_tensor("wq", [NH, E, E], F32, kind="ExternalInput").ap()
    wp_in = nc.dram_tensor("wp", [D, D], F32, kind="ExternalInput").ap()
    bp_in = nc.dram_tensor("bp", [D], F32, kind="ExternalInput").ap()
    w1_in = nc.dram_tensor("w1", [HF, D], F32, kind="ExternalInput").ap()
    b1_in = nc.dram_tensor("b1", [HF], F32, kind="ExternalInput").ap()
    w2_in = nc.dram_tensor("w2", [D, HF], F32, kind="ExternalInput").ap()
    b2_in = nc.dram_tensor("b2", [D], F32, kind="ExternalInput").ap()
    out_d = nc.dram_tensor("out", [SEQ, N, D], F32, kind="ExternalOutput").ap()

    bw1t_d = nc.dram_tensor("bw1t_d", [HF // 128, 128, 1024], BF16, kind="Internal").ap()
    bw2t_d = nc.dram_tensor("bw2t_d", [D // 128, 128, 4096], BF16, kind="Internal").ap()
    wpt_d = nc.dram_tensor("wpt_d", [8, 128, 1024], BF16, kind="Internal").ap()

    dbg = {}
    if DBG:
        dbg["xnf"] = nc.dram_tensor("dbg_xnf", [128, 16384], BF16, kind="ExternalOutput").ap()
        dbg["resf"] = nc.dram_tensor("dbg_resf", [128, 16384], BF16, kind="ExternalOutput").ap()
        dbg["qA"] = nc.dram_tensor("dbg_qA", [128, 2048], BF16, kind="ExternalOutput").ap()
        dbg["qT"] = nc.dram_tensor("dbg_qT", [128, 2112], BF16, kind="ExternalOutput").ap()
        dbg["mxc"] = nc.dram_tensor("dbg_mxc", [128, 32], F32, kind="ExternalOutput").ap()
        dbg["pt"] = nc.dram_tensor("dbg_pt", [128, 1024], BF16, kind="ExternalOutput").ap()
        dbg["bw1t"] = nc.dram_tensor("dbg_bw1t", [128, 1024], BF16, kind="ExternalOutput").ap()
        dbg["hsb"] = nc.dram_tensor("dbg_hsb", [128, 16384], BF16, kind="ExternalOutput").ap()

    es = contextlib.ExitStack()
    with es:
        EC = es.enter_context
        xnf = EC(nc.sbuf_tensor("xnf", [128, 16384], BF16))
        resf = EC(nc.sbuf_tensor("resf", [128, 16384], BF16))
        gfm = EC(nc.sbuf_tensor("gfm", [128, 16384], BF16))
        wqt = EC(nc.sbuf_tensor("wqt", [128, 512], BF16))
        bpc = EC(nc.sbuf_tensor("bpc", [128, 8], F32))
        b1c = EC(nc.sbuf_tensor("b1c", [128, 32], F32))
        b2c = EC(nc.sbuf_tensor("b2c", [128, 8], F32))
        identb = EC(nc.sbuf_tensor("identb", [128, 128], BF16))
        identf = EC(nc.sbuf_tensor("identf", [128, 128], F32))
        PS0 = EC(nc.psum_tensor("PS0", [128, 1024], F32))
        PS1 = EC(nc.psum_tensor("PS1", [128, 1024], F32))
        PS2 = EC(nc.psum_tensor("PS2", [128, 1024], F32))
        PS3 = EC(nc.psum_tensor("PS3", [128, 1024], F32))
        PS2B = PS2.bitcast(BF16)
        PS3B = PS3.bitcast(BF16)
        PS0B = PS0.bitcast(BF16)
        PS1B = PS1.bitcast(BF16)

        block = EC(nc.Block())

        ch = {n: Ch(es, nc, n) for n in [
            "ldp", "bwout", "qp", "s1", "mx", "mxrow", "p2", "exp", "av",
            "divd", "rest", "resf", "lh", "lw1", "lw2", "lxr", "h", "hs",
            "gy", "gys", "sg", "sgt", "ost", "oout", "lnx", "lnst", "lnt",
            "lnf", "misc", "mrow", "vch", "ach", "ldpA", "ldpB", "lnxA", "lnxB",
            "lw1A", "lw1B", "lw2A", "lw2B", "ooutA", "ooutB",
            "yc", "sev", "gyh", "gact", "lwpA", "lwpB", "lxrA", "lxrB"]}

        def vedge(v, inst):
            inst.then_inc(ch["vch"].sem, 1)
            ch["vch"].n += 1
            v.wait_ge(ch["vch"].sem, ch["vch"].n)

        def aedge(a, inst):
            inst.then_inc(ch["ach"].sem, 1)
            ch["ach"].n += 1
            a.wait_ge(ch["ach"].sem, ch["ach"].n)

        with nc.sbuf_tensor("pstage0", [128, 4096], F32) as pstage0, \
             nc.sbuf_tensor("pstage1", [128, 4096], F32) as pstage1, \
             nc.sbuf_tensor("psgn", [128, 4096], BF16) as psgn, \
             nc.sbuf_tensor("psgn2", [128, 4096], BF16) as psgn2, \
             nc.sbuf_tensor("ptout", [128, 4096], BF16) as ptout, \
             nc.sbuf_tensor("palph", [128, 4], F32) as palph, \
             nc.sbuf_tensor("prow", [128, 128], F32) as prow, \
             nc.sbuf_tensor("iscr", [128, 128], I32) as iscr:

            pstages = [pstage0, pstage1]

            def _prep_gpsimd(g):
                g.iota(iscr[:, :], pattern=[[1, 128]], base=0,
                       channel_multiplier=-1).then_inc(ch["misc"].sem, 1)
            ch["misc"].n += 1
            block.gpsimd(_prep_gpsimd)

            m_iota = ch["misc"].n

            def _prep_vec0(v):
                v.wait_ge(ch["misc"].sem, m_iota)
                v.tensor_single_scalar(out=identb[:, :], in_=iscr[:, :], scalar=0,
                                       op=ALU.is_equal)
                v.tensor_single_scalar(out=identf[:, :], in_=iscr[:, :], scalar=0,
                                       op=ALU.is_equal).then_inc(ch["misc"].sem, 1)
            ch["misc"].n += 1
            block.vector(_prep_vec0)
            m_ident = ch["misc"].n
            m_biaspe_g = [m_ident]

            if BI_PREP >= 2:
                def _prep_bias_sync(s):
                    s.dma_start(out=prow[0:8, 0:128], in_=bp_in.rearrange("(a b) -> a b", b=128)).then_inc(ch["ldp"].sem, 16)
                    s.wait_ge(ch["misc"].sem, 1)
                    s.dma_start(out=prow[64:72, 0:128], in_=b2_in.rearrange("(a b) -> a b", b=128)).then_inc(ch["ldp"].sem, 16)
                    s.wait_ge(ch["misc"].sem, 1)
                    s.dma_start(out=prow[32:64, 0:128], in_=b1_in.rearrange("(a b) -> a b", b=128)).then_inc(ch["ldp"].sem, 16)
                    s.wait_ge(ch["misc"].sem, 1)
                ch["ldp"].n += 48
                block.sync(_prep_bias_sync)

                ldp_bias = ch["ldp"].n

                def _prep_bias_pe(t):
                    t.wait_ge(ch["ldp"].sem, ldp_bias)
                    t.wait_ge(ch["misc"].sem, m_ident)
                    t.matmul(out=PS0[:, 0:8], lhsT=prow[0:8, 0:128],
                             rhs=identf[0:8, 0:8], start=True, stop=True)
                    t.matmul(out=PS0[:, 8:16], lhsT=prow[64:72, 0:128],
                             rhs=identf[64:72, 64:72], start=True, stop=True)
                    t.matmul(out=PS1[:, 0:32], lhsT=prow[32:64, 0:128],
                             rhs=identf[32:64, 32:64], start=True,
                             stop=True).then_inc(ch["misc"].sem, 1)
                ch["misc"].n += 1
                block.tensor(_prep_bias_pe)

                m_biaspe = ch["misc"].n
                m_biaspe_g[0] = m_biaspe

                def _prep_bias_v(v):
                    v.wait_ge(ch["misc"].sem, m_biaspe)
                    v.tensor_copy(out=bpc[:, :], in_=PS0[:, 0:8])
                    v.tensor_copy(out=b2c[:, :], in_=PS0[:, 8:16])
                    v.tensor_copy(out=b1c[:, :],
                                  in_=PS1[:, 0:32]).then_inc(ch["misc"].sem, 1)
                ch["misc"].n += 1
                block.vector(_prep_bias_v)

            if BI_PREP >= 3:
                wq_bw0s = ch["bwout"].n

                def _prep_wq_sync(s):
                    s.wait_ge(ch["misc"].sem, m_biaspe_g[0])
                    for p in range(8):
                        if p >= 1:
                            s.wait_ge(ch["bwout"].sem, wq_bw0s + p * 16)
                        s.dma_start(out=prow[0:64, 0:64],
                                    in_=wq_in[2 * p]).then_inc(ch["ldp"].sem, 16)
                        s.wait_ge(ch["misc"].sem, 1)
                        s.dma_start(out=prow[0:64, 64:128],
                                    in_=wq_in[2 * p + 1]).then_inc(ch["ldp"].sem, 16)
                        s.wait_ge(ch["misc"].sem, 1)
                        s.wait_ge(ch["mrow"].sem, (p + 1))
                ch["ldp"].n += 8 * 32
                block.sync(_prep_wq_sync)

                wq_bw0 = ch["bwout"].n
                wq_s10 = ch["s1"].n

                def _prep_wq_v(v):
                    base = ch["ldp"].n - 8 * 32
                    for p in range(8):
                        v.wait_ge(ch["ldp"].sem, base + (p + 1) * 32)
                        v.tensor_copy(out=psgn[0:64, 0:128], in_=prow[0:64, 0:128]).then_inc(ch["misc"].sem, 1)
                        ch["misc"].n += 1
                        v.wait_ge(ch["s1"].sem, wq_s10 + 2 * (p + 1))
                        if p >= 1:
                            v.wait_ge(ch["bwout"].sem, wq_bw0 + p * 16)
                        v.tensor_copy(out=wqt[0:64, p * 64:(p + 1) * 64], in_=PS2B[0:64, 0:64])
                        v.tensor_copy(out=prow[0:64, 64:96].bitcast(BF16)[0:64, 0:64],
                                      in_=PS2B[0:64, 128:192]).then_inc(ch["mrow"].sem, 1)
                block.vector(_prep_wq_v)

                def _prep_wq_pe(t):
                    for p in range(8):
                        t.wait_ge(ch["misc"].sem, ch["misc"].n - 8 + (p + 1))
                        t.transpose(out=PS2B[0:64, 0:64], in_=psgn[0:64, 0:64],
                                    identity=identb[0:64, 0:64]).then_inc(ch["s1"].sem, 1)
                        ch["s1"].n += 1
                        t.transpose(out=PS2B[0:64, 128:192], in_=psgn[0:64, 64:128],
                                    identity=identb[0:64, 0:64]).then_inc(ch["s1"].sem, 1)
                        ch["s1"].n += 1
                block.tensor(_prep_wq_pe)

                def _prep_wq_g(g):
                    for p in range(8):
                        g.wait_ge(ch["mrow"].sem, p + 1)
                        g.dma_start(out=wqt[64:128, p * 64:(p + 1) * 64],
                                    in_=prow[0:64, 64:96].bitcast(BF16)[0:64, 0:64]
                                    ).then_inc(ch["bwout"].sem, 16)
                ch["bwout"].n += 8 * 16
                ch["mrow"].n += 8
                block.gpsimd(_prep_wq_g)

            if BI_PREP >= 4:
                mxb0 = ch["mx"].n
                expb0 = ch["exp"].n

                ldP = [ch["ldpA"], ch["ldpB"]]
                base_p = [ldP[0].n, ldP[1].n]

                def wload_sem(g):
                    return ldP[g % 2], base_p[g % 2] + (g // 2 + 1) * 16

                def _prep_w_sync(s):
                    s.wait_ge(ch["bwout"].sem, ch["bwout"].n)
                    for i in range(W1N):
                        if i >= 2:
                            s.wait_ge(ch["mx"].sem, mxb0 + i - 1)
                            s.wait_ge(ch["exp"].sem, expb0 + i - 1)
                        s.dma_start(out=pstages[i % 2][:, 0:1024],
                                    in_=w1_in[i * 128:(i + 1) * 128, :]).then_inc(
                                        ldP[i % 2].sem, 16)
                    for i in range(W2N):
                        g = W1N + i
                        s.wait_ge(ch["mx"].sem, mxb0 + W1N + max(0, i - 1))
                        s.wait_ge(ch["exp"].sem, expb0 + W1N + max(0, i - 1))
                        s.dma_start(out=pstages[g % 2][:, :],
                                    in_=w2_in[i * 128:(i + 1) * 128, :]).then_inc(
                                        ldP[g % 2].sem, 16)
                    for i in range(WPN):
                        g = W1N + W2N + i
                        s.wait_ge(ch["mx"].sem, mxb0 + W1N + W2N + max(0, i - 1))
                        s.dma_start(out=pstages[g % 2][:, 0:1024],
                                    in_=wp_in[i * 128:(i + 1) * 128, :]).then_inc(
                                        ldP[g % 2].sem, 16)
                nw = W1N + W2N + WPN
                ch["ldpA"].n += ((nw + 1) // 2) * 16
                ch["ldpB"].n += (nw // 2) * 16
                block.sync(_prep_w_sync)

                s1b0 = ch["s1"].n
                bwo0 = ch["bwout"].n
                sbase = ch["s1"].n
                p2b0 = ch["p2"].n

                def _prep_w_v(v):
                    n_b = W1N + W2N * 4 + WPN
                    for i in range(W1N):
                        sm, val = wload_sem(i)
                        v.wait_ge(sm.sem, val)
                        if i >= 1:
                            v.wait_ge(ch["mx"].sem, mxb0 + i)
                        vedge(v, v.reduce_sum(out=palph[:, 0:1],
                                               in_=pstages[i % 2][:, 0:1024],
                                               axis=AX.X, apply_absolute_value=True))
                        v.wait_ge(ch["exp"].sem, expb0 + i + 1)
                        if i >= 1:
                            v.wait_ge(ch["s1"].sem, s1b0 + i)
                        v.tensor_scalar(out=ptout[:, 0:1024], in0=psgn2[:, 0:1024],
                                        scalar1=palph[:, 0:1], scalar2=1.0 / 1024,
                                        op0=ALU.mult, op1=ALU.mult).then_inc(ch["mx"].sem, 1)
                        if i >= 1:
                            v.wait_ge(ch["bwout"].sem, bwo0 + (i - 1) * 16)
                            pb = PS2B if ((i - 1) % 2 == 0 or NOALT) else PS3B
                            v.tensor_copy(out=psgn[:, 0:1024],
                                          in_=pb[:, 0:1024]).then_inc(ch["p2"].sem, 1)
                    if W1N >= 1:
                        v.wait_ge(ch["s1"].sem, s1b0 + W1N)
                        v.wait_ge(ch["bwout"].sem, bwo0 + (W1N - 1) * 16)
                        pb = PS2B if ((W1N - 1) % 2 == 0 or NOALT) else PS3B
                        v.tensor_copy(out=psgn[:, 0:1024],
                                      in_=pb[:, 0:1024]).then_inc(ch["p2"].sem, 1)
                    for i in range(W2N):
                        sm, val = wload_sem(W1N + i)
                        v.wait_ge(sm.sem, val)
                        v.wait_ge(ch["mx"].sem, mxb0 + W1N + i)
                        vedge(v, v.reduce_sum(out=palph[:, 0:1],
                                               in_=pstages[i % 2][:, :],
                                               axis=AX.X, apply_absolute_value=True))
                        v.wait_ge(ch["exp"].sem, expb0 + W1N + i + 1)
                        v.tensor_scalar(out=ptout[:, :], in0=psgn2[:, :],
                                        scalar1=palph[:, 0:1], scalar2=1.0 / 4096,
                                        op0=ALU.mult, op1=ALU.mult).then_inc(ch["mx"].sem, 1)
                        v.wait_ge(ch["bwout"].sem, bwo0 + (W1N + i) * 16)
                        for b4 in range(4):
                            idx = W1N + i * 4 + b4
                            v.wait_ge(ch["s1"].sem, sbase + idx + 1)
                            pb = PS2B if idx % 2 == 0 else PS3B
                            v.tensor_copy(out=psgn[:, b4 * 1024:(b4 + 1) * 1024],
                                          in_=pb[:, 0:1024]).then_inc(ch["p2"].sem, 1)
                    for i in range(WPN):
                        sm, val = wload_sem(W1N + W2N + i)
                        v.wait_ge(sm.sem, val)
                        v.wait_ge(ch["s1"].sem, s1b0 + W1N + W2N * 4 + i)
                        v.tensor_copy(out=ptout[:, 0:1024],
                                      in_=pstages[i % 2][:, 0:1024]).then_inc(ch["mx"].sem, 1)
                        idx = W1N + W2N * 4 + i
                        v.wait_ge(ch["s1"].sem, sbase + idx + 1)
                        v.wait_ge(ch["bwout"].sem, bwo0 + (W1N + W2N + i) * 16)
                        pb = PS2B if idx % 2 == 0 else PS3B
                        v.tensor_copy(out=psgn[:, 0:1024],
                                      in_=pb[:, 0:1024]).then_inc(ch["p2"].sem, 1)
                ch["mx"].n += W1N + W2N + WPN
                ch["p2"].n += W1N + W2N * 4 + WPN
                block.vector(_prep_w_v)

                def _prep_w_act(sc):
                    for i in range(W1N):
                        sm, val = wload_sem(i)
                        sc.wait_ge(sm.sem, val)
                        if i >= 1:
                            sc.wait_ge(ch["mx"].sem, mxb0 + i)
                        sc.activation(out=psgn2[:, 0:1024], in_=pstages[i % 2][:, 0:1024],
                                      func=AF.Sign).then_inc(ch["exp"].sem, 1)
                    for i in range(W2N):
                        sm, val = wload_sem(W1N + i)
                        sc.wait_ge(sm.sem, val)
                        sc.wait_ge(ch["mx"].sem, mxb0 + W1N + i)
                        sc.activation(out=psgn2[:, :], in_=pstages[i % 2][:, :],
                                      func=AF.Sign).then_inc(ch["exp"].sem, 1)
                ch["exp"].n += W1N + W2N
                block.scalar(_prep_w_act)

                def _prep_w_pe(t):
                    cnt = 0
                    for i in range(W1N):
                        t.wait_ge(ch["mx"].sem, mxb0 + i + 1)
                        if i >= 2 or NOALT:
                            t.wait_ge(ch["p2"].sem, p2b0 + max(0, i - (1 if not NOALT else 0)))
                        pb = PS2B if (i % 2 == 0 or NOALT) else PS3B
                        for dc in range(8):
                            inst = t.transpose(out=pb[:, dc * 128:(dc + 1) * 128],
                                               in_=ptout[:, dc * 128:(dc + 1) * 128],
                                               identity=identb[:, :])
                        inst.then_inc(ch["s1"].sem, 1)
                        cnt += 1
                    for i in range(W2N):
                        t.wait_ge(ch["mx"].sem, mxb0 + W1N + i + 1)
                        for b4 in range(4):
                            idx = i * 4 + b4
                            if W1N + idx >= 2:
                                t.wait_ge(ch["p2"].sem, p2b0 + W1N + idx - 1)
                            pb = PS2B if (32 + idx) % 2 == 0 else PS3B
                            for k in range(8):
                                ht = b4 * 8 + k
                                inst = t.transpose(out=pb[:, k * 128:(k + 1) * 128],
                                                   in_=ptout[:, ht * 128:(ht + 1) * 128],
                                                   identity=identb[:, :])
                            inst.then_inc(ch["s1"].sem, 1)
                            cnt += 1
                    for i in range(WPN):
                        t.wait_ge(ch["mx"].sem, mxb0 + W1N + W2N + i + 1)
                        idx = W1N + W2N * 4 + i
                        t.wait_ge(ch["p2"].sem, p2b0 + idx - 1)
                        pb = PS2B if idx % 2 == 0 else PS3B
                        for dc in range(8):
                            inst = t.transpose(out=pb[:, dc * 128:(dc + 1) * 128],
                                               in_=ptout[:, dc * 128:(dc + 1) * 128],
                                               identity=identb[:, :])
                        inst.then_inc(ch["s1"].sem, 1)
                        cnt += 1
                n_batches = W1N + W2N * 4 + WPN
                ch["s1"].n += n_batches
                block.tensor(_prep_w_pe)

                def _prep_w_out(g):
                    pbase = ch["p2"].n - n_batches
                    for i in range(W1N):
                        g.wait_ge(ch["p2"].sem, pbase + i + 1)
                        g.dma_start(out=bw1t_d[i], in_=psgn[:, 0:1024]).then_inc(ch["bwout"].sem, 16)
                    for i in range(W2N):
                        g.wait_ge(ch["p2"].sem, pbase + W1N + (i + 1) * 4)
                        g.dma_start(out=bw2t_d[i], in_=psgn[:, :]).then_inc(ch["bwout"].sem, 16)
                    for i in range(WPN):
                        g.wait_ge(ch["p2"].sem, pbase + W1N + W2N * 4 + i + 1)
                        g.dma_start(out=wpt_d[i], in_=psgn[:, 0:1024]).then_inc(ch["bwout"].sem, 16)
                ch["bwout"].n += (W1N + W2N + WPN) * 16
                block.gpsimd(_prep_w_out)

            if DBG:
                def _dbg_bw(s):
                    s.wait_ge(ch["bwout"].sem, ch["bwout"].n)
                    s.dma_start(out=dbg["bw1t"], in_=bw1t_d[0]).then_inc(ch["oout"].sem, 16)
                ch["oout"].n += 16
                block.sync(_dbg_bw)

        qA = EC(nc.sbuf_tensor("qA", [128, 2048], BF16))
        qB = EC(nc.sbuf_tensor("qB", [128, 2048], BF16))
        qC = EC(nc.sbuf_tensor("qC", [128, 2048], BF16))
        qD = EC(nc.sbuf_tensor("qD", [128, 2048], BF16))
        qE = EC(nc.sbuf_tensor("qE", [128, 2048], BF16))
        qT = EC(nc.sbuf_tensor("qT", [128, 2112], BF16))
        qsub = EC(nc.sbuf_tensor("qsub", [128, 512], BF16))
        pt0 = EC(nc.sbuf_tensor("pt0", [128, 1024], BF16))
        pt1 = EC(nc.sbuf_tensor("pt1", [128, 1024], BF16))
        subm = EC(nc.sbuf_tensor("subm", [128, 32], F32))
        diag = EC(nc.sbuf_tensor("diag", [128, 32], F32))
        mxc = EC(nc.sbuf_tensor("mxc", [128, 32], F32))
        tmpc = EC(nc.sbuf_tensor("tmpc", [128, 8], F32))
        rcz = EC(nc.sbuf_tensor("rcz", [128, 16], F32))
        cneg = EC(nc.sbuf_tensor("cneg", [128, 1], F32))
        xst0 = EC(nc.sbuf_tensor("xst0", [128, 1024], F32))
        xst1 = EC(nc.sbuf_tensor("xst1", [128, 1024], F32))
        sqscr = EC(nc.sbuf_tensor("sqscr", [128, 1024], F32))
        lns = EC(nc.sbuf_tensor("lns", [128, 8], F32))
        xnstage = EC(nc.sbuf_tensor("xnstage", [128, 1024], BF16))
        hsb = EC(nc.sbuf_tensor("hsb", [128, 16384], BF16))
        w1s0 = EC(nc.sbuf_tensor("w1s0", [128, 1024], BF16))
        w1s1 = EC(nc.sbuf_tensor("w1s1", [128, 1024], BF16))
        w2s0 = EC(nc.sbuf_tensor("w2s0", [128, 2048], BF16))
        w2s1 = EC(nc.sbuf_tensor("w2s1", [128, 2048], BF16))
        rtm = EC(nc.sbuf_tensor("rtm", [128, 2048], BF16))
        gscr = EC(nc.sbuf_tensor("gscr", [128, 512], F32))
        sscr = EC(nc.sbuf_tensor("sscr", [128, 512], F32))
        sgscr = EC(nc.sbuf_tensor("sgscr", [128, 512], F32))
        xrj0 = EC(nc.sbuf_tensor("xrj0", [128, 512], F32))
        xrj1 = EC(nc.sbuf_tensor("xrj1", [128, 512], F32))
        wps0 = EC(nc.sbuf_tensor("wps0", [128, 1024], BF16))
        wps1 = EC(nc.sbuf_tensor("wps1", [128, 1024], BF16))
        sgst0 = EC(nc.sbuf_tensor("sgst0", [128, 512], F32))
        sgst1 = EC(nc.sbuf_tensor("sgst1", [128, 512], F32))
        mxst = sqscr.bitcast(BF16)
        restm = rtm[:, :]

        prep_done = {"bwout": ch["bwout"].n, "p2": ch["p2"].n, "mrow": ch["mrow"].n,
                     "misc": ch["misc"].n}

        def _barrier(e):
            e.wait_ge(ch["bwout"].sem, prep_done["bwout"])
            e.wait_ge(ch["p2"].sem, prep_done["p2"])
        for engname in ["tensor", "vector", "scalar", "gpsimd", "sync"]:
            getattr(block, engname)(_barrier)

        def _post_prep(v):
            if BI_PREP >= 5:
                v.memset(qA[64:96, :], 0.0)
                v.memset(qB[64:96, :], 0.0)
                vedge(v, v.memset(qD[64:96, :], 0.0))
                vedge(v, v.memset(qE[64:96, :], 0.0))
                v.memset(qA[64:65, :], 1.0)
                v.memset(qD[64:65, :], 1.0)
                v.memset(qT[:, :].rearrange("p (s c) -> p s c", c=66)[:, :, 64:65], 1.0)
                v.memset(qT[:, :].rearrange("p (s c) -> p s c", c=66)[:, :, 65:66], 0.0)
                v.memset(lns[:, 7:8], 1e-5)
                v.memset(cneg[:, :], -32.0)
            v.memset(tmpc[:, 0:8], 0.0).then_inc(ch["qp"].sem, 1)
        ch["qp"].n += 1
        block.vector(_post_prep)



        def emit_ln(s):
            xstg = [xst0, xst1]

            lnP = [ch["lnxA"], ch["lnxB"]]
            lnb = [lnP[0].n, lnP[1].n]

            def _sync(sy):
                for tt in range(16):
                    if tt >= 2:
                        sy.wait_ge(ch["lnst"].sem, ch["lnst"].n + tt - 1)
                    sy.dma_start(out=xstg[tt % 2][:, :],
                                 in_=x_in[s, tt * 128:(tt + 1) * 128, :]
                                 ).then_inc(lnP[tt % 2].sem, 16)
            ch["lnxA"].n += 8 * 16
            ch["lnxB"].n += 8 * 16
            block.sync(_sync)

            lnt0 = ch["lnt"].n
            lnf0 = ch["lnf"].n
            lnst0 = ch["lnst"].n
            s1b = ch["s1"].n
            h_prev = ch["h"].n

            def _act(sc):
                for tt in range(16):
                    sc.wait_ge(lnP[tt % 2].sem, lnb[tt % 2] + (tt // 2 + 1) * 16)
                    if tt >= 1:
                        sc.wait_ge(ch["lnst"].sem, lnst0 + tt)
                    sc.activation(out=sqscr[:, :], in_=xstg[tt % 2][:, :], func=AF.Square,
                                  accum_out=lns[:, 1:2]).then_inc(ch["lnt"].sem, 1)
                    sc.wait_ge(ch["lnf"].sem, lnf0 + 2 * tt + 1)
                    sc.activation(out=lns[:, 5:6], in_=lns[:, 4:5], func=AF.Sqrt,
                                  bias=lns[:, 7:8], scale=1.0).then_inc(ch["lnt"].sem, 1)
            ch["lnt"].n += 32
            block.scalar(_act)

            def _vec(v):
                for tt in range(16):
                    v.wait_ge(lnP[tt % 2].sem, lnb[tt % 2] + (tt // 2 + 1) * 16)
                    vedge(v, v.reduce_sum(out=lns[:, 0:1], in_=xstg[tt % 2][:, :],
                                           axis=AX.X))
                    vedge(v, v.tensor_scalar(out=lns[:, 2:3], in0=lns[:, 0:1],
                                             scalar1=-1.0 / 1024,
                                             scalar2=None, op0=ALU.mult))
                    vedge(v, v.tensor_mul(out=lns[:, 3:4], in0=lns[:, 2:3],
                                          in1=lns[:, 2:3]))
                    v.wait_ge(ch["lnt"].sem, lnt0 + 2 * tt + 1)
                    v.scalar_tensor_tensor(out=lns[:, 4:5], in0=lns[:, 1:2],
                                           scalar=1.0 / 1024, in1=lns[:, 3:4],
                                           op0=ALU.mult, op1=ALU.subtract
                                           ).then_inc(ch["lnf"].sem, 1)
                    v.wait_ge(ch["lnt"].sem, lnt0 + 2 * tt + 2)
                    vedge(v, v.reciprocal(out=lns[:, 6:7], in_=lns[:, 5:6]))
                    if tt >= 1:
                        v.wait_ge(ch["s1"].sem, s1b + tt)
                    v.tensor_scalar(out=xnstage[:, :], in0=xstg[tt % 2][:, :],
                                    scalar1=lns[:, 2:3], scalar2=lns[:, 6:7],
                                    op0=ALU.add, op1=ALU.mult).then_inc(ch["lnst"].sem, 1)
                    if tt == 0:
                        v.wait_ge(ch["h"].sem, h_prev)
                    v.wait_ge(ch["s1"].sem, s1b + tt + 1)
                    v.tensor_copy(
                        out=xnf[:, :].rearrange("p (dc t) -> p dc t", t=2048)
                        [:, :, tt * 128:(tt + 1) * 128],
                        in_=PS1B[:, 1024:2048].rearrange("p (dc t) -> p dc t", t=128)
                        ).then_inc(ch["lnf"].sem, 1)
            ch["lnf"].n += 32
            ch["lnst"].n += 16
            block.vector(_vec)

            hs_prev = ch["hs"].n
            gys_prev = ch["gys"].n

            def _pe(t):
                t.wait_ge(ch["hs"].sem, hs_prev)
                t.wait_ge(ch["gys"].sem, gys_prev)
                for tt in range(16):
                    t.wait_ge(ch["lnst"].sem, lnst0 + tt + 1)
                    if tt >= 1:
                        t.wait_ge(ch["lnf"].sem, lnf0 + 2 * tt)
                    for dc in range(8):
                        inst = t.transpose(out=PS1B[:, 1024 + dc * 128:1024 + (dc + 1) * 128],
                                           in_=xnstage[:, dc * 128:(dc + 1) * 128],
                                           identity=identb[:, :])
                    inst.then_inc(ch["s1"].sem, 1)
            ch["s1"].n += 16
            block.tensor(_pe)

        pair_ctx = {}

        def emit_qproj_fm(s, p):
            qp0 = ch["qp"].n
            sev_prev = ch["sev"].n
            pair_ctx[p] = qp0

            def _fm_pe(t):
                t.wait_ge(ch["lnf"].sem, ch["lnf"].n)
                if p == 0:
                    t.wait_ge(ch["sev"].sem, sev_prev)
                for half in range(2):
                    if half == 1:
                        t.wait_ge(ch["qp"].sem, qp0 + 5)
                    for c in range(2):
                        t.matmul(out=PS2[0:64, c * 512:(c + 1) * 512],
                                 lhsT=wqt[0:64, p * 64:(p + 1) * 64],
                                 rhs=xnf[0:64, p * 2048 + half * 1024 + c * 512:
                                         p * 2048 + half * 1024 + (c + 1) * 512],
                                 start=True, stop=True, tile_position=(0, 0))
                        t.matmul(out=PS2[64:128, c * 512:(c + 1) * 512],
                                 lhsT=wqt[64:128, p * 64:(p + 1) * 64],
                                 rhs=xnf[64:128, p * 2048 + half * 1024 + c * 512:
                                         p * 2048 + half * 1024 + (c + 1) * 512],
                                 start=True, stop=True, tile_position=(64, 64))
                        inst = t.matmul(out=PS3[0:64, c * 512:(c + 1) * 512],
                                        lhsT=wqt[64:128, p * 64:(p + 1) * 64],
                                        rhs=xnf[64:128, p * 2048 + half * 1024 + c * 512:
                                                p * 2048 + half * 1024 + (c + 1) * 512],
                                        start=True, stop=True, tile_position=(64, 0))
                    inst.then_inc(ch["p2"].sem, 1)
                    pef(t, 0.003)
            p2q = ch["p2"].n
            ch["p2"].n += 2
            block.tensor(_fm_pe)

            def _fm_ev(sc):
                for half in range(2):
                    sc.wait_ge(ch["p2"].sem, p2q + half + 1)
                    hs = slice(half * 1024, (half + 1) * 1024)
                    for dst in (qA, qB):
                        sc.activation(out=dst[0:64, hs], in_=PS2[0:64, 0:1024],
                                      func=AF.Identity, bias=0.0, scale=1.0)
                    sc.activation(out=qC[64:128, hs], in_=PS2[64:128, 0:1024],
                                  func=AF.Identity, bias=0.0, scale=1.0)
                    sc.activation(out=qD[0:64, hs], in_=PS3[0:64, 0:1024],
                                  func=AF.Identity, bias=0.0, scale=1.0)
                    sc.activation(out=qE[0:64, hs], in_=PS3[0:64, 0:1024],
                                  func=AF.Identity, bias=0.0,
                                  scale=1.0).then_inc(ch["qp"].sem, 5)
            ch["qp"].n += 10
            block.scalar(_fm_ev)

            def _fm_gather(v):
                v.wait_ge(ch["qp"].sem, qp0 + 10)
                v.tensor_copy(out=qsub[0:64, :],
                              in_=qA[0:64, :].rearrange("p (a b) -> p a b", b=4)
                              [:, :, 0])
                v.tensor_copy(out=qsub[64:128, :],
                              in_=qC[64:128, :].rearrange("p (a b) -> p a b", b=4)
                              [:, :, 0]).then_inc(ch["qp"].sem, 1)
                vef(v, 0.05)
            ch["qp"].n += 1
            block.vector(_fm_gather)

        def emit_attn_main(s, p):
            STG = BI_STAGE
            qp0 = pair_ctx[p]
            if fq["act"]:
                def _af(sc):
                    aef(sc, 1.0)
                block.scalar(_af)

            resf_prev = ch["resf"].n
            ost_prev = ch["ost"].n

            def _qproj_tm(t):
                for bb in range(2):
                    if bb == 0:
                        t.wait_ge(ch["resf"].sem, resf_prev)
                        if p == 0:
                            t.wait_ge(ch["ost"].sem, ost_prev)
                    t.wait_ge(ch["qp"].sem, qp0 + 10 + (0 if bb == 0 else 2))
                    for k in range(8):
                        nt = bb * 8 + k
                        t.transpose(out=PS0B[:, k * 64:(k + 1) * 64],
                                    in_=qA[0:64, nt * 128:(nt + 1) * 128],
                                    identity=identb[0:64, 0:64])
                        inst = t.transpose(out=PS1B[:, k * 64:(k + 1) * 64],
                                           in_=qC[64:128, nt * 128:(nt + 1) * 128],
                                           identity=identb[64:128, 64:128])
                    inst.then_inc(ch["p2"].sem, 1)
                    pef(t, 0.003)
            p2q = ch["p2"].n
            ch["p2"].n += 2
            block.tensor(_qproj_tm)

            def _qproj_ev(v):
                for bb in range(2 if QPN >= 3 else 0):
                    v.wait_ge(ch["p2"].sem, p2q + 1 + bb)
                    v.tensor_copy(
                        out=qT[:, :].rearrange("p (ss c) -> p ss c", c=66)
                        [:, bb * 8:(bb + 1) * 8, 0:64],
                        in_=PS0B[:, 0:512].rearrange("p (k e) -> p k e", e=64))
                    v.tensor_copy(
                        out=qT[:, :].rearrange("p (ss c) -> p ss c", c=66)
                        [:, 16 + bb * 8:16 + (bb + 1) * 8, 0:64],
                        in_=PS1B[:, 0:512].rearrange("p (k e) -> p k e", e=64)
                        ).then_inc(ch["qp"].sem, 1)
                if QPN >= 4:
                    v.wait_ge(ch["qp"].sem, qp0 + 13)
                    vedge(v, v.tensor_tensor(
                        out=sqscr[:, :].rearrange("p (ss e) -> p ss e", e=64),
                        in0=qT[:, :].rearrange("p (ss c) -> p ss c", c=66)[:, 0:16, 0:64],
                        in1=qT[:, :].rearrange("p (ss c) -> p ss c", c=66)[:, 0:16, 0:64],
                        op=ALU.mult))
                    vedge(v, v.reduce_sum(
                        out=diag[:, 0:16],
                        in_=sqscr[:, :].rearrange("p (ss e) -> p ss e", e=64), axis=AX.X))
                    vedge(v, v.tensor_tensor(
                        out=sqscr[:, :].rearrange("p (ss e) -> p ss e", e=64),
                        in0=qT[:, :].rearrange("p (ss c) -> p ss c", c=66)[:, 16:32, 0:64],
                        in1=qT[:, :].rearrange("p (ss c) -> p ss c", c=66)[:, 16:32, 0:64],
                        op=ALU.mult))
                    v.reduce_sum(out=diag[:, 16:32],
                                 in_=sqscr[:, :].rearrange("p (ss e) -> p ss e", e=64),
                                 axis=AX.X).then_inc(ch["qp"].sem, 1)
                    vef(v, 0.05)
            ch["qp"].n += 3
            block.vector(_qproj_ev)

            if STG < 3:
                return
            s10 = ch["s1"].n
            mx0 = ch["mx"].n

            qp_all = ch["qp"].n

            P1R = [PS0, PS2, PS3]

            def _p1_t(t, nt):
                if nt == 0:
                    t.wait_ge(ch["qp"].sem, qp_all - 3)
                if nt == 2:
                    t.wait_ge(ch["qp"].sem, qp_all - 1)
                PSe = P1R[(nt + 1) % 3]
                if nt >= 3:
                    t.wait_ge(ch["mx"].sem, mx0 + (nt - 2) * 2)
                t.matmul(out=PSe[0:64, 0:512],
                         lhsT=qA[0:64, nt * 128:nt * 128 + 64],
                         rhs=qsub[0:64, :], start=True, stop=True,
                         tile_position=(0, 0))
                t.matmul(out=PSe[64:128, 0:512],
                         lhsT=qA[0:64, nt * 128 + 64:nt * 128 + 128],
                         rhs=qsub[0:64, :], start=True, stop=True,
                         tile_position=(0, 64))
                t.matmul(out=PSe[0:64, 512:1024],
                         lhsT=qC[64:128, nt * 128:nt * 128 + 64],
                         rhs=qsub[64:128, :], start=True, stop=True,
                         tile_position=(64, 0))
                mm = t.matmul(out=PSe[64:128, 512:1024],
                              lhsT=qC[64:128, nt * 128 + 64:nt * 128 + 128],
                              rhs=qsub[64:128, :], start=True, stop=True,
                              tile_position=(64, 64))
                mm.then_inc(ch["s1"].sem, 1)
                pef(t, 0.003)

            def _p1_vv(v, nt):
                PSe = P1R[(nt + 1) % 3]
                v.wait_ge(ch["s1"].sem, s10 + nt + 1)
                v.reduce_max(out=subm[:, nt:nt + 1], in_=PSe[:, 0:512], axis=AX.X)
                inst = v.reduce_max(out=subm[:, 16 + nt:17 + nt],
                                    in_=PSe[:, 512:1024], axis=AX.X)
                inst.then_inc(ch["mx"].sem, 2)
                if nt == 15:
                    v.wait_ge(ch["mx"].sem, mx0 + 32)
                    v.wait_ge(ch["qp"].sem, qp_all)
                    v.tensor_max(out=mxc[:, :], in0=subm[:, :],
                                 in1=diag[:, :]).then_inc(ch["mx"].sem, 1)

            for nt in range(16):
                block.tensor(lambda t, nt=nt: _p1_t(t, nt))
                block.vector(lambda v, nt=nt: _p1_vv(v, nt))
            ch["s1"].n += 16
            ch["mx"].n += 33

            if STG < 4:
                return
            mrow0 = ch["mrow"].n

            def _mx_pe(t):
                t.wait_ge(ch["mx"].sem, ch["mx"].n)
                for h in range(2):
                    for half in range(2):
                        PSx = PS2 if h == 0 else PS3
                        r = 64 * half
                        for c in range(8):
                            nt = half * 8 + c
                            inst = t.matmul(out=PSx[r:r + 1, c * 128:(c + 1) * 128],
                                            lhsT=mxc[:, h * 16 + nt:h * 16 + nt + 1],
                                            rhs=identf[:, :], start=(c in (0, 4)),
                                            stop=(c in (3, 7)),
                                            tile_position=(0, r))
                        inst.then_inc(ch["p2"].sem, 1)
                        pef(t, 0.005)
            ch["p2"].n += 4
            block.tensor(_mx_pe)

            p2m = ch["p2"].n - 4

            mxrow_prev = ch["mxrow"].n

            def _mx_v(v):
                v.wait_ge(ch["p2"].sem, p2m + 4)
                for i in range(4):
                    h, half = divmod(i, 2)
                    PSx = PS2 if h == 0 else PS3
                    r = 64 * half
                    if i == 0:
                        v.wait_ge(ch["mxrow"].sem, mxrow_prev)
                    if i == 2:
                        v.wait_ge(ch["mxrow"].sem, mxrow_prev + 16)
                    v.tensor_scalar(out=mxst[0:1, half * 1024:(half + 1) * 1024],
                                    in0=PSx[r:r + 1, 0:1024], scalar1=-1.0, scalar2=None,
                                    op0=ALU.mult).then_inc(ch["mrow"].sem, 1)
                vef(v, 0.03)
            ch["mrow"].n += 4
            block.vector(_mx_v)

            def _mx_g(g):
                g.wait_ge(ch["mrow"].sem, mrow0 + 2)
                g.dma_start(out=qB[64:65, :], in_=mxst[0:1, :]).then_inc(ch["mxrow"].sem, 16)
                g.wait_ge(ch["mrow"].sem, mrow0 + 4)
                g.dma_start(out=qE[64:65, :], in_=mxst[0:1, :]).then_inc(ch["mxrow"].sem, 16)
            ch["mxrow"].n += 32
            block.sync(_mx_g)

            if STG < 5:
                return
            slot_map = []
            for nt in range(16):
                if nt < 7:
                    slot_map.append((PS0, nt * 66))
                elif nt < 14:
                    slot_map.append((PS0, 512 + (nt - 7) * 66))
                else:
                    slot_map.append((PS1, (nt - 14) * 66))

            for h in range(2):
                qlhs = qA if h == 0 else qD
                qrhs = qB if h == 0 else qE
                p20 = ch["p2"].n
                exp0 = ch["exp"].n
                av0 = ch["av"].n

                def _p2_grp(t, grp, h=h, qlhs=qlhs, qrhs=qrhs, p20=p20,
                            exp0=exp0, av0=av0):
                    if grp == 0:
                        t.wait_ge(ch["mxrow"].sem, ch["mxrow"].n - 16 * (1 - h))
                        if h == 0:
                            t.wait_ge(ch["mx"].sem, ch["mx"].n)
                    for it in range(grp * 4, grp * 4 + 4):
                        half, J = divmod(it, 16)
                        PSa = PS2 if it % 2 == 0 else PS3
                        if it >= 2:
                            t.wait_ge(ch["exp"].sem, exp0 + it - 1)
                        t.matmul(out=PSa[:, 0:512],
                                 lhsT=qlhs[0:96, J * 128:(J + 1) * 128],
                                 rhs=qrhs[0:96, half * 1024:half * 1024 + 512],
                                 start=True, stop=True)
                        t.matmul(out=PSa[:, 512:1024],
                                 lhsT=qlhs[0:96, J * 128:(J + 1) * 128],
                                 rhs=qrhs[0:96, half * 1024 + 512:half * 1024 + 1024],
                                 start=True, stop=True).then_inc(ch["p2"].sem, 1)
                        pef(t, 0.0145)
                        if it >= 1:
                            Jp = (it - 1) % 16
                            halfp = (it - 1) // 16
                            t.wait_ge(ch["exp"].sem, exp0 + it)
                            if h == 1 and it == 1 and BI_STAGE >= 6:
                                t.wait_ge(ch["divd"].sem, ch["divd"].n)
                            ptb = pt0 if (it - 1) % 2 == 0 else pt1
                            for k in range(8):
                                nt = halfp * 8 + k
                                pst, col = slot_map[nt]
                                inst = t.matmul(
                                    out=pst[:, col:col + 65],
                                    lhsT=ptb[:, k * 128:(k + 1) * 128],
                                    rhs=qT[:, (h * 16 + Jp) * 66:(h * 16 + Jp) * 66 + 65],
                                    start=(Jp == 0 and nt in (0, 7, 8, 14)),
                                    stop=(Jp == 15 and nt in (6, 7, 13, 15)))
                            if Jp == 15:
                                inst.then_inc(ch["av"].sem, 1)
                    if grp == 7:
                        t.wait_ge(ch["exp"].sem, exp0 + 32)
                        ptb = pt1
                        for k in range(8):
                            nt = 8 + k
                            pst, col = slot_map[nt]
                            inst = t.matmul(
                                out=pst[:, col:col + 65],
                                lhsT=ptb[:, k * 128:(k + 1) * 128],
                                rhs=qT[:, (h * 16 + 15) * 66:(h * 16 + 15) * 66 + 65],
                                start=False, stop=(nt in (13, 15)))
                        inst.then_inc(ch["av"].sem, 1)
                        pef(t, 0.0145)

                for grp in range(8):
                    block.tensor(lambda t, grp=grp: _p2_grp(t, grp))
                    if fq["vec"]:
                        block.vector(lambda v: vef(v, 0.03))
                ch["p2"].n += 32
                ch["av"].n += 2

                def _p2_act(sc, p20=p20, exp0=exp0, av0=av0):
                    for half in range(2):
                        for J in range(16):
                            it = half * 16 + J
                            sc.wait_ge(ch["p2"].sem, p20 + it + 1)
                            if it >= 2:
                                sc.wait_ge(ch["av"].sem, av0)
                            PSa = PS2 if it % 2 == 0 else PS3
                            ptb = pt0 if it % 2 == 0 else pt1
                            sc.activation(out=ptb[:, :], in_=PSa[:, :], func=AF.Exp,
                                          bias=cneg[:, :], scale=8.0).then_inc(
                                              ch["exp"].sem, 1)
                ch["exp"].n += 32
                block.scalar(_p2_act)

                div0 = ch["divd"].n

                if STG < 6:
                    continue_div = False
                else:
                    continue_div = True

                def _div_v(v, h=h, av0=av0):
                    for half in range(2):
                        v.wait_ge(ch["av"].sem, av0 + 2)
                        for k in range(8):
                            nt = half * 8 + k
                            pst, col = slot_map[nt]
                            inst = v.reciprocal(out=rcz[:, nt:nt + 1],
                                                in_=pst[:, col + 64:col + 65])
                        vedge(v, inst)
                        for k in range(8):
                            nt = half * 8 + k
                            pst, col = slot_map[nt]
                            inst = v.tensor_scalar(
                                out=restm[:, nt * 128 + h * 64:nt * 128 + h * 64 + 64],
                                in0=pst[:, col:col + 64],
                                scalar1=rcz[:, nt:nt + 1], scalar2=None,
                                op0=ALU.mult)
                        inst.then_inc(ch["divd"].sem, 1)
                        vef(v, 0.1)
                if continue_div:
                    ch["divd"].n += 2
                    block.vector(_div_v)

        def emit_attn_res(s, p):
            rst0 = ch["rest"].n
            divn = ch["divd"].n

            def _res_pe(t):
                t.wait_ge(ch["divd"].sem, divn)
                for bb in range(4):
                    if bb >= 2:
                        t.wait_ge(ch["resf"].sem, ch["resf"].n + bb - 1)
                    pb = PS0B[:, (bb % 2) * 1024:(bb % 2) * 1024 + 512]
                    for k in range(4):
                        nt = bb * 4 + k
                        inst = t.transpose(out=pb[:, k * 128:(k + 1) * 128],
                                           in_=restm[:, nt * 128:(nt + 1) * 128],
                                           identity=identb[:, :])
                    inst.then_inc(ch["rest"].sem, 1)
                    pef(t, 0.003)
            ch["rest"].n += 4
            block.tensor(_res_pe)

            def _res_v(v):
                for bb in range(4):
                    v.wait_ge(ch["rest"].sem, rst0 + bb + 1)
                    pb = PS0B[:, (bb % 2) * 1024:(bb % 2) * 1024 + 512]
                    v.tensor_copy(out=resf[:, p * 2048 + bb * 512:p * 2048 + (bb + 1) * 512],
                                  in_=pb[:, 0:512]).then_inc(ch["resf"].sem, 1)
                vef(v, 0.08)
            ch["resf"].n += 4
            block.vector(_res_v)

        seq_silu = {}
        cnt = {"wps": 0}

        from collections import deque
        fq = {"pe": deque(), "vec": deque(), "act": deque()}
        pace_d = {"pe": 0.0, "vec": 0.0, "act": 0.0}
        pace_t = {"pe": 0.0, "vec": 0.0, "act": 0.0}
        pair_budget = {"pe": 0.0, "vec": 0.0, "act": 0.0}

        emt = {"h": 0, "gy": 0, "hs": 0, "gys": 0, "gact": 0}

        def _fill(qn, eng, frac):
            pace_t[qn] += frac * pair_budget[qn]
            q = fq[qn]
            eager = qn == "vec"
            while q and (eager or pace_d[qn] + q[0][0] * 0.5 <= pace_t[qn]):
                c, fn, reqs, prov = q[0]
                if any(emt[k] < v for k, v in reqs):
                    break
                q.popleft()
                fn(eng)
                emt[prov] += 1
                pace_d[qn] += c

        def pef(t, frac):
            _fill("pe", t, frac)

        def vef(v, frac):
            _fill("vec", v, frac)

        def aef(sc, frac):
            _fill("act", sc, frac)

        def pace_reset():
            for qn in fq:
                pair_budget[qn] = sum(e[0] for e in fq[qn]) / 8.5
                pace_d[qn] = 0.0
                pace_t[qn] = 0.0

        def drain_all():
            qs = {"pe": block.tensor, "vec": block.vector, "act": block.scalar}
            while any(fq[qn] for qn in qs):
                progress = False
                for qn, emit in qs.items():
                    runnable = []
                    while fq[qn]:
                        c, fn, reqs, prov = fq[qn][0]
                        if any(emt[k] < v for k, v in reqs):
                            break
                        fq[qn].popleft()
                        runnable.append(fn)
                        emt[prov] += 1
                    if runnable:
                        progress = True

                        def _sec(e, runnable=runnable):
                            for fn in runnable:
                                fn(e)
                        emit(_sec)
                assert progress, "ffn queue drain deadlock"

        def plan_ffn_chunk(s, ci, defer_dma=None):
            tc, half = divmod(ci, 2)
            fb = PS1[:, 512:1024]
            w1P = [ch["lw1A"], ch["lw1B"]]
            w2P = [ch["lw2A"], ch["lw2B"]]
            M0 = ch["h"].n
            G0 = ch["gy"].n
            gy_now = ch["gy"].n
            gys_now = ch["gys"].n
            sg_now = ch["sg"].n
            gact0 = ch["gact"].n
            lnf_now = ch["lnf"].n

            w1_wait = []
            for k in range(16):
                par = (M0 + k) % 2
                w1P[par].n += 16
                w1_wait.append(w1P[par].n)
            w2_wait = []
            if half == 1:
                for dt in range(8):
                    w2P[0].n += 16
                    a = w2P[0].n
                    w2P[1].n += 16
                    w2_wait.append((a, w2P[1].n))

            yc_now = ch["yc"].n

            def _dma(g, M0=M0, G0=G0, tc=tc, half=half, ci=ci, yc_now=yc_now):
                for k in range(16):
                    M = M0 + k
                    if M >= 2:
                        g.wait_ge(ch["h"].sem, M - 1)
                    g.dma_start(out=(w1s0 if M % 2 == 0 else w1s1)[:, :],
                                in_=bw1t_d[half * 16 + k]).then_inc(w1P[M % 2].sem, 16)
                if half == 1:
                    for dt in range(8):
                        if G0 + dt >= 1:
                            g.wait_ge(ch["gyh"].sem, G0 + dt)
                        g.dma_start(out=w2s0[:, :],
                                    in_=bw2t_d[dt][:, 0:2048]).then_inc(w2P[0].sem, 16)
                        if G0 + dt >= 1:
                            g.wait_ge(ch["gy"].sem, G0 + dt)
                        g.dma_start(out=w2s1[:, :],
                                    in_=bw2t_d[dt][:, 2048:4096]).then_inc(w2P[1].sem, 16)
            if defer_dma is None:
                block.gpsimd(_dma)
            else:
                defer_dma.append(_dma)

            for k in range(16):
                def _h(t, k=k, M0=M0, tc=tc, half=half, ci=ci,
                       w1_wait=w1_wait, gys_now=gys_now, lnf_now=lnf_now):
                    M = M0 + k
                    par = M % 2
                    t.wait_ge(w1P[par].sem, w1_wait[k])
                    if ci == 0 and k == 0:
                        t.wait_ge(ch["lnf"].sem, lnf_now)
                    if half == 0 and k == 0:
                        t.wait_ge(ch["gys"].sem, gys_now)
                    if M >= 1:
                        t.wait_ge(ch["hs"].sem, M)
                    w1b = w1s0 if par == 0 else w1s1
                    for dc in range(8):
                        inst = t.matmul(
                            out=fb, lhsT=w1b[:, dc * 128:(dc + 1) * 128],
                            rhs=xnf[:, dc * 2048 + tc * 512:dc * 2048 + (tc + 1) * 512],
                            start=(dc == 0), stop=(dc == 7))
                    inst.then_inc(ch["h"].sem, 1)
                fq["pe"].append((1700, _h, [("hs", M0 + k)], "h"))
            if half == 1:
                for dt in range(8):
                    def _g(t, dt=dt, M0=M0, G0=G0, tc=tc,
                           w2_wait=w2_wait, gact0=gact0):
                        G = G0 + dt
                        t.wait_ge(w2P[0].sem, w2_wait[dt][0])
                        if dt == 0:
                            t.wait_ge(ch["gact"].sem, gact0 + 1)
                            t.wait_ge(ch["hs"].sem, M0 + 16)
                        else:
                            t.wait_ge(ch["gys"].sem, G)
                        for kk in range(32):
                            if kk == 16:
                                t.wait_ge(w2P[1].sem, w2_wait[dt][1])
                            w2b = w2s0 if kk < 16 else w2s1
                            inst = t.matmul(out=fb,
                                            lhsT=w2b[:, (kk % 16) * 128:(kk % 16 + 1) * 128],
                                            rhs=hsb[:, kk * 512:(kk + 1) * 512],
                                            start=(kk == 0), stop=(kk == 31))
                            if kk == 15:
                                inst.then_inc(ch["gyh"].sem, 1)
                        inst.then_inc(ch["gy"].sem, 1)
                    fq["pe"].append(
                        (6800, _g,
                         [("gact", gact0 + 1), ("hs", M0 + 16)] if dt == 0
                         else [("gys", G0 + dt)], "gy"))
            ch["h"].n += 16
            if half == 1:
                ch["gy"].n += 8
                ch["gyh"].n += 8

            for k in range(16):
                def _e(v, k=k, M0=M0, half=half, gy_now=gy_now):
                    M = M0 + k
                    v.wait_ge(ch["h"].sem, M + 1)
                    if half == 0 and k == 0:
                        v.wait_ge(ch["gy"].sem, gy_now)
                    ht = half * 16 + k
                    v.tensor_scalar(out=hsb[:, ht * 512:(ht + 1) * 512], in0=fb,
                                    scalar1=b1c[:, ht:ht + 1], scalar2=None,
                                    op0=ALU.add).then_inc(ch["hs"].sem, 1)
                fq["vec"].append((250, _e, [("h", M0 + k + 1)], "hs"))
            if half == 1:
                for dt in range(8):
                    def _ge(v, dt=dt, G0=G0, tc=tc, ci=ci, s=s, sg_now=sg_now):
                        G = G0 + dt
                        v.wait_ge(ch["gy"].sem, G + 1)
                        if s > 0 and ci == 1 and dt == 0:
                            v.wait_ge(ch["sg"].sem, sg_now)
                        v.tensor_scalar(out=gfm[:, tc * 4096 + dt * 512:
                                                 tc * 4096 + (dt + 1) * 512],
                                        in0=fb, scalar1=b2c[:, dt:dt + 1], scalar2=None,
                                        op0=ALU.add).then_inc(ch["gys"].sem, 1)
                    fq["vec"].append((250, _ge, [("gy", G0 + dt + 1)], "gys"))
            ch["hs"].n += 16
            if half == 1:
                ch["gys"].n += 8

            def _gelu(sc, M0=M0, half=half):
                sc.wait_ge(ch["hs"].sem, M0 + 16)
                sc.activation(out=hsb[:, half * 8192:(half + 1) * 8192],
                              in_=hsb[:, half * 8192:(half + 1) * 8192],
                              func=AF.Gelu, bias=0.0, scale=1.0).then_inc(
                    ch["gact"].sem, 1)
            fq["act"].append((7100, _gelu, [("hs", M0 + 16)], "gact"))
            ch["gact"].n += 1
            if half == 1:
                def _silu(sc, G0=G0, tc=tc):
                    sc.wait_ge(ch["gys"].sem, G0 + 8)
                    sc.activation(out=gfm[:, tc * 4096:(tc + 1) * 4096],
                                  in_=gfm[:, tc * 4096:(tc + 1) * 4096],
                                  func=AF.Silu, bias=0.0, scale=1.0).then_inc(
                        ch["gact"].sem, 1)
                fq["act"].append((3700, _silu, [("gys", G0 + 8)], "gact"))
                ch["gact"].n += 1
                seq_silu[(s, tc)] = ch["gact"].n

        def emit_combine_a(s):
            lwpP = [ch["lwpA"], ch["lwpB"]]
            ouP = [ch["ooutA"], ch["ooutB"]]
            resf_end = ch["resf"].n
            divd_end = ch["divd"].n
            exp_end = ch["exp"].n
            c = {"s": s, "Y0": ch["yc"].n, "W0": cnt["wps"], "sg0": ch["sg"].n,
                 "sev0": ch["sev"].n, "ost0": ch["ost"].n,
                 "silu_all": seq_silu[(s, 3)], "ou0": [ouP[0].n, ouP[1].n]}
            Y0, W0, sg0, ost0 = c["Y0"], c["W0"], c["sg0"], c["ost0"]
            h_now = ch["h"].n

            wps_wait = []
            for j in range(32):
                par = (W0 + j) % 2
                lwpP[par].n += 16
                wps_wait.append(lwpP[par].n)
            cnt["wps"] += 32
            lxrP = [ch["lxrA"], ch["lxrB"]]
            xres_ready = []
            for j in range(32):
                lxrP[j % 2].n += 16
                xres_ready.append(lxrP[j % 2].n)
            c["xres_ready"] = xres_ready
            c["ou_abs"] = [c["ou0"][j % 2] + 16 * (j // 2 + 1) for j in range(32)]
            YPS = [PS2[:, 0:512], PS2[:, 512:1024], PS3[:, 0:512], PS3[:, 512:1024]]
            c["YPS"] = YPS

            def _dma(g):
                for j in range(32):
                    dt = j % 8
                    Wg = W0 + j
                    if Wg >= 2:
                        g.wait_ge(ch["yc"].sem, Wg - 1)
                    g.dma_start(out=(wps0 if Wg % 2 == 0 else wps1)[:, :],
                                in_=wpt_d[dt]).then_inc(lwpP[Wg % 2].sem, 16)
            block.gpsimd(_dma)

            def _xld(sy, j):
                tc, dt = divmod(j, 8)
                sy.dma_start(
                    out=(xrj0 if j % 2 == 0 else xrj1)[:, :]
                    .rearrange("p (k d) -> p k d", d=128),
                    in_=x_in[s, tc * 512:(tc + 1) * 512, dt * 128:(dt + 1) * 128]
                    .rearrange("(k p) d -> p k d", p=128)
                    ).then_inc(lxrP[j % 2].sem, 16)
            c["xld"] = _xld

            def _sgT(t, j):
                Yg = Y0 + j
                t.wait_ge(ch["sg"].sem, sg0 + j + 1)
                if Yg >= 2:
                    t.wait_ge(ch["ost"].sem, Yg - 1)
                for k in range(4):
                    inst = t.transpose(
                        out=PS0[:, (j % 2) * 512 + k * 128:(j % 2) * 512 + (k + 1) * 128],
                        in_=sgscr[:, k * 128:(k + 1) * 128], identity=identf[:, :])
                inst.then_inc(ch["sgt"].sem, 1)

            def _pe(t):
                t.wait_ge(ch["divd"].sem, divd_end)
                t.wait_ge(ch["exp"].sem, exp_end)
                t.wait_ge(ch["resf"].sem, resf_end)
                for j in range(32):
                    tc, dt = divmod(j, 8)
                    Yg = Y0 + j
                    par = (W0 + j) % 2
                    t.wait_ge(lwpP[par].sem, wps_wait[j])
                    if Yg >= 4:
                        t.wait_ge(ch["sev"].sem, Yg - 3)
                    buf = wps0 if par == 0 else wps1
                    for dc in range(8):
                        inst = t.matmul(
                            out=YPS[j % 4],
                            lhsT=buf[:, dc * 128:(dc + 1) * 128],
                            rhs=resf[:, dc * 2048 + tc * 512:dc * 2048 + (tc + 1) * 512],
                            start=(dc == 0), stop=(dc == 7))
                    inst.then_inc(ch["yc"].sem, 1)
                    if j >= 1:
                        _sgT(t, j - 1)
                _sgT(t, 31)
            ch["yc"].n += 32
            ch["sgt"].n += 32
            block.tensor(_pe)

            def _act(sc):
                for j in range(32):
                    dt = j % 8
                    Yg = Y0 + j
                    sc.wait_ge(ch["yc"].sem, Yg + 1)
                    if Yg >= 2:
                        sc.wait_ge(ch["sg"].sem, Yg - 1)
                    sc.activation(out=(sscr if j % 2 == 0 else gscr)[:, :],
                                  in_=YPS[j % 4],
                                  func=AF.Identity, bias=bpc[:, dt:dt + 1],
                                  scale=1.0).then_inc(ch["sev"].sem, 1)
            ch["sev"].n += 32
            block.scalar(_act)
            return c

        def emit_combine_b(c):
            s = c["s"]
            ouP = [ch["ooutA"], ch["ooutB"]]
            lxrP = [ch["lxrA"], ch["lxrB"]]
            Y0, sg0, sev0, ost0 = c["Y0"], c["sg0"], c["sev0"], c["ost0"]

            def _vec(v):
                for j in range(32):
                    tc, dt = divmod(j, 8)
                    Yg = Y0 + j
                    v.wait_ge(ch["sev"].sem, sev0 + j + 1)
                    if j == 0:
                        v.wait_ge(ch["gact"].sem, c["silu_all"])
                    if Yg >= 1:
                        v.wait_ge(ch["sgt"].sem, Yg)
                    v.tensor_tensor(out=sgscr[:, :],
                                    in0=gfm[:, tc * 4096 + dt * 512:
                                            tc * 4096 + (dt + 1) * 512],
                                    in1=(sscr if j % 2 == 0 else gscr)[:, :],
                                    op=ALU.mult).then_inc(ch["sg"].sem, 1)
                    v.wait_ge(ch["sgt"].sem, Yg + 1)
                    v.wait_ge(lxrP[j % 2].sem, c["xres_ready"][j])
                    if j >= 2:
                        v.wait_ge(ouP[j % 2].sem, c["ou_abs"][j - 2])
                    else:
                        v.wait_ge(ouP[j % 2].sem, c["ou0"][j % 2])
                    stg = sgst0 if j % 2 == 0 else sgst1
                    v.tensor_tensor(
                        out=stg[:, :].rearrange("p (k d) -> p k d", d=128),
                        in0=PS0[:, (j % 2) * 512:(j % 2) * 512 + 512]
                        .rearrange("p (k d) -> p k d", d=128),
                        in1=(xrj0 if j % 2 == 0 else xrj1)[:, :]
                        .rearrange("p (k d) -> p k d", d=128),
                        op=ALU.add).then_inc(ch["ost"].sem, 1)
            ch["sg"].n += 32
            ch["ost"].n += 32
            block.vector(_vec)

            def _out(sy):
                sy.wait_ge(ch["ost"].sem, ost0)
                c["xld"](sy, 0)
                c["xld"](sy, 1)
                for j in range(32):
                    tc, dt = divmod(j, 8)
                    sy.wait_ge(ch["ost"].sem, ost0 + j + 1)
                    if j + 2 < 32:
                        c["xld"](sy, j + 2)
                    stg = sgst0 if j % 2 == 0 else sgst1
                    sy.dma_start(
                        out=out_d[s, tc * 512:(tc + 1) * 512,
                                  dt * 128:(dt + 1) * 128]
                        .rearrange("(k p) d -> p k d", p=128),
                        in_=stg[:, :].rearrange("p (k d) -> p k d", d=128)
                        ).then_inc(ouP[j % 2].sem, 16)
            ch["ooutA"].n += 16 * 16
            ch["ooutB"].n += 16 * 16
            block.sync(_out)

        ln_done = set()
        planned = set()
        for s in range(min(SEQ, BI_SEQ)):
            if BI_STAGE >= 1 and s not in ln_done:
                emit_ln(s)
            full = BI_STAGE >= 8 and BI_PAIR >= 8
            if full and s not in planned:
                for ci in range(8):
                    plan_ffn_chunk(s, ci)
                pace_reset()
                planned.add(s)
            if BI_STAGE >= 2:
                emit_qproj_fm(s, 0)
                for p in range(min(8, BI_PAIR)):
                    emit_attn_main(s, p)
                    if p + 1 < min(8, BI_PAIR):
                        emit_qproj_fm(s, p + 1)
                    emit_attn_res(s, p)
            if DBG and s == 0:
                def _dbg0(sy):
                    sy.wait_ge(ch["resf"].sem, ch["resf"].n)
                    sy.dma_start(out=dbg["xnf"], in_=xnf[:, :]).then_inc(ch["oout"].sem, 16)
                    sy.dma_start(out=dbg["resf"], in_=resf[:, :]).then_inc(ch["oout"].sem, 16)
                    sy.dma_start(out=dbg["qA"], in_=qA[:, :]).then_inc(ch["oout"].sem, 16)
                    sy.dma_start(out=dbg["qT"], in_=qT[:, :]).then_inc(ch["oout"].sem, 16)
                    sy.dma_start(out=dbg["mxc"], in_=mxc[:, :]).then_inc(ch["oout"].sem, 16)
                    sy.dma_start(out=dbg["pt"], in_=pt1[:, :]).then_inc(ch["oout"].sem, 16)
                ch["oout"].n += 96
                block.sync(_dbg0)
            if full:
                drain_all()
                boundary = BI_SEQ > SEQ - 1 and s + 1 < min(SEQ, BI_SEQ)
                if boundary:
                    emit_ln(s + 1)
                    ln_done.add(s + 1)
                    dma_defer = []
                    for ci in range(8):
                        plan_ffn_chunk(s + 1, ci, defer_dma=dma_defer)
                    pace_reset()
                    planned.add(s + 1)
                cctx = emit_combine_a(s)
                if boundary:
                    for f in dma_defer:
                        block.gpsimd(f)
                emit_combine_b(cctx)
                if boundary:
                    for _ in range(24):
                        block.tensor(lambda t: pef(t, 0.08))
                        block.vector(lambda v: vef(v, 0.07))
            if DBG and s == 0:
                def _dbg1(sy):
                    sy.wait_ge(ch["oout"].sem, ch["oout"].n)
                    sy.dma_start(out=dbg["hsb"], in_=hsb[:, :]).then_inc(ch["oout"].sem, 16)
                ch["oout"].n += 16
                block.sync(_dbg1)

        def _final(sy):
            sy.wait_ge(ch["ooutA"].sem, ch["ooutA"].n)
            sy.wait_ge(ch["ooutB"].sem, ch["ooutB"].n)
        block.sync(_final)

    return nc


def kernel(L0_attn_x, L0_attn_Wq, L0_attn_Wp, L0_attn_bp,
           L0_ffn_w1, L0_ffn_b1, L0_ffn_w2, L0_ffn_b2):
    x = np.ascontiguousarray(np.asarray(L0_attn_x, dtype=np.float32))
    wq = np.ascontiguousarray(np.asarray(L0_attn_Wq, dtype=np.float32))
    wp = np.ascontiguousarray(np.asarray(L0_attn_Wp, dtype=np.float32))
    bp = np.ascontiguousarray(np.asarray(L0_attn_bp, dtype=np.float32))
    w1 = np.ascontiguousarray(np.asarray(L0_ffn_w1, dtype=np.float32))
    b1 = np.ascontiguousarray(np.asarray(L0_ffn_b1, dtype=np.float32))
    w2 = np.ascontiguousarray(np.asarray(L0_ffn_w2, dtype=np.float32))
    b2 = np.ascontiguousarray(np.asarray(L0_ffn_b2, dtype=np.float32))

    nc = build_nc()
    in_maps = []
    for c in range(NCORE):
        in_maps.append({
            "x": np.ascontiguousarray(x[c * SEQ:(c + 1) * SEQ]),
            "wq": wq, "wp": wp, "bp": bp, "w1": w1, "b1": b1, "w2": w2, "b2": b2,
        })
    res = run_bass_kernel_spmd(nc, in_maps, core_ids=list(range(NCORE)))
    out = np.concatenate([np.asarray(r["out"]) for r in res.results], axis=0)
    if DBG:
        kernel.dbg = res.results
    return out


# version 30
# speedup vs baseline: 1.0104x; 1.0021x over previous
import numpy as np
import concourse.bass as bass
import concourse.mybir as mybir
from concourse.bass_utils import run_bass_kernel_spmd

F32 = mybir.dt.float32
BF16 = mybir.dt.bfloat16
I32 = mybir.dt.int32
AF = mybir.ActivationFunctionType
AX = mybir.AxisListType
ALU = mybir.AluOpType

NCORE = 8
SEQ = 4
N = 2048
D = 1024
NH = 16
E = 64
HF = 4096
TC = 512
DBG = False
import os
BI_SEQ = int(os.environ.get("BI_SEQ", "4"))
BI_PAIR = int(os.environ.get("BI_PAIR", "8"))
BI_STAGE = int(os.environ.get("BI_STAGE", "9"))
BI_PREP = int(os.environ.get("BI_PREP", "9"))
NOALT = int(os.environ.get("NOALT", "0"))
QPN = int(os.environ.get("QPN", "4"))
P1M = int(os.environ.get("P1M", "2"))
W1N = int(os.environ.get("W1N", "32"))
W2N = int(os.environ.get("W2N", "8"))
WPN = int(os.environ.get("WPN", "8"))
BI_PREP = int(os.environ.get("BI_PREP", "9"))
NOALT = int(os.environ.get("NOALT", "0"))
QPN = int(os.environ.get("QPN", "4"))
P1M = int(os.environ.get("P1M", "2"))


class Ch:

    def __init__(self, es, nc, name):
        self.sem = es.enter_context(nc.semaphore(name))
        self.n = 0

    def inc(self, inst, k=1):
        inst.then_inc(self.sem, k)
        self.n += k
        return self.n


def build_nc():
    import contextlib
    nc = bass.Bass("TRN2", target_bir_lowering=False)

    x_in = nc.dram_tensor("x", [SEQ, N, D], F32, kind="ExternalInput").ap()
    wq_in = nc.dram_tensor("wq", [NH, E, E], F32, kind="ExternalInput").ap()
    wp_in = nc.dram_tensor("wp", [D, D], F32, kind="ExternalInput").ap()
    bp_in = nc.dram_tensor("bp", [D], F32, kind="ExternalInput").ap()
    w1_in = nc.dram_tensor("w1", [HF, D], F32, kind="ExternalInput").ap()
    b1_in = nc.dram_tensor("b1", [HF], F32, kind="ExternalInput").ap()
    w2_in = nc.dram_tensor("w2", [D, HF], F32, kind="ExternalInput").ap()
    b2_in = nc.dram_tensor("b2", [D], F32, kind="ExternalInput").ap()
    out_d = nc.dram_tensor("out", [SEQ, N, D], F32, kind="ExternalOutput").ap()

    bw1t_d = nc.dram_tensor("bw1t_d", [HF // 128, 128, 1024], BF16, kind="Internal").ap()
    bw2t_d = nc.dram_tensor("bw2t_d", [D // 128, 128, 4096], BF16, kind="Internal").ap()
    wpt_d = nc.dram_tensor("wpt_d", [8, 128, 1024], BF16, kind="Internal").ap()

    dbg = {}
    if DBG:
        dbg["xnf"] = nc.dram_tensor("dbg_xnf", [128, 16384], BF16, kind="ExternalOutput").ap()
        dbg["resf"] = nc.dram_tensor("dbg_resf", [128, 16384], BF16, kind="ExternalOutput").ap()
        dbg["qA"] = nc.dram_tensor("dbg_qA", [128, 2048], BF16, kind="ExternalOutput").ap()
        dbg["qT"] = nc.dram_tensor("dbg_qT", [128, 2112], BF16, kind="ExternalOutput").ap()
        dbg["mxc"] = nc.dram_tensor("dbg_mxc", [128, 32], F32, kind="ExternalOutput").ap()
        dbg["pt"] = nc.dram_tensor("dbg_pt", [128, 1024], BF16, kind="ExternalOutput").ap()
        dbg["bw1t"] = nc.dram_tensor("dbg_bw1t", [128, 1024], BF16, kind="ExternalOutput").ap()
        dbg["hsb"] = nc.dram_tensor("dbg_hsb", [128, 16384], BF16, kind="ExternalOutput").ap()

    es = contextlib.ExitStack()
    with es:
        EC = es.enter_context
        xnf = EC(nc.sbuf_tensor("xnf", [128, 16384], BF16))
        resf = EC(nc.sbuf_tensor("resf", [128, 16384], BF16))
        gfm = EC(nc.sbuf_tensor("gfm", [128, 16384], BF16))
        wqt = EC(nc.sbuf_tensor("wqt", [128, 512], BF16))
        bpc = EC(nc.sbuf_tensor("bpc", [128, 8], F32))
        b1c = EC(nc.sbuf_tensor("b1c", [128, 32], F32))
        b2c = EC(nc.sbuf_tensor("b2c", [128, 8], F32))
        identb = EC(nc.sbuf_tensor("identb", [128, 128], BF16))
        identf = EC(nc.sbuf_tensor("identf", [128, 128], F32))
        PS0 = EC(nc.psum_tensor("PS0", [128, 1024], F32))
        PS1 = EC(nc.psum_tensor("PS1", [128, 1024], F32))
        PS2 = EC(nc.psum_tensor("PS2", [128, 1024], F32))
        PS3 = EC(nc.psum_tensor("PS3", [128, 1024], F32))
        PS2B = PS2.bitcast(BF16)
        PS3B = PS3.bitcast(BF16)
        PS0B = PS0.bitcast(BF16)
        PS1B = PS1.bitcast(BF16)

        block = EC(nc.Block())

        ch = {n: Ch(es, nc, n) for n in [
            "ldp", "bwout", "qp", "s1", "mx", "mxrow", "p2", "exp", "av",
            "divd", "rest", "resf", "lh", "lw1", "lw2", "lxr", "h", "hs",
            "gy", "gys", "sg", "sgt", "ost", "oout", "lnx", "lnst", "lnt",
            "lnf", "misc", "mrow", "vch", "ach", "ldpA", "ldpB", "lnxA", "lnxB",
            "lw1A", "lw1B", "lw2A", "lw2B", "ooutA", "ooutB",
            "yc", "sev", "gyh", "gact", "lwpA", "lwpB", "lxrA", "lxrB", "qg"]}

        def vedge(v, inst):
            inst.then_inc(ch["vch"].sem, 1)
            ch["vch"].n += 1
            v.wait_ge(ch["vch"].sem, ch["vch"].n)

        def aedge(a, inst):
            inst.then_inc(ch["ach"].sem, 1)
            ch["ach"].n += 1
            a.wait_ge(ch["ach"].sem, ch["ach"].n)

        with nc.sbuf_tensor("pstage0", [128, 4096], F32) as pstage0, \
             nc.sbuf_tensor("pstage1", [128, 4096], F32) as pstage1, \
             nc.sbuf_tensor("psgn", [128, 4096], BF16) as psgn, \
             nc.sbuf_tensor("psgn2", [128, 4096], BF16) as psgn2, \
             nc.sbuf_tensor("ptout", [128, 4096], BF16) as ptout, \
             nc.sbuf_tensor("palph", [128, 4], F32) as palph, \
             nc.sbuf_tensor("prow", [128, 128], F32) as prow, \
             nc.sbuf_tensor("iscr", [128, 128], I32) as iscr:

            pstages = [pstage0, pstage1]

            def _prep_gpsimd(g):
                g.iota(iscr[:, :], pattern=[[1, 128]], base=0,
                       channel_multiplier=-1).then_inc(ch["misc"].sem, 1)
            ch["misc"].n += 1
            block.gpsimd(_prep_gpsimd)

            m_iota = ch["misc"].n

            def _prep_vec0(v):
                v.wait_ge(ch["misc"].sem, m_iota)
                v.tensor_single_scalar(out=identb[:, :], in_=iscr[:, :], scalar=0,
                                       op=ALU.is_equal)
                v.tensor_single_scalar(out=identf[:, :], in_=iscr[:, :], scalar=0,
                                       op=ALU.is_equal).then_inc(ch["misc"].sem, 1)
            ch["misc"].n += 1
            block.vector(_prep_vec0)
            m_ident = ch["misc"].n
            m_biaspe_g = [m_ident]

            if BI_PREP >= 2:
                def _prep_bias_sync(s):
                    s.dma_start(out=prow[0:8, 0:128], in_=bp_in.rearrange("(a b) -> a b", b=128)).then_inc(ch["ldp"].sem, 16)
                    s.wait_ge(ch["misc"].sem, 1)
                    s.dma_start(out=prow[64:72, 0:128], in_=b2_in.rearrange("(a b) -> a b", b=128)).then_inc(ch["ldp"].sem, 16)
                    s.wait_ge(ch["misc"].sem, 1)
                    s.dma_start(out=prow[32:64, 0:128], in_=b1_in.rearrange("(a b) -> a b", b=128)).then_inc(ch["ldp"].sem, 16)
                    s.wait_ge(ch["misc"].sem, 1)
                ch["ldp"].n += 48
                block.sync(_prep_bias_sync)

                ldp_bias = ch["ldp"].n

                def _prep_bias_pe(t):
                    t.wait_ge(ch["ldp"].sem, ldp_bias)
                    t.wait_ge(ch["misc"].sem, m_ident)
                    t.matmul(out=PS0[:, 0:8], lhsT=prow[0:8, 0:128],
                             rhs=identf[0:8, 0:8], start=True, stop=True)
                    t.matmul(out=PS0[:, 8:16], lhsT=prow[64:72, 0:128],
                             rhs=identf[64:72, 64:72], start=True, stop=True)
                    t.matmul(out=PS1[:, 0:32], lhsT=prow[32:64, 0:128],
                             rhs=identf[32:64, 32:64], start=True,
                             stop=True).then_inc(ch["misc"].sem, 1)
                ch["misc"].n += 1
                block.tensor(_prep_bias_pe)

                m_biaspe = ch["misc"].n
                m_biaspe_g[0] = m_biaspe

                def _prep_bias_v(v):
                    v.wait_ge(ch["misc"].sem, m_biaspe)
                    v.tensor_copy(out=bpc[:, :], in_=PS0[:, 0:8])
                    v.tensor_copy(out=b2c[:, :], in_=PS0[:, 8:16])
                    v.tensor_copy(out=b1c[:, :],
                                  in_=PS1[:, 0:32]).then_inc(ch["misc"].sem, 1)
                ch["misc"].n += 1
                block.vector(_prep_bias_v)

            if BI_PREP >= 3:
                wq_bw0s = ch["bwout"].n

                def _prep_wq_sync(s):
                    s.wait_ge(ch["misc"].sem, m_biaspe_g[0])
                    for p in range(8):
                        if p >= 1:
                            s.wait_ge(ch["bwout"].sem, wq_bw0s + p * 16)
                        s.dma_start(out=prow[0:64, 0:64],
                                    in_=wq_in[2 * p]).then_inc(ch["ldp"].sem, 16)
                        s.wait_ge(ch["misc"].sem, 1)
                        s.dma_start(out=prow[0:64, 64:128],
                                    in_=wq_in[2 * p + 1]).then_inc(ch["ldp"].sem, 16)
                        s.wait_ge(ch["misc"].sem, 1)
                        s.wait_ge(ch["mrow"].sem, (p + 1))
                ch["ldp"].n += 8 * 32
                block.sync(_prep_wq_sync)

                wq_bw0 = ch["bwout"].n
                wq_s10 = ch["s1"].n

                def _prep_wq_v(v):
                    base = ch["ldp"].n - 8 * 32
                    for p in range(8):
                        v.wait_ge(ch["ldp"].sem, base + (p + 1) * 32)
                        v.tensor_copy(out=psgn[0:64, 0:128], in_=prow[0:64, 0:128]).then_inc(ch["misc"].sem, 1)
                        ch["misc"].n += 1
                        v.wait_ge(ch["s1"].sem, wq_s10 + 2 * (p + 1))
                        if p >= 1:
                            v.wait_ge(ch["bwout"].sem, wq_bw0 + p * 16)
                        v.tensor_copy(out=wqt[0:64, p * 64:(p + 1) * 64], in_=PS2B[0:64, 0:64])
                        v.tensor_copy(out=prow[0:64, 64:96].bitcast(BF16)[0:64, 0:64],
                                      in_=PS2B[0:64, 128:192]).then_inc(ch["mrow"].sem, 1)
                block.vector(_prep_wq_v)

                def _prep_wq_pe(t):
                    for p in range(8):
                        t.wait_ge(ch["misc"].sem, ch["misc"].n - 8 + (p + 1))
                        t.transpose(out=PS2B[0:64, 0:64], in_=psgn[0:64, 0:64],
                                    identity=identb[0:64, 0:64]).then_inc(ch["s1"].sem, 1)
                        ch["s1"].n += 1
                        t.transpose(out=PS2B[0:64, 128:192], in_=psgn[0:64, 64:128],
                                    identity=identb[0:64, 0:64]).then_inc(ch["s1"].sem, 1)
                        ch["s1"].n += 1
                block.tensor(_prep_wq_pe)

                def _prep_wq_g(g):
                    for p in range(8):
                        g.wait_ge(ch["mrow"].sem, p + 1)
                        g.dma_start(out=wqt[64:128, p * 64:(p + 1) * 64],
                                    in_=prow[0:64, 64:96].bitcast(BF16)[0:64, 0:64]
                                    ).then_inc(ch["bwout"].sem, 16)
                ch["bwout"].n += 8 * 16
                ch["mrow"].n += 8
                block.gpsimd(_prep_wq_g)

            if BI_PREP >= 4:
                mxb0 = ch["mx"].n
                expb0 = ch["exp"].n

                ldP = [ch["ldpA"], ch["ldpB"]]
                base_p = [ldP[0].n, ldP[1].n]

                def wload_sem(g):
                    return ldP[g % 2], base_p[g % 2] + (g // 2 + 1) * 16

                def _prep_w_sync(s):
                    s.wait_ge(ch["bwout"].sem, ch["bwout"].n)
                    for i in range(W1N):
                        if i >= 2:
                            s.wait_ge(ch["mx"].sem, mxb0 + i - 1)
                            s.wait_ge(ch["exp"].sem, expb0 + i - 1)
                        s.dma_start(out=pstages[i % 2][:, 0:1024],
                                    in_=w1_in[i * 128:(i + 1) * 128, :]).then_inc(
                                        ldP[i % 2].sem, 16)
                    for i in range(W2N):
                        g = W1N + i
                        s.wait_ge(ch["mx"].sem, mxb0 + W1N + max(0, i - 1))
                        s.wait_ge(ch["exp"].sem, expb0 + W1N + max(0, i - 1))
                        s.dma_start(out=pstages[g % 2][:, :],
                                    in_=w2_in[i * 128:(i + 1) * 128, :]).then_inc(
                                        ldP[g % 2].sem, 16)
                    for i in range(WPN):
                        g = W1N + W2N + i
                        s.wait_ge(ch["mx"].sem, mxb0 + W1N + W2N + max(0, i - 1))
                        s.dma_start(out=pstages[g % 2][:, 0:1024],
                                    in_=wp_in[i * 128:(i + 1) * 128, :]).then_inc(
                                        ldP[g % 2].sem, 16)
                nw = W1N + W2N + WPN
                ch["ldpA"].n += ((nw + 1) // 2) * 16
                ch["ldpB"].n += (nw // 2) * 16
                block.sync(_prep_w_sync)

                s1b0 = ch["s1"].n
                bwo0 = ch["bwout"].n
                sbase = ch["s1"].n
                p2b0 = ch["p2"].n

                def _prep_w_v(v):
                    n_b = W1N + W2N * 4 + WPN
                    for i in range(W1N):
                        sm, val = wload_sem(i)
                        v.wait_ge(sm.sem, val)
                        if i >= 1:
                            v.wait_ge(ch["mx"].sem, mxb0 + i)
                        vedge(v, v.reduce_sum(out=palph[:, 0:1],
                                               in_=pstages[i % 2][:, 0:1024],
                                               axis=AX.X, apply_absolute_value=True))
                        v.wait_ge(ch["exp"].sem, expb0 + i + 1)
                        if i >= 1:
                            v.wait_ge(ch["s1"].sem, s1b0 + i)
                        v.tensor_scalar(out=ptout[:, 0:1024], in0=psgn2[:, 0:1024],
                                        scalar1=palph[:, 0:1], scalar2=1.0 / 1024,
                                        op0=ALU.mult, op1=ALU.mult).then_inc(ch["mx"].sem, 1)
                        if i >= 1:
                            v.wait_ge(ch["bwout"].sem, bwo0 + (i - 1) * 16)
                            pb = PS2B if ((i - 1) % 2 == 0 or NOALT) else PS3B
                            v.tensor_copy(out=psgn[:, 0:1024],
                                          in_=pb[:, 0:1024]).then_inc(ch["p2"].sem, 1)
                    if W1N >= 1:
                        v.wait_ge(ch["s1"].sem, s1b0 + W1N)
                        v.wait_ge(ch["bwout"].sem, bwo0 + (W1N - 1) * 16)
                        pb = PS2B if ((W1N - 1) % 2 == 0 or NOALT) else PS3B
                        v.tensor_copy(out=psgn[:, 0:1024],
                                      in_=pb[:, 0:1024]).then_inc(ch["p2"].sem, 1)
                    for i in range(W2N):
                        sm, val = wload_sem(W1N + i)
                        v.wait_ge(sm.sem, val)
                        v.wait_ge(ch["mx"].sem, mxb0 + W1N + i)
                        vedge(v, v.reduce_sum(out=palph[:, 0:1],
                                               in_=pstages[i % 2][:, :],
                                               axis=AX.X, apply_absolute_value=True))
                        v.wait_ge(ch["exp"].sem, expb0 + W1N + i + 1)
                        v.tensor_scalar(out=ptout[:, :], in0=psgn2[:, :],
                                        scalar1=palph[:, 0:1], scalar2=1.0 / 4096,
                                        op0=ALU.mult, op1=ALU.mult).then_inc(ch["mx"].sem, 1)
                        v.wait_ge(ch["bwout"].sem, bwo0 + (W1N + i) * 16)
                        for b4 in range(4):
                            idx = W1N + i * 4 + b4
                            v.wait_ge(ch["s1"].sem, sbase + idx + 1)
                            pb = PS2B if idx % 2 == 0 else PS3B
                            v.tensor_copy(out=psgn[:, b4 * 1024:(b4 + 1) * 1024],
                                          in_=pb[:, 0:1024]).then_inc(ch["p2"].sem, 1)
                    for i in range(WPN):
                        sm, val = wload_sem(W1N + W2N + i)
                        v.wait_ge(sm.sem, val)
                        v.wait_ge(ch["s1"].sem, s1b0 + W1N + W2N * 4 + i)
                        v.tensor_copy(out=ptout[:, 0:1024],
                                      in_=pstages[i % 2][:, 0:1024]).then_inc(ch["mx"].sem, 1)
                        idx = W1N + W2N * 4 + i
                        v.wait_ge(ch["s1"].sem, sbase + idx + 1)
                        v.wait_ge(ch["bwout"].sem, bwo0 + (W1N + W2N + i) * 16)
                        pb = PS2B if idx % 2 == 0 else PS3B
                        v.tensor_copy(out=psgn[:, 0:1024],
                                      in_=pb[:, 0:1024]).then_inc(ch["p2"].sem, 1)
                ch["mx"].n += W1N + W2N + WPN
                ch["p2"].n += W1N + W2N * 4 + WPN
                block.vector(_prep_w_v)

                def _prep_w_act(sc):
                    for i in range(W1N):
                        sm, val = wload_sem(i)
                        sc.wait_ge(sm.sem, val)
                        if i >= 1:
                            sc.wait_ge(ch["mx"].sem, mxb0 + i)
                        sc.activation(out=psgn2[:, 0:1024], in_=pstages[i % 2][:, 0:1024],
                                      func=AF.Sign).then_inc(ch["exp"].sem, 1)
                    for i in range(W2N):
                        sm, val = wload_sem(W1N + i)
                        sc.wait_ge(sm.sem, val)
                        sc.wait_ge(ch["mx"].sem, mxb0 + W1N + i)
                        sc.activation(out=psgn2[:, :], in_=pstages[i % 2][:, :],
                                      func=AF.Sign).then_inc(ch["exp"].sem, 1)
                ch["exp"].n += W1N + W2N
                block.scalar(_prep_w_act)

                def _prep_w_pe(t):
                    cnt = 0
                    for i in range(W1N):
                        t.wait_ge(ch["mx"].sem, mxb0 + i + 1)
                        if i >= 2 or NOALT:
                            t.wait_ge(ch["p2"].sem, p2b0 + max(0, i - (1 if not NOALT else 0)))
                        pb = PS2B if (i % 2 == 0 or NOALT) else PS3B
                        for dc in range(8):
                            inst = t.transpose(out=pb[:, dc * 128:(dc + 1) * 128],
                                               in_=ptout[:, dc * 128:(dc + 1) * 128],
                                               identity=identb[:, :])
                        inst.then_inc(ch["s1"].sem, 1)
                        cnt += 1
                    for i in range(W2N):
                        t.wait_ge(ch["mx"].sem, mxb0 + W1N + i + 1)
                        for b4 in range(4):
                            idx = i * 4 + b4
                            if W1N + idx >= 2:
                                t.wait_ge(ch["p2"].sem, p2b0 + W1N + idx - 1)
                            pb = PS2B if (32 + idx) % 2 == 0 else PS3B
                            for k in range(8):
                                ht = b4 * 8 + k
                                inst = t.transpose(out=pb[:, k * 128:(k + 1) * 128],
                                                   in_=ptout[:, ht * 128:(ht + 1) * 128],
                                                   identity=identb[:, :])
                            inst.then_inc(ch["s1"].sem, 1)
                            cnt += 1
                    for i in range(WPN):
                        t.wait_ge(ch["mx"].sem, mxb0 + W1N + W2N + i + 1)
                        idx = W1N + W2N * 4 + i
                        t.wait_ge(ch["p2"].sem, p2b0 + idx - 1)
                        pb = PS2B if idx % 2 == 0 else PS3B
                        for dc in range(8):
                            inst = t.transpose(out=pb[:, dc * 128:(dc + 1) * 128],
                                               in_=ptout[:, dc * 128:(dc + 1) * 128],
                                               identity=identb[:, :])
                        inst.then_inc(ch["s1"].sem, 1)
                        cnt += 1
                n_batches = W1N + W2N * 4 + WPN
                ch["s1"].n += n_batches
                block.tensor(_prep_w_pe)

                def _prep_w_out(g):
                    pbase = ch["p2"].n - n_batches
                    for i in range(W1N):
                        g.wait_ge(ch["p2"].sem, pbase + i + 1)
                        g.dma_start(out=bw1t_d[i], in_=psgn[:, 0:1024]).then_inc(ch["bwout"].sem, 16)
                    for i in range(W2N):
                        g.wait_ge(ch["p2"].sem, pbase + W1N + (i + 1) * 4)
                        g.dma_start(out=bw2t_d[i], in_=psgn[:, :]).then_inc(ch["bwout"].sem, 16)
                    for i in range(WPN):
                        g.wait_ge(ch["p2"].sem, pbase + W1N + W2N * 4 + i + 1)
                        g.dma_start(out=wpt_d[i], in_=psgn[:, 0:1024]).then_inc(ch["bwout"].sem, 16)
                ch["bwout"].n += (W1N + W2N + WPN) * 16
                block.gpsimd(_prep_w_out)

            if DBG:
                def _dbg_bw(s):
                    s.wait_ge(ch["bwout"].sem, ch["bwout"].n)
                    s.dma_start(out=dbg["bw1t"], in_=bw1t_d[0]).then_inc(ch["oout"].sem, 16)
                ch["oout"].n += 16
                block.sync(_dbg_bw)

        qA = EC(nc.sbuf_tensor("qA", [128, 2048], BF16))
        qB = EC(nc.sbuf_tensor("qB", [128, 2048], BF16))
        qC = EC(nc.sbuf_tensor("qC", [128, 2048], BF16))
        qD = EC(nc.sbuf_tensor("qD", [128, 2048], BF16))
        qE = EC(nc.sbuf_tensor("qE", [128, 2048], BF16))
        qT = EC(nc.sbuf_tensor("qT", [128, 2112], BF16))
        qsub = EC(nc.sbuf_tensor("qsub", [128, 512], BF16))
        pt0 = EC(nc.sbuf_tensor("pt0", [128, 1024], BF16))
        pt1 = EC(nc.sbuf_tensor("pt1", [128, 1024], BF16))
        subm = EC(nc.sbuf_tensor("subm", [128, 32], F32))
        diag = EC(nc.sbuf_tensor("diag", [128, 32], F32))
        mxc = EC(nc.sbuf_tensor("mxc", [128, 32], F32))
        tmpc = EC(nc.sbuf_tensor("tmpc", [128, 8], F32))
        rcz = EC(nc.sbuf_tensor("rcz", [128, 16], F32))
        cneg = EC(nc.sbuf_tensor("cneg", [128, 1], F32))
        xst0 = EC(nc.sbuf_tensor("xst0", [128, 1024], F32))
        xst1 = EC(nc.sbuf_tensor("xst1", [128, 1024], F32))
        sqscr = EC(nc.sbuf_tensor("sqscr", [128, 1024], F32))
        lns = EC(nc.sbuf_tensor("lns", [128, 8], F32))
        xnstage = EC(nc.sbuf_tensor("xnstage", [128, 1024], BF16))
        hsb = EC(nc.sbuf_tensor("hsb", [128, 16384], BF16))
        w1s0 = EC(nc.sbuf_tensor("w1s0", [128, 1024], BF16))
        w1s1 = EC(nc.sbuf_tensor("w1s1", [128, 1024], BF16))
        w2s0 = EC(nc.sbuf_tensor("w2s0", [128, 2048], BF16))
        w2s1 = EC(nc.sbuf_tensor("w2s1", [128, 2048], BF16))
        rtm = EC(nc.sbuf_tensor("rtm", [128, 2048], BF16))
        gscr = EC(nc.sbuf_tensor("gscr", [128, 512], F32))
        sscr = EC(nc.sbuf_tensor("sscr", [128, 512], F32))
        sgscr = EC(nc.sbuf_tensor("sgscr", [128, 512], F32))
        xrj0 = EC(nc.sbuf_tensor("xrj0", [128, 512], F32))
        xrj1 = EC(nc.sbuf_tensor("xrj1", [128, 512], F32))
        wps0 = EC(nc.sbuf_tensor("wps0", [128, 1024], BF16))
        wps1 = EC(nc.sbuf_tensor("wps1", [128, 1024], BF16))
        sgst0 = EC(nc.sbuf_tensor("sgst0", [128, 512], F32))
        sgst1 = EC(nc.sbuf_tensor("sgst1", [128, 512], F32))
        mxst = sqscr.bitcast(BF16)
        restm = rtm[:, :]

        prep_done = {"bwout": ch["bwout"].n, "p2": ch["p2"].n, "mrow": ch["mrow"].n,
                     "misc": ch["misc"].n}

        def _barrier(e):
            e.wait_ge(ch["bwout"].sem, prep_done["bwout"])
            e.wait_ge(ch["p2"].sem, prep_done["p2"])
        for engname in ["tensor", "vector", "scalar", "gpsimd", "sync"]:
            getattr(block, engname)(_barrier)

        def _post_prep(v):
            if BI_PREP >= 5:
                v.memset(qA[64:96, :], 0.0)
                v.memset(qB[64:96, :], 0.0)
                vedge(v, v.memset(qD[64:96, :], 0.0))
                vedge(v, v.memset(qE[64:96, :], 0.0))
                v.memset(qA[64:65, :], 1.0)
                v.memset(qD[64:65, :], 1.0)
                v.memset(qT[:, :].rearrange("p (s c) -> p s c", c=66)[:, :, 64:65], 1.0)
                v.memset(qT[:, :].rearrange("p (s c) -> p s c", c=66)[:, :, 65:66], 0.0)
                v.memset(lns[:, 7:8], 1e-5)
                v.memset(cneg[:, :], -32.0)
            v.memset(tmpc[:, 0:8], 0.0).then_inc(ch["qp"].sem, 1)
        ch["qp"].n += 1
        block.vector(_post_prep)



        def emit_ln(s):
            xstg = [xst0, xst1]

            lnP = [ch["lnxA"], ch["lnxB"]]
            lnb = [lnP[0].n, lnP[1].n]

            def _sync(sy):
                for tt in range(16):
                    if tt >= 2:
                        sy.wait_ge(ch["lnst"].sem, ch["lnst"].n + tt - 1)
                    sy.dma_start(out=xstg[tt % 2][:, :],
                                 in_=x_in[s, tt * 128:(tt + 1) * 128, :]
                                 ).then_inc(lnP[tt % 2].sem, 16)
            ch["lnxA"].n += 8 * 16
            ch["lnxB"].n += 8 * 16
            block.sync(_sync)

            lnt0 = ch["lnt"].n
            lnf0 = ch["lnf"].n
            lnst0 = ch["lnst"].n
            s1b = ch["s1"].n
            h_prev = ch["h"].n

            def _act(sc):
                for tt in range(16):
                    sc.wait_ge(lnP[tt % 2].sem, lnb[tt % 2] + (tt // 2 + 1) * 16)
                    if tt >= 1:
                        sc.wait_ge(ch["lnst"].sem, lnst0 + tt)
                    sc.activation(out=sqscr[:, :], in_=xstg[tt % 2][:, :], func=AF.Square,
                                  accum_out=lns[:, 1:2]).then_inc(ch["lnt"].sem, 1)
                    sc.wait_ge(ch["lnf"].sem, lnf0 + 2 * tt + 1)
                    sc.activation(out=lns[:, 5:6], in_=lns[:, 4:5], func=AF.Sqrt,
                                  bias=lns[:, 7:8], scale=1.0).then_inc(ch["lnt"].sem, 1)
            ch["lnt"].n += 32
            block.scalar(_act)

            def _vec(v):
                for tt in range(16):
                    v.wait_ge(lnP[tt % 2].sem, lnb[tt % 2] + (tt // 2 + 1) * 16)
                    vedge(v, v.reduce_sum(out=lns[:, 0:1], in_=xstg[tt % 2][:, :],
                                           axis=AX.X))
                    vedge(v, v.tensor_scalar(out=lns[:, 2:3], in0=lns[:, 0:1],
                                             scalar1=-1.0 / 1024,
                                             scalar2=None, op0=ALU.mult))
                    vedge(v, v.tensor_mul(out=lns[:, 3:4], in0=lns[:, 2:3],
                                          in1=lns[:, 2:3]))
                    v.wait_ge(ch["lnt"].sem, lnt0 + 2 * tt + 1)
                    v.scalar_tensor_tensor(out=lns[:, 4:5], in0=lns[:, 1:2],
                                           scalar=1.0 / 1024, in1=lns[:, 3:4],
                                           op0=ALU.mult, op1=ALU.subtract
                                           ).then_inc(ch["lnf"].sem, 1)
                    v.wait_ge(ch["lnt"].sem, lnt0 + 2 * tt + 2)
                    vedge(v, v.reciprocal(out=lns[:, 6:7], in_=lns[:, 5:6]))
                    if tt >= 1:
                        v.wait_ge(ch["s1"].sem, s1b + tt)
                    v.tensor_scalar(out=xnstage[:, :], in0=xstg[tt % 2][:, :],
                                    scalar1=lns[:, 2:3], scalar2=lns[:, 6:7],
                                    op0=ALU.add, op1=ALU.mult).then_inc(ch["lnst"].sem, 1)
                    if tt == 0:
                        v.wait_ge(ch["h"].sem, h_prev)
                    v.wait_ge(ch["s1"].sem, s1b + tt + 1)
                    v.tensor_copy(
                        out=xnf[:, :].rearrange("p (dc t) -> p dc t", t=2048)
                        [:, :, tt * 128:(tt + 1) * 128],
                        in_=PS1B[:, 1024:2048].rearrange("p (dc t) -> p dc t", t=128)
                        ).then_inc(ch["lnf"].sem, 1)
            ch["lnf"].n += 32
            ch["lnst"].n += 16
            block.vector(_vec)

            hs_prev = ch["hs"].n
            gys_prev = ch["gys"].n

            def _pe(t):
                t.wait_ge(ch["hs"].sem, hs_prev)
                t.wait_ge(ch["gys"].sem, gys_prev)
                for tt in range(16):
                    t.wait_ge(ch["lnst"].sem, lnst0 + tt + 1)
                    if tt >= 1:
                        t.wait_ge(ch["lnf"].sem, lnf0 + 2 * tt)
                    for dc in range(8):
                        inst = t.transpose(out=PS1B[:, 1024 + dc * 128:1024 + (dc + 1) * 128],
                                           in_=xnstage[:, dc * 128:(dc + 1) * 128],
                                           identity=identb[:, :])
                    inst.then_inc(ch["s1"].sem, 1)
            ch["s1"].n += 16
            block.tensor(_pe)

        pair_ctx = {}

        def emit_qproj_fm(s, p):
            qp0 = ch["qp"].n
            sev_prev = ch["sev"].n
            pair_ctx[p] = qp0

            def _fm_pe(t):
                t.wait_ge(ch["lnf"].sem, ch["lnf"].n)
                if p == 0:
                    t.wait_ge(ch["sev"].sem, sev_prev)
                for half in range(2):
                    if half == 1:
                        t.wait_ge(ch["qp"].sem, qp0 + 5)
                    for c in range(2):
                        t.matmul(out=PS2[0:64, c * 512:(c + 1) * 512],
                                 lhsT=wqt[0:64, p * 64:(p + 1) * 64],
                                 rhs=xnf[0:64, p * 2048 + half * 1024 + c * 512:
                                         p * 2048 + half * 1024 + (c + 1) * 512],
                                 start=True, stop=True, tile_position=(0, 0))
                        t.matmul(out=PS2[64:128, c * 512:(c + 1) * 512],
                                 lhsT=wqt[64:128, p * 64:(p + 1) * 64],
                                 rhs=xnf[64:128, p * 2048 + half * 1024 + c * 512:
                                         p * 2048 + half * 1024 + (c + 1) * 512],
                                 start=True, stop=True, tile_position=(64, 64))
                        inst = t.matmul(out=PS3[0:64, c * 512:(c + 1) * 512],
                                        lhsT=wqt[64:128, p * 64:(p + 1) * 64],
                                        rhs=xnf[64:128, p * 2048 + half * 1024 + c * 512:
                                                p * 2048 + half * 1024 + (c + 1) * 512],
                                        start=True, stop=True, tile_position=(64, 0))
                    inst.then_inc(ch["p2"].sem, 1)
                    pef(t, 0.003)
            p2q = ch["p2"].n
            ch["p2"].n += 2
            block.tensor(_fm_pe)

            def _fm_ev(sc):
                for half in range(2):
                    sc.wait_ge(ch["p2"].sem, p2q + half + 1)
                    hs = slice(half * 1024, (half + 1) * 1024)
                    sc.activation(out=qA[0:64, hs], in_=PS2[0:64, 0:1024],
                                  func=AF.Identity, bias=0.0, scale=1.0)
                    sc.activation(out=qC[64:128, hs], in_=PS2[64:128, 0:1024],
                                  func=AF.Identity, bias=0.0,
                                  scale=1.0).then_inc(ch["qp"].sem, 2)
                    sc.activation(out=qB[0:64, hs], in_=PS2[0:64, 0:1024],
                                  func=AF.Identity, bias=0.0, scale=1.0)
                    sc.activation(out=qD[0:64, hs], in_=PS3[0:64, 0:1024],
                                  func=AF.Identity, bias=0.0, scale=1.0)
                    sc.activation(out=qE[0:64, hs], in_=PS3[0:64, 0:1024],
                                  func=AF.Identity, bias=0.0,
                                  scale=1.0).then_inc(ch["qp"].sem, 3)
            ch["qp"].n += 10
            block.scalar(_fm_ev)

            def _fm_gather(v):
                v.wait_ge(ch["qp"].sem, qp0 + 7)
                v.tensor_copy(out=qsub[0:64, :],
                              in_=qA[0:64, :].rearrange("p (a b) -> p a b", b=4)
                              [:, :, 0])
                v.tensor_copy(out=qsub[64:128, :],
                              in_=qC[64:128, :].rearrange("p (a b) -> p a b", b=4)
                              [:, :, 0]).then_inc(ch["qg"].sem, 1)
                vef(v, 0.05)
            ch["qg"].n += 1
            block.vector(_fm_gather)

        def emit_attn_main(s, p):
            STG = BI_STAGE
            qp0 = pair_ctx[p]
            if fq["act"]:
                def _af(sc):
                    aef(sc, 1.0)
                block.scalar(_af)

            resf_prev = ch["resf"].n
            ost_prev = ch["ost"].n

            def _qproj_tm(t):
                for bb in range(2):
                    if bb == 0:
                        t.wait_ge(ch["resf"].sem, resf_prev)
                        if p == 0:
                            t.wait_ge(ch["ost"].sem, ost_prev)
                    t.wait_ge(ch["qp"].sem, qp0 + 10 + (0 if bb == 0 else 1))
                    for k in range(8):
                        nt = bb * 8 + k
                        t.transpose(out=PS0B[:, k * 64:(k + 1) * 64],
                                    in_=qA[0:64, nt * 128:(nt + 1) * 128],
                                    identity=identb[0:64, 0:64])
                        inst = t.transpose(out=PS1B[:, k * 64:(k + 1) * 64],
                                           in_=qC[64:128, nt * 128:(nt + 1) * 128],
                                           identity=identb[64:128, 64:128])
                    inst.then_inc(ch["p2"].sem, 1)
                    pef(t, 0.003)
            p2q = ch["p2"].n
            ch["p2"].n += 2
            block.tensor(_qproj_tm)

            def _qproj_ev(v):
                for bb in range(2 if QPN >= 3 else 0):
                    v.wait_ge(ch["p2"].sem, p2q + 1 + bb)
                    v.tensor_copy(
                        out=qT[:, :].rearrange("p (ss c) -> p ss c", c=66)
                        [:, bb * 8:(bb + 1) * 8, 0:64],
                        in_=PS0B[:, 0:512].rearrange("p (k e) -> p k e", e=64))
                    v.tensor_copy(
                        out=qT[:, :].rearrange("p (ss c) -> p ss c", c=66)
                        [:, 16 + bb * 8:16 + (bb + 1) * 8, 0:64],
                        in_=PS1B[:, 0:512].rearrange("p (k e) -> p k e", e=64)
                        ).then_inc(ch["qp"].sem, 1)
                if QPN >= 4:
                    v.wait_ge(ch["qp"].sem, qp0 + 12)
                    vedge(v, v.tensor_tensor(
                        out=sqscr[:, :].rearrange("p (ss e) -> p ss e", e=64),
                        in0=qT[:, :].rearrange("p (ss c) -> p ss c", c=66)[:, 0:16, 0:64],
                        in1=qT[:, :].rearrange("p (ss c) -> p ss c", c=66)[:, 0:16, 0:64],
                        op=ALU.mult))
                    vedge(v, v.reduce_sum(
                        out=diag[:, 0:16],
                        in_=sqscr[:, :].rearrange("p (ss e) -> p ss e", e=64), axis=AX.X))
                    vedge(v, v.tensor_tensor(
                        out=sqscr[:, :].rearrange("p (ss e) -> p ss e", e=64),
                        in0=qT[:, :].rearrange("p (ss c) -> p ss c", c=66)[:, 16:32, 0:64],
                        in1=qT[:, :].rearrange("p (ss c) -> p ss c", c=66)[:, 16:32, 0:64],
                        op=ALU.mult))
                    v.reduce_sum(out=diag[:, 16:32],
                                 in_=sqscr[:, :].rearrange("p (ss e) -> p ss e", e=64),
                                 axis=AX.X).then_inc(ch["qp"].sem, 1)
                    vef(v, 0.05)
            ch["qp"].n += 3
            block.vector(_qproj_ev)

            if STG < 3:
                return
            s10 = ch["s1"].n
            mx0 = ch["mx"].n

            qp_all = ch["qp"].n

            P1R = [PS0, PS2, PS3]

            qg_all = ch["qg"].n

            def _p1_t(t, nt):
                if nt == 0:
                    t.wait_ge(ch["qp"].sem, qp_all - 6)
                    t.wait_ge(ch["qg"].sem, qg_all)
                if nt == 2:
                    t.wait_ge(ch["qp"].sem, qp_all - 1)
                PSe = P1R[(nt + 1) % 3]
                if nt >= 3:
                    t.wait_ge(ch["mx"].sem, mx0 + (nt - 2) * 2)
                t.matmul(out=PSe[0:64, 0:512],
                         lhsT=qA[0:64, nt * 128:nt * 128 + 64],
                         rhs=qsub[0:64, :], start=True, stop=True,
                         tile_position=(0, 0))
                t.matmul(out=PSe[64:128, 0:512],
                         lhsT=qA[0:64, nt * 128 + 64:nt * 128 + 128],
                         rhs=qsub[0:64, :], start=True, stop=True,
                         tile_position=(0, 64))
                t.matmul(out=PSe[0:64, 512:1024],
                         lhsT=qC[64:128, nt * 128:nt * 128 + 64],
                         rhs=qsub[64:128, :], start=True, stop=True,
                         tile_position=(64, 0))
                mm = t.matmul(out=PSe[64:128, 512:1024],
                              lhsT=qC[64:128, nt * 128 + 64:nt * 128 + 128],
                              rhs=qsub[64:128, :], start=True, stop=True,
                              tile_position=(64, 64))
                mm.then_inc(ch["s1"].sem, 1)
                pef(t, 0.003)

            def _p1_vv(v, nt):
                PSe = P1R[(nt + 1) % 3]
                v.wait_ge(ch["s1"].sem, s10 + nt + 1)
                v.reduce_max(out=subm[:, nt:nt + 1], in_=PSe[:, 0:512], axis=AX.X)
                inst = v.reduce_max(out=subm[:, 16 + nt:17 + nt],
                                    in_=PSe[:, 512:1024], axis=AX.X)
                inst.then_inc(ch["mx"].sem, 2)
                if nt == 15:
                    v.wait_ge(ch["mx"].sem, mx0 + 32)
                    v.wait_ge(ch["qp"].sem, qp_all)
                    v.tensor_max(out=mxc[:, :], in0=subm[:, :],
                                 in1=diag[:, :]).then_inc(ch["mx"].sem, 1)

            for nt in range(16):
                block.tensor(lambda t, nt=nt: _p1_t(t, nt))
                block.vector(lambda v, nt=nt: _p1_vv(v, nt))
            ch["s1"].n += 16
            ch["mx"].n += 33

            if STG < 4:
                return
            mrow0 = ch["mrow"].n

            def _mx_pe(t):
                t.wait_ge(ch["mx"].sem, ch["mx"].n)
                for h in range(2):
                    for half in range(2):
                        PSx = PS2 if h == 0 else PS3
                        r = 64 * half
                        for c in range(8):
                            nt = half * 8 + c
                            inst = t.matmul(out=PSx[r:r + 1, c * 128:(c + 1) * 128],
                                            lhsT=mxc[:, h * 16 + nt:h * 16 + nt + 1],
                                            rhs=identf[:, :], start=(c in (0, 4)),
                                            stop=(c in (3, 7)),
                                            tile_position=(0, r))
                        inst.then_inc(ch["p2"].sem, 1)
                        pef(t, 0.005)
            ch["p2"].n += 4
            block.tensor(_mx_pe)

            p2m = ch["p2"].n - 4

            mxrow_prev = ch["mxrow"].n

            def _mx_v(v):
                v.wait_ge(ch["p2"].sem, p2m + 4)
                for i in range(4):
                    h, half = divmod(i, 2)
                    PSx = PS2 if h == 0 else PS3
                    r = 64 * half
                    if i == 0:
                        v.wait_ge(ch["mxrow"].sem, mxrow_prev)
                    if i == 2:
                        v.wait_ge(ch["mxrow"].sem, mxrow_prev + 16)
                    v.tensor_scalar(out=mxst[0:1, half * 1024:(half + 1) * 1024],
                                    in0=PSx[r:r + 1, 0:1024], scalar1=-1.0, scalar2=None,
                                    op0=ALU.mult).then_inc(ch["mrow"].sem, 1)
                vef(v, 0.03)
            ch["mrow"].n += 4
            block.vector(_mx_v)

            def _mx_g(g):
                g.wait_ge(ch["mrow"].sem, mrow0 + 2)
                g.dma_start(out=qB[64:65, :], in_=mxst[0:1, :]).then_inc(ch["mxrow"].sem, 16)
                g.wait_ge(ch["mrow"].sem, mrow0 + 4)
                g.dma_start(out=qE[64:65, :], in_=mxst[0:1, :]).then_inc(ch["mxrow"].sem, 16)
            ch["mxrow"].n += 32
            block.sync(_mx_g)

            if STG < 5:
                return
            slot_map = []
            for nt in range(16):
                if nt < 7:
                    slot_map.append((PS0, nt * 66))
                elif nt < 14:
                    slot_map.append((PS0, 512 + (nt - 7) * 66))
                else:
                    slot_map.append((PS1, (nt - 14) * 66))

            for h in range(2):
                qlhs = qA if h == 0 else qD
                qrhs = qB if h == 0 else qE
                p20 = ch["p2"].n
                exp0 = ch["exp"].n
                av0 = ch["av"].n

                def _p2_grp(t, grp, h=h, qlhs=qlhs, qrhs=qrhs, p20=p20,
                            exp0=exp0, av0=av0):
                    if grp == 0:
                        t.wait_ge(ch["mxrow"].sem, ch["mxrow"].n - 16 * (1 - h))
                        if h == 0:
                            t.wait_ge(ch["mx"].sem, ch["mx"].n)
                    for it in range(grp * 4, grp * 4 + 4):
                        half, J = divmod(it, 16)
                        PSa = PS2 if it % 2 == 0 else PS3
                        if it >= 2:
                            t.wait_ge(ch["exp"].sem, exp0 + it - 1)
                        t.matmul(out=PSa[:, 0:512],
                                 lhsT=qlhs[0:96, J * 128:(J + 1) * 128],
                                 rhs=qrhs[0:96, half * 1024:half * 1024 + 512],
                                 start=True, stop=True)
                        t.matmul(out=PSa[:, 512:1024],
                                 lhsT=qlhs[0:96, J * 128:(J + 1) * 128],
                                 rhs=qrhs[0:96, half * 1024 + 512:half * 1024 + 1024],
                                 start=True, stop=True).then_inc(ch["p2"].sem, 1)
                        pef(t, 0.0145)
                        if it >= 1:
                            Jp = (it - 1) % 16
                            halfp = (it - 1) // 16
                            t.wait_ge(ch["exp"].sem, exp0 + it)
                            if h == 1 and it == 1 and BI_STAGE >= 6:
                                t.wait_ge(ch["divd"].sem, ch["divd"].n)
                            ptb = pt0 if (it - 1) % 2 == 0 else pt1
                            for k in range(8):
                                nt = halfp * 8 + k
                                pst, col = slot_map[nt]
                                inst = t.matmul(
                                    out=pst[:, col:col + 65],
                                    lhsT=ptb[:, k * 128:(k + 1) * 128],
                                    rhs=qT[:, (h * 16 + Jp) * 66:(h * 16 + Jp) * 66 + 65],
                                    start=(Jp == 0 and nt in (0, 7, 8, 14)),
                                    stop=(Jp == 15 and nt in (6, 7, 13, 15)))
                            if Jp == 15:
                                inst.then_inc(ch["av"].sem, 1)
                    if grp == 7:
                        t.wait_ge(ch["exp"].sem, exp0 + 32)
                        ptb = pt1
                        for k in range(8):
                            nt = 8 + k
                            pst, col = slot_map[nt]
                            inst = t.matmul(
                                out=pst[:, col:col + 65],
                                lhsT=ptb[:, k * 128:(k + 1) * 128],
                                rhs=qT[:, (h * 16 + 15) * 66:(h * 16 + 15) * 66 + 65],
                                start=False, stop=(nt in (13, 15)))
                        inst.then_inc(ch["av"].sem, 1)
                        pef(t, 0.0145)

                for grp in range(8):
                    block.tensor(lambda t, grp=grp: _p2_grp(t, grp))
                    if fq["vec"]:
                        block.vector(lambda v: vef(v, 0.03))
                ch["p2"].n += 32
                ch["av"].n += 2

                def _p2_act(sc, p20=p20, exp0=exp0, av0=av0):
                    for half in range(2):
                        for J in range(16):
                            it = half * 16 + J
                            sc.wait_ge(ch["p2"].sem, p20 + it + 1)
                            if it >= 2:
                                sc.wait_ge(ch["av"].sem, av0)
                            PSa = PS2 if it % 2 == 0 else PS3
                            ptb = pt0 if it % 2 == 0 else pt1
                            sc.activation(out=ptb[:, :], in_=PSa[:, :], func=AF.Exp,
                                          bias=cneg[:, :], scale=8.0).then_inc(
                                              ch["exp"].sem, 1)
                ch["exp"].n += 32
                block.scalar(_p2_act)

                div0 = ch["divd"].n

                if STG < 6:
                    continue_div = False
                else:
                    continue_div = True

                def _div_v(v, h=h, av0=av0):
                    for half in range(2):
                        v.wait_ge(ch["av"].sem, av0 + 2)
                        for k in range(8):
                            nt = half * 8 + k
                            pst, col = slot_map[nt]
                            inst = v.reciprocal(out=rcz[:, nt:nt + 1],
                                                in_=pst[:, col + 64:col + 65])
                        vedge(v, inst)
                        for k in range(8):
                            nt = half * 8 + k
                            pst, col = slot_map[nt]
                            inst = v.tensor_scalar(
                                out=restm[:, nt * 128 + h * 64:nt * 128 + h * 64 + 64],
                                in0=pst[:, col:col + 64],
                                scalar1=rcz[:, nt:nt + 1], scalar2=None,
                                op0=ALU.mult)
                        inst.then_inc(ch["divd"].sem, 1)
                        vef(v, 0.1)
                if continue_div:
                    ch["divd"].n += 2
                    block.vector(_div_v)

        def emit_attn_res(s, p):
            rst0 = ch["rest"].n
            divn = ch["divd"].n

            def _res_pe(t):
                t.wait_ge(ch["divd"].sem, divn)
                for bb in range(4):
                    if bb >= 2:
                        t.wait_ge(ch["resf"].sem, ch["resf"].n + bb - 1)
                    pb = PS0B[:, (bb % 2) * 1024:(bb % 2) * 1024 + 512]
                    for k in range(4):
                        nt = bb * 4 + k
                        inst = t.transpose(out=pb[:, k * 128:(k + 1) * 128],
                                           in_=restm[:, nt * 128:(nt + 1) * 128],
                                           identity=identb[:, :])
                    inst.then_inc(ch["rest"].sem, 1)
                    pef(t, 0.003)
            ch["rest"].n += 4
            block.tensor(_res_pe)

            def _res_v(v):
                for bb in range(4):
                    v.wait_ge(ch["rest"].sem, rst0 + bb + 1)
                    pb = PS0B[:, (bb % 2) * 1024:(bb % 2) * 1024 + 512]
                    v.tensor_copy(out=resf[:, p * 2048 + bb * 512:p * 2048 + (bb + 1) * 512],
                                  in_=pb[:, 0:512]).then_inc(ch["resf"].sem, 1)
                vef(v, 0.08)
            ch["resf"].n += 4
            block.vector(_res_v)

        seq_silu = {}
        cnt = {"wps": 0}

        from collections import deque
        fq = {"pe": deque(), "vec": deque(), "act": deque()}
        pace_d = {"pe": 0.0, "vec": 0.0, "act": 0.0}
        pace_t = {"pe": 0.0, "vec": 0.0, "act": 0.0}
        pair_budget = {"pe": 0.0, "vec": 0.0, "act": 0.0}

        emt = {"h": 0, "gy": 0, "hs": 0, "gys": 0, "gact": 0}

        def _fill(qn, eng, frac):
            pace_t[qn] += frac * pair_budget[qn]
            q = fq[qn]
            eager = qn == "vec"
            while q and (eager or pace_d[qn] + q[0][0] * 0.5 <= pace_t[qn]):
                c, fn, reqs, prov = q[0]
                if any(emt[k] < v for k, v in reqs):
                    break
                q.popleft()
                fn(eng)
                emt[prov] += 1
                pace_d[qn] += c

        def pef(t, frac):
            _fill("pe", t, frac)

        def vef(v, frac):
            _fill("vec", v, frac)

        def aef(sc, frac):
            _fill("act", sc, frac)

        def pace_reset():
            for qn in fq:
                pair_budget[qn] = sum(e[0] for e in fq[qn]) / 8.5
                pace_d[qn] = 0.0
                pace_t[qn] = 0.0

        def drain_all():
            qs = {"pe": block.tensor, "vec": block.vector, "act": block.scalar}
            while any(fq[qn] for qn in qs):
                progress = False
                for qn, emit in qs.items():
                    runnable = []
                    while fq[qn]:
                        c, fn, reqs, prov = fq[qn][0]
                        if any(emt[k] < v for k, v in reqs):
                            break
                        fq[qn].popleft()
                        runnable.append(fn)
                        emt[prov] += 1
                    if runnable:
                        progress = True

                        def _sec(e, runnable=runnable):
                            for fn in runnable:
                                fn(e)
                        emit(_sec)
                assert progress, "ffn queue drain deadlock"

        def plan_ffn_chunk(s, ci, defer_dma=None):
            tc, half = divmod(ci, 2)
            fb = PS1[:, 512:1024]
            w1P = [ch["lw1A"], ch["lw1B"]]
            w2P = [ch["lw2A"], ch["lw2B"]]
            M0 = ch["h"].n
            G0 = ch["gy"].n
            gy_now = ch["gy"].n
            gys_now = ch["gys"].n
            sg_now = ch["sg"].n
            gact0 = ch["gact"].n
            lnf_now = ch["lnf"].n

            w1_wait = []
            for k in range(16):
                par = (M0 + k) % 2
                w1P[par].n += 16
                w1_wait.append(w1P[par].n)
            w2_wait = []
            if half == 1:
                for dt in range(8):
                    w2P[0].n += 16
                    a = w2P[0].n
                    w2P[1].n += 16
                    w2_wait.append((a, w2P[1].n))

            yc_now = ch["yc"].n

            def _dma(g, M0=M0, G0=G0, tc=tc, half=half, ci=ci, yc_now=yc_now):
                for k in range(16):
                    M = M0 + k
                    if M >= 2:
                        g.wait_ge(ch["h"].sem, M - 1)
                    g.dma_start(out=(w1s0 if M % 2 == 0 else w1s1)[:, :],
                                in_=bw1t_d[half * 16 + k]).then_inc(w1P[M % 2].sem, 16)
                if half == 1:
                    for dt in range(8):
                        if G0 + dt >= 1:
                            g.wait_ge(ch["gyh"].sem, G0 + dt)
                        g.dma_start(out=w2s0[:, :],
                                    in_=bw2t_d[dt][:, 0:2048]).then_inc(w2P[0].sem, 16)
                        if G0 + dt >= 1:
                            g.wait_ge(ch["gy"].sem, G0 + dt)
                        g.dma_start(out=w2s1[:, :],
                                    in_=bw2t_d[dt][:, 2048:4096]).then_inc(w2P[1].sem, 16)
            if defer_dma is None:
                block.gpsimd(_dma)
            else:
                defer_dma.append(_dma)

            for k in range(16):
                def _h(t, k=k, M0=M0, tc=tc, half=half, ci=ci,
                       w1_wait=w1_wait, gys_now=gys_now, lnf_now=lnf_now):
                    M = M0 + k
                    par = M % 2
                    t.wait_ge(w1P[par].sem, w1_wait[k])
                    if ci == 0 and k == 0:
                        t.wait_ge(ch["lnf"].sem, lnf_now)
                    if half == 0 and k == 0:
                        t.wait_ge(ch["gys"].sem, gys_now)
                    if M >= 1:
                        t.wait_ge(ch["hs"].sem, M)
                    w1b = w1s0 if par == 0 else w1s1
                    for dc in range(8):
                        inst = t.matmul(
                            out=fb, lhsT=w1b[:, dc * 128:(dc + 1) * 128],
                            rhs=xnf[:, dc * 2048 + tc * 512:dc * 2048 + (tc + 1) * 512],
                            start=(dc == 0), stop=(dc == 7))
                    inst.then_inc(ch["h"].sem, 1)
                fq["pe"].append((1700, _h, [("hs", M0 + k)], "h"))
            if half == 1:
                for dt in range(8):
                    def _g(t, dt=dt, M0=M0, G0=G0, tc=tc,
                           w2_wait=w2_wait, gact0=gact0):
                        G = G0 + dt
                        t.wait_ge(w2P[0].sem, w2_wait[dt][0])
                        if dt == 0:
                            t.wait_ge(ch["gact"].sem, gact0 + 1)
                            t.wait_ge(ch["hs"].sem, M0 + 16)
                        else:
                            t.wait_ge(ch["gys"].sem, G)
                        for kk in range(32):
                            if kk == 16:
                                t.wait_ge(w2P[1].sem, w2_wait[dt][1])
                            w2b = w2s0 if kk < 16 else w2s1
                            inst = t.matmul(out=fb,
                                            lhsT=w2b[:, (kk % 16) * 128:(kk % 16 + 1) * 128],
                                            rhs=hsb[:, kk * 512:(kk + 1) * 512],
                                            start=(kk == 0), stop=(kk == 31))
                            if kk == 15:
                                inst.then_inc(ch["gyh"].sem, 1)
                        inst.then_inc(ch["gy"].sem, 1)
                    fq["pe"].append(
                        (6800, _g,
                         [("gact", gact0 + 1), ("hs", M0 + 16)] if dt == 0
                         else [("gys", G0 + dt)], "gy"))
            ch["h"].n += 16
            if half == 1:
                ch["gy"].n += 8
                ch["gyh"].n += 8

            for k in range(16):
                def _e(v, k=k, M0=M0, half=half, gy_now=gy_now):
                    M = M0 + k
                    v.wait_ge(ch["h"].sem, M + 1)
                    if half == 0 and k == 0:
                        v.wait_ge(ch["gy"].sem, gy_now)
                    ht = half * 16 + k
                    v.tensor_scalar(out=hsb[:, ht * 512:(ht + 1) * 512], in0=fb,
                                    scalar1=b1c[:, ht:ht + 1], scalar2=None,
                                    op0=ALU.add).then_inc(ch["hs"].sem, 1)
                fq["vec"].append((250, _e, [("h", M0 + k + 1)], "hs"))
            if half == 1:
                for dt in range(8):
                    def _ge(v, dt=dt, G0=G0, tc=tc, ci=ci, s=s, sg_now=sg_now):
                        G = G0 + dt
                        v.wait_ge(ch["gy"].sem, G + 1)
                        if s > 0 and ci == 1 and dt == 0:
                            v.wait_ge(ch["sg"].sem, sg_now)
                        v.tensor_scalar(out=gfm[:, tc * 4096 + dt * 512:
                                                 tc * 4096 + (dt + 1) * 512],
                                        in0=fb, scalar1=b2c[:, dt:dt + 1], scalar2=None,
                                        op0=ALU.add).then_inc(ch["gys"].sem, 1)
                    fq["vec"].append((250, _ge, [("gy", G0 + dt + 1)], "gys"))
            ch["hs"].n += 16
            if half == 1:
                ch["gys"].n += 8

            def _gelu(sc, M0=M0, half=half):
                sc.wait_ge(ch["hs"].sem, M0 + 16)
                sc.activation(out=hsb[:, half * 8192:(half + 1) * 8192],
                              in_=hsb[:, half * 8192:(half + 1) * 8192],
                              func=AF.Gelu, bias=0.0, scale=1.0).then_inc(
                    ch["gact"].sem, 1)
            fq["act"].append((7100, _gelu, [("hs", M0 + 16)], "gact"))
            ch["gact"].n += 1
            if half == 1:
                def _silu(sc, G0=G0, tc=tc):
                    sc.wait_ge(ch["gys"].sem, G0 + 8)
                    sc.activation(out=gfm[:, tc * 4096:(tc + 1) * 4096],
                                  in_=gfm[:, tc * 4096:(tc + 1) * 4096],
                                  func=AF.Silu, bias=0.0, scale=1.0).then_inc(
                        ch["gact"].sem, 1)
                fq["act"].append((3700, _silu, [("gys", G0 + 8)], "gact"))
                ch["gact"].n += 1
                seq_silu[(s, tc)] = ch["gact"].n

        def emit_combine_a(s):
            lwpP = [ch["lwpA"], ch["lwpB"]]
            ouP = [ch["ooutA"], ch["ooutB"]]
            resf_end = ch["resf"].n
            divd_end = ch["divd"].n
            exp_end = ch["exp"].n
            c = {"s": s, "Y0": ch["yc"].n, "W0": cnt["wps"], "sg0": ch["sg"].n,
                 "sev0": ch["sev"].n, "ost0": ch["ost"].n,
                 "silu_all": seq_silu[(s, 3)], "ou0": [ouP[0].n, ouP[1].n]}
            Y0, W0, sg0, ost0 = c["Y0"], c["W0"], c["sg0"], c["ost0"]
            h_now = ch["h"].n

            wps_wait = []
            for j in range(32):
                par = (W0 + j) % 2
                lwpP[par].n += 16
                wps_wait.append(lwpP[par].n)
            cnt["wps"] += 32
            lxrP = [ch["lxrA"], ch["lxrB"]]
            xres_ready = []
            for j in range(32):
                lxrP[j % 2].n += 16
                xres_ready.append(lxrP[j % 2].n)
            c["xres_ready"] = xres_ready
            c["ou_abs"] = [c["ou0"][j % 2] + 16 * (j // 2 + 1) for j in range(32)]
            YPS = [PS2[:, 0:512], PS2[:, 512:1024], PS3[:, 0:512], PS3[:, 512:1024]]
            c["YPS"] = YPS

            def _dma(g):
                for j in range(32):
                    dt = j % 8
                    Wg = W0 + j
                    if Wg >= 2:
                        g.wait_ge(ch["yc"].sem, Wg - 1)
                    g.dma_start(out=(wps0 if Wg % 2 == 0 else wps1)[:, :],
                                in_=wpt_d[dt]).then_inc(lwpP[Wg % 2].sem, 16)
            block.gpsimd(_dma)

            def _xld(sy, j):
                tc, dt = divmod(j, 8)
                sy.dma_start(
                    out=(xrj0 if j % 2 == 0 else xrj1)[:, :]
                    .rearrange("p (k d) -> p k d", d=128),
                    in_=x_in[s, tc * 512:(tc + 1) * 512, dt * 128:(dt + 1) * 128]
                    .rearrange("(k p) d -> p k d", p=128)
                    ).then_inc(lxrP[j % 2].sem, 16)
            c["xld"] = _xld

            def _sgT(t, j):
                Yg = Y0 + j
                t.wait_ge(ch["sg"].sem, sg0 + j + 1)
                if Yg >= 2:
                    t.wait_ge(ch["ost"].sem, Yg - 1)
                for k in range(4):
                    inst = t.transpose(
                        out=PS0[:, (j % 2) * 512 + k * 128:(j % 2) * 512 + (k + 1) * 128],
                        in_=sgscr[:, k * 128:(k + 1) * 128], identity=identf[:, :])
                inst.then_inc(ch["sgt"].sem, 1)

            def _pe(t):
                t.wait_ge(ch["divd"].sem, divd_end)
                t.wait_ge(ch["exp"].sem, exp_end)
                t.wait_ge(ch["resf"].sem, resf_end)
                for j in range(32):
                    tc, dt = divmod(j, 8)
                    Yg = Y0 + j
                    par = (W0 + j) % 2
                    t.wait_ge(lwpP[par].sem, wps_wait[j])
                    if Yg >= 4:
                        t.wait_ge(ch["sev"].sem, Yg - 3)
                    buf = wps0 if par == 0 else wps1
                    for dc in range(8):
                        inst = t.matmul(
                            out=YPS[j % 4],
                            lhsT=buf[:, dc * 128:(dc + 1) * 128],
                            rhs=resf[:, dc * 2048 + tc * 512:dc * 2048 + (tc + 1) * 512],
                            start=(dc == 0), stop=(dc == 7))
                    inst.then_inc(ch["yc"].sem, 1)
                    if j >= 1:
                        _sgT(t, j - 1)
                _sgT(t, 31)
            ch["yc"].n += 32
            ch["sgt"].n += 32
            block.tensor(_pe)

            def _act(sc):
                for j in range(32):
                    dt = j % 8
                    Yg = Y0 + j
                    sc.wait_ge(ch["yc"].sem, Yg + 1)
                    if Yg >= 2:
                        sc.wait_ge(ch["sg"].sem, Yg - 1)
                    sc.activation(out=(sscr if j % 2 == 0 else gscr)[:, :],
                                  in_=YPS[j % 4],
                                  func=AF.Identity, bias=bpc[:, dt:dt + 1],
                                  scale=1.0).then_inc(ch["sev"].sem, 1)
            ch["sev"].n += 32
            block.scalar(_act)
            return c

        def emit_combine_b(c):
            s = c["s"]
            ouP = [ch["ooutA"], ch["ooutB"]]
            lxrP = [ch["lxrA"], ch["lxrB"]]
            Y0, sg0, sev0, ost0 = c["Y0"], c["sg0"], c["sev0"], c["ost0"]

            def _vec(v):
                for j in range(32):
                    tc, dt = divmod(j, 8)
                    Yg = Y0 + j
                    v.wait_ge(ch["sev"].sem, sev0 + j + 1)
                    if j == 0:
                        v.wait_ge(ch["gact"].sem, c["silu_all"])
                    if Yg >= 1:
                        v.wait_ge(ch["sgt"].sem, Yg)
                    v.tensor_tensor(out=sgscr[:, :],
                                    in0=gfm[:, tc * 4096 + dt * 512:
                                            tc * 4096 + (dt + 1) * 512],
                                    in1=(sscr if j % 2 == 0 else gscr)[:, :],
                                    op=ALU.mult).then_inc(ch["sg"].sem, 1)
                    v.wait_ge(ch["sgt"].sem, Yg + 1)
                    v.wait_ge(lxrP[j % 2].sem, c["xres_ready"][j])
                    if j >= 2:
                        v.wait_ge(ouP[j % 2].sem, c["ou_abs"][j - 2])
                    else:
                        v.wait_ge(ouP[j % 2].sem, c["ou0"][j % 2])
                    stg = sgst0 if j % 2 == 0 else sgst1
                    v.tensor_tensor(
                        out=stg[:, :].rearrange("p (k d) -> p k d", d=128),
                        in0=PS0[:, (j % 2) * 512:(j % 2) * 512 + 512]
                        .rearrange("p (k d) -> p k d", d=128),
                        in1=(xrj0 if j % 2 == 0 else xrj1)[:, :]
                        .rearrange("p (k d) -> p k d", d=128),
                        op=ALU.add).then_inc(ch["ost"].sem, 1)
            ch["sg"].n += 32
            ch["ost"].n += 32
            block.vector(_vec)

            def _out(sy):
                sy.wait_ge(ch["ost"].sem, ost0)
                c["xld"](sy, 0)
                c["xld"](sy, 1)
                for j in range(32):
                    tc, dt = divmod(j, 8)
                    sy.wait_ge(ch["ost"].sem, ost0 + j + 1)
                    if j + 2 < 32:
                        c["xld"](sy, j + 2)
                    stg = sgst0 if j % 2 == 0 else sgst1
                    sy.dma_start(
                        out=out_d[s, tc * 512:(tc + 1) * 512,
                                  dt * 128:(dt + 1) * 128]
                        .rearrange("(k p) d -> p k d", p=128),
                        in_=stg[:, :].rearrange("p (k d) -> p k d", d=128)
                        ).then_inc(ouP[j % 2].sem, 16)
            ch["ooutA"].n += 16 * 16
            ch["ooutB"].n += 16 * 16
            block.sync(_out)

        ln_done = set()
        planned = set()
        for s in range(min(SEQ, BI_SEQ)):
            if BI_STAGE >= 1 and s not in ln_done:
                emit_ln(s)
            full = BI_STAGE >= 8 and BI_PAIR >= 8
            if full and s not in planned:
                for ci in range(8):
                    plan_ffn_chunk(s, ci)
                pace_reset()
                planned.add(s)
            if BI_STAGE >= 2:
                emit_qproj_fm(s, 0)
                for p in range(min(8, BI_PAIR)):
                    emit_attn_main(s, p)
                    if p + 1 < min(8, BI_PAIR):
                        emit_qproj_fm(s, p + 1)
                    emit_attn_res(s, p)
            if DBG and s == 0:
                def _dbg0(sy):
                    sy.wait_ge(ch["resf"].sem, ch["resf"].n)
                    sy.dma_start(out=dbg["xnf"], in_=xnf[:, :]).then_inc(ch["oout"].sem, 16)
                    sy.dma_start(out=dbg["resf"], in_=resf[:, :]).then_inc(ch["oout"].sem, 16)
                    sy.dma_start(out=dbg["qA"], in_=qA[:, :]).then_inc(ch["oout"].sem, 16)
                    sy.dma_start(out=dbg["qT"], in_=qT[:, :]).then_inc(ch["oout"].sem, 16)
                    sy.dma_start(out=dbg["mxc"], in_=mxc[:, :]).then_inc(ch["oout"].sem, 16)
                    sy.dma_start(out=dbg["pt"], in_=pt1[:, :]).then_inc(ch["oout"].sem, 16)
                ch["oout"].n += 96
                block.sync(_dbg0)
            if full:
                drain_all()
                boundary = BI_SEQ > SEQ - 1 and s + 1 < min(SEQ, BI_SEQ)
                if boundary:
                    emit_ln(s + 1)
                    ln_done.add(s + 1)
                    dma_defer = []
                    for ci in range(8):
                        plan_ffn_chunk(s + 1, ci, defer_dma=dma_defer)
                    pace_reset()
                    planned.add(s + 1)
                cctx = emit_combine_a(s)
                if boundary:
                    for f in dma_defer:
                        block.gpsimd(f)
                emit_combine_b(cctx)
                if boundary:
                    for _ in range(24):
                        block.tensor(lambda t: pef(t, 0.08))
                        block.vector(lambda v: vef(v, 0.07))
            if DBG and s == 0:
                def _dbg1(sy):
                    sy.wait_ge(ch["oout"].sem, ch["oout"].n)
                    sy.dma_start(out=dbg["hsb"], in_=hsb[:, :]).then_inc(ch["oout"].sem, 16)
                ch["oout"].n += 16
                block.sync(_dbg1)

        def _final(sy):
            sy.wait_ge(ch["ooutA"].sem, ch["ooutA"].n)
            sy.wait_ge(ch["ooutB"].sem, ch["ooutB"].n)
        block.sync(_final)

    return nc


def kernel(L0_attn_x, L0_attn_Wq, L0_attn_Wp, L0_attn_bp,
           L0_ffn_w1, L0_ffn_b1, L0_ffn_w2, L0_ffn_b2):
    x = np.ascontiguousarray(np.asarray(L0_attn_x, dtype=np.float32))
    wq = np.ascontiguousarray(np.asarray(L0_attn_Wq, dtype=np.float32))
    wp = np.ascontiguousarray(np.asarray(L0_attn_Wp, dtype=np.float32))
    bp = np.ascontiguousarray(np.asarray(L0_attn_bp, dtype=np.float32))
    w1 = np.ascontiguousarray(np.asarray(L0_ffn_w1, dtype=np.float32))
    b1 = np.ascontiguousarray(np.asarray(L0_ffn_b1, dtype=np.float32))
    w2 = np.ascontiguousarray(np.asarray(L0_ffn_w2, dtype=np.float32))
    b2 = np.ascontiguousarray(np.asarray(L0_ffn_b2, dtype=np.float32))

    nc = build_nc()
    in_maps = []
    for c in range(NCORE):
        in_maps.append({
            "x": np.ascontiguousarray(x[c * SEQ:(c + 1) * SEQ]),
            "wq": wq, "wp": wp, "bp": bp, "w1": w1, "b1": b1, "w2": w2, "b2": b2,
        })
    res = run_bass_kernel_spmd(nc, in_maps, core_ids=list(range(NCORE)))
    out = np.concatenate([np.asarray(r["out"]) for r in res.results], axis=0)
    if DBG:
        kernel.dbg = res.results
    return out


# version 32
# speedup vs baseline: 1.0169x; 1.0064x over previous
import numpy as np
import concourse.bass as bass
import concourse.mybir as mybir
from concourse.bass_utils import run_bass_kernel_spmd

F32 = mybir.dt.float32
BF16 = mybir.dt.bfloat16
I32 = mybir.dt.int32
AF = mybir.ActivationFunctionType
AX = mybir.AxisListType
ALU = mybir.AluOpType

NCORE = 8
SEQ = 4
N = 2048
D = 1024
NH = 16
E = 64
HF = 4096
TC = 512
DBG = False
import os
BI_SEQ = int(os.environ.get("BI_SEQ", "4"))
BI_PAIR = int(os.environ.get("BI_PAIR", "8"))
BI_STAGE = int(os.environ.get("BI_STAGE", "9"))
BI_PREP = int(os.environ.get("BI_PREP", "9"))
NOALT = int(os.environ.get("NOALT", "0"))
QPN = int(os.environ.get("QPN", "4"))
P1M = int(os.environ.get("P1M", "2"))
W1N = int(os.environ.get("W1N", "32"))
W2N = int(os.environ.get("W2N", "8"))
WPN = int(os.environ.get("WPN", "8"))
BI_PREP = int(os.environ.get("BI_PREP", "9"))
NOALT = int(os.environ.get("NOALT", "0"))
QPN = int(os.environ.get("QPN", "4"))
P1M = int(os.environ.get("P1M", "2"))


class Ch:

    def __init__(self, es, nc, name):
        self.sem = es.enter_context(nc.semaphore(name))
        self.n = 0

    def inc(self, inst, k=1):
        inst.then_inc(self.sem, k)
        self.n += k
        return self.n


def build_nc():
    import contextlib
    nc = bass.Bass("TRN2", target_bir_lowering=False)

    x_in = nc.dram_tensor("x", [SEQ, N, D], F32, kind="ExternalInput").ap()
    wq_in = nc.dram_tensor("wq", [NH, E, E], F32, kind="ExternalInput").ap()
    wp_in = nc.dram_tensor("wp", [D, D], F32, kind="ExternalInput").ap()
    bp_in = nc.dram_tensor("bp", [D], F32, kind="ExternalInput").ap()
    w1_in = nc.dram_tensor("w1", [HF, D], F32, kind="ExternalInput").ap()
    b1_in = nc.dram_tensor("b1", [HF], F32, kind="ExternalInput").ap()
    w2_in = nc.dram_tensor("w2", [D, HF], F32, kind="ExternalInput").ap()
    b2_in = nc.dram_tensor("b2", [D], F32, kind="ExternalInput").ap()
    out_d = nc.dram_tensor("out", [SEQ, N, D], F32, kind="ExternalOutput").ap()

    bw1t_d = nc.dram_tensor("bw1t_d", [HF // 128, 128, 1024], BF16, kind="Internal").ap()
    bw2t_d = nc.dram_tensor("bw2t_d", [D // 128, 128, 4096], BF16, kind="Internal").ap()
    wpt_d = nc.dram_tensor("wpt_d", [8, 128, 1024], BF16, kind="Internal").ap()

    dbg = {}
    if DBG:
        dbg["xnf"] = nc.dram_tensor("dbg_xnf", [128, 16384], BF16, kind="ExternalOutput").ap()
        dbg["resf"] = nc.dram_tensor("dbg_resf", [128, 16384], BF16, kind="ExternalOutput").ap()
        dbg["qA"] = nc.dram_tensor("dbg_qA", [128, 2048], BF16, kind="ExternalOutput").ap()
        dbg["qT"] = nc.dram_tensor("dbg_qT", [128, 2112], BF16, kind="ExternalOutput").ap()
        dbg["mxc"] = nc.dram_tensor("dbg_mxc", [128, 32], F32, kind="ExternalOutput").ap()
        dbg["pt"] = nc.dram_tensor("dbg_pt", [128, 1024], BF16, kind="ExternalOutput").ap()
        dbg["bw1t"] = nc.dram_tensor("dbg_bw1t", [128, 1024], BF16, kind="ExternalOutput").ap()
        dbg["hsb"] = nc.dram_tensor("dbg_hsb", [128, 16384], BF16, kind="ExternalOutput").ap()

    es = contextlib.ExitStack()
    with es:
        EC = es.enter_context
        xnf = EC(nc.sbuf_tensor("xnf", [128, 16384], BF16))
        resf = EC(nc.sbuf_tensor("resf", [128, 16384], BF16))
        gfm = EC(nc.sbuf_tensor("gfm", [128, 16384], BF16))
        wqt = EC(nc.sbuf_tensor("wqt", [128, 512], BF16))
        bpc = EC(nc.sbuf_tensor("bpc", [128, 8], F32))
        b1c = EC(nc.sbuf_tensor("b1c", [128, 32], F32))
        b2c = EC(nc.sbuf_tensor("b2c", [128, 8], F32))
        identb = EC(nc.sbuf_tensor("identb", [128, 128], BF16))
        identf = EC(nc.sbuf_tensor("identf", [128, 128], F32))
        PS0 = EC(nc.psum_tensor("PS0", [128, 1024], F32))
        PS1 = EC(nc.psum_tensor("PS1", [128, 1024], F32))
        PS2 = EC(nc.psum_tensor("PS2", [128, 1024], F32))
        PS3 = EC(nc.psum_tensor("PS3", [128, 1024], F32))
        PS2B = PS2.bitcast(BF16)
        PS3B = PS3.bitcast(BF16)
        PS0B = PS0.bitcast(BF16)
        PS1B = PS1.bitcast(BF16)

        block = EC(nc.Block())

        ch = {n: Ch(es, nc, n) for n in [
            "ldp", "bwout", "qp", "s1", "mx", "mxrow", "p2", "exp", "av",
            "divd", "rest", "resf", "lh", "lw1", "lw2", "lxr", "h", "hs",
            "gy", "gys", "sg", "sgt", "ost", "oout", "lnx", "lnst", "lnt",
            "lnf", "misc", "mrow", "vch", "ach", "ldpA", "ldpB", "lnxA", "lnxB",
            "lw1A", "lw1B", "lw2A", "lw2B", "ooutA", "ooutB",
            "yc", "sev", "gyh", "gact", "lwpA", "lwpB", "lxrA", "lxrB", "qg"]}

        def vedge(v, inst):
            inst.then_inc(ch["vch"].sem, 1)
            ch["vch"].n += 1
            v.wait_ge(ch["vch"].sem, ch["vch"].n)

        def aedge(a, inst):
            inst.then_inc(ch["ach"].sem, 1)
            ch["ach"].n += 1
            a.wait_ge(ch["ach"].sem, ch["ach"].n)

        with nc.sbuf_tensor("pstage0", [128, 4096], F32) as pstage0, \
             nc.sbuf_tensor("pstage1", [128, 4096], F32) as pstage1, \
             nc.sbuf_tensor("psgn", [128, 4096], BF16) as psgn, \
             nc.sbuf_tensor("psgn2", [128, 4096], BF16) as psgn2, \
             nc.sbuf_tensor("ptout", [128, 4096], BF16) as ptout, \
             nc.sbuf_tensor("palph", [128, 4], F32) as palph, \
             nc.sbuf_tensor("prow", [128, 128], F32) as prow, \
             nc.sbuf_tensor("iscr", [128, 128], I32) as iscr:

            pstages = [pstage0, pstage1]

            def _prep_gpsimd(g):
                g.iota(iscr[:, :], pattern=[[1, 128]], base=0,
                       channel_multiplier=-1).then_inc(ch["misc"].sem, 1)
            ch["misc"].n += 1
            block.gpsimd(_prep_gpsimd)

            m_iota = ch["misc"].n

            def _prep_vec0(v):
                v.wait_ge(ch["misc"].sem, m_iota)
                v.tensor_single_scalar(out=identb[:, :], in_=iscr[:, :], scalar=0,
                                       op=ALU.is_equal)
                v.tensor_single_scalar(out=identf[:, :], in_=iscr[:, :], scalar=0,
                                       op=ALU.is_equal).then_inc(ch["misc"].sem, 1)
            ch["misc"].n += 1
            block.vector(_prep_vec0)
            m_ident = ch["misc"].n
            m_biaspe_g = [m_ident]

            if BI_PREP >= 2:
                def _prep_bias_sync(s):
                    s.dma_start(out=prow[0:8, 0:128], in_=bp_in.rearrange("(a b) -> a b", b=128)).then_inc(ch["ldp"].sem, 16)
                    s.wait_ge(ch["misc"].sem, 1)
                    s.dma_start(out=prow[64:72, 0:128], in_=b2_in.rearrange("(a b) -> a b", b=128)).then_inc(ch["ldp"].sem, 16)
                    s.wait_ge(ch["misc"].sem, 1)
                    s.dma_start(out=prow[32:64, 0:128], in_=b1_in.rearrange("(a b) -> a b", b=128)).then_inc(ch["ldp"].sem, 16)
                    s.wait_ge(ch["misc"].sem, 1)
                ch["ldp"].n += 48
                block.sync(_prep_bias_sync)

                ldp_bias = ch["ldp"].n

                def _prep_bias_pe(t):
                    t.wait_ge(ch["ldp"].sem, ldp_bias)
                    t.wait_ge(ch["misc"].sem, m_ident)
                    t.matmul(out=PS0[:, 0:8], lhsT=prow[0:8, 0:128],
                             rhs=identf[0:8, 0:8], start=True, stop=True)
                    t.matmul(out=PS0[:, 8:16], lhsT=prow[64:72, 0:128],
                             rhs=identf[64:72, 64:72], start=True, stop=True)
                    t.matmul(out=PS1[:, 0:32], lhsT=prow[32:64, 0:128],
                             rhs=identf[32:64, 32:64], start=True,
                             stop=True).then_inc(ch["misc"].sem, 1)
                ch["misc"].n += 1
                block.tensor(_prep_bias_pe)

                m_biaspe = ch["misc"].n
                m_biaspe_g[0] = m_biaspe

                def _prep_bias_v(v):
                    v.wait_ge(ch["misc"].sem, m_biaspe)
                    v.tensor_copy(out=bpc[:, :], in_=PS0[:, 0:8])
                    v.tensor_copy(out=b2c[:, :], in_=PS0[:, 8:16])
                    v.tensor_copy(out=b1c[:, :],
                                  in_=PS1[:, 0:32]).then_inc(ch["misc"].sem, 1)
                ch["misc"].n += 1
                block.vector(_prep_bias_v)

            if BI_PREP >= 3:
                wq_bw0s = ch["bwout"].n

                def _prep_wq_sync(s):
                    s.wait_ge(ch["misc"].sem, m_biaspe_g[0])
                    for p in range(8):
                        if p >= 1:
                            s.wait_ge(ch["bwout"].sem, wq_bw0s + p * 16)
                        s.dma_start(out=prow[0:64, 0:64],
                                    in_=wq_in[2 * p]).then_inc(ch["ldp"].sem, 16)
                        s.wait_ge(ch["misc"].sem, 1)
                        s.dma_start(out=prow[0:64, 64:128],
                                    in_=wq_in[2 * p + 1]).then_inc(ch["ldp"].sem, 16)
                        s.wait_ge(ch["misc"].sem, 1)
                        s.wait_ge(ch["mrow"].sem, (p + 1))
                ch["ldp"].n += 8 * 32
                block.sync(_prep_wq_sync)

                wq_bw0 = ch["bwout"].n
                wq_s10 = ch["s1"].n

                def _prep_wq_v(v):
                    base = ch["ldp"].n - 8 * 32
                    for p in range(8):
                        v.wait_ge(ch["ldp"].sem, base + (p + 1) * 32)
                        v.tensor_copy(out=psgn[0:64, 0:128], in_=prow[0:64, 0:128]).then_inc(ch["misc"].sem, 1)
                        ch["misc"].n += 1
                        v.wait_ge(ch["s1"].sem, wq_s10 + 2 * (p + 1))
                        if p >= 1:
                            v.wait_ge(ch["bwout"].sem, wq_bw0 + p * 16)
                        v.tensor_copy(out=wqt[0:64, p * 64:(p + 1) * 64], in_=PS2B[0:64, 0:64])
                        v.tensor_copy(out=prow[0:64, 64:96].bitcast(BF16)[0:64, 0:64],
                                      in_=PS2B[0:64, 128:192]).then_inc(ch["mrow"].sem, 1)
                block.vector(_prep_wq_v)

                def _prep_wq_pe(t):
                    for p in range(8):
                        t.wait_ge(ch["misc"].sem, ch["misc"].n - 8 + (p + 1))
                        t.transpose(out=PS2B[0:64, 0:64], in_=psgn[0:64, 0:64],
                                    identity=identb[0:64, 0:64]).then_inc(ch["s1"].sem, 1)
                        ch["s1"].n += 1
                        t.transpose(out=PS2B[0:64, 128:192], in_=psgn[0:64, 64:128],
                                    identity=identb[0:64, 0:64]).then_inc(ch["s1"].sem, 1)
                        ch["s1"].n += 1
                block.tensor(_prep_wq_pe)

                def _prep_wq_g(g):
                    for p in range(8):
                        g.wait_ge(ch["mrow"].sem, p + 1)
                        g.dma_start(out=wqt[64:128, p * 64:(p + 1) * 64],
                                    in_=prow[0:64, 64:96].bitcast(BF16)[0:64, 0:64]
                                    ).then_inc(ch["bwout"].sem, 16)
                ch["bwout"].n += 8 * 16
                ch["mrow"].n += 8
                block.gpsimd(_prep_wq_g)

            if BI_PREP >= 4:
                mxb0 = ch["mx"].n
                expb0 = ch["exp"].n

                ldP = [ch["ldpA"], ch["ldpB"]]
                base_p = [ldP[0].n, ldP[1].n]

                def wload_sem(g):
                    return ldP[g % 2], base_p[g % 2] + (g // 2 + 1) * 16

                def _prep_w_sync(s):
                    s.wait_ge(ch["bwout"].sem, ch["bwout"].n)
                    for i in range(W1N):
                        if i >= 2:
                            s.wait_ge(ch["mx"].sem, mxb0 + i - 1)
                            s.wait_ge(ch["exp"].sem, expb0 + i - 1)
                        s.dma_start(out=pstages[i % 2][:, 0:1024],
                                    in_=w1_in[i * 128:(i + 1) * 128, :]).then_inc(
                                        ldP[i % 2].sem, 16)
                    for i in range(W2N):
                        g = W1N + i
                        s.wait_ge(ch["mx"].sem, mxb0 + W1N + max(0, i - 1))
                        s.wait_ge(ch["exp"].sem, expb0 + W1N + max(0, i - 1))
                        s.dma_start(out=pstages[g % 2][:, :],
                                    in_=w2_in[i * 128:(i + 1) * 128, :]).then_inc(
                                        ldP[g % 2].sem, 16)
                    for i in range(WPN):
                        g = W1N + W2N + i
                        s.wait_ge(ch["mx"].sem, mxb0 + W1N + W2N + max(0, i - 1))
                        s.dma_start(out=pstages[g % 2][:, 0:1024],
                                    in_=wp_in[i * 128:(i + 1) * 128, :]).then_inc(
                                        ldP[g % 2].sem, 16)
                nw = W1N + W2N + WPN
                ch["ldpA"].n += ((nw + 1) // 2) * 16
                ch["ldpB"].n += (nw // 2) * 16
                block.sync(_prep_w_sync)

                s1b0 = ch["s1"].n
                bwo0 = ch["bwout"].n
                sbase = ch["s1"].n
                p2b0 = ch["p2"].n

                def _prep_w_v(v):
                    n_b = W1N + W2N * 4 + WPN
                    for i in range(W1N):
                        sm, val = wload_sem(i)
                        v.wait_ge(sm.sem, val)
                        if i >= 1:
                            v.wait_ge(ch["mx"].sem, mxb0 + i)
                        vedge(v, v.reduce_sum(out=palph[:, 0:1],
                                               in_=pstages[i % 2][:, 0:1024],
                                               axis=AX.X, apply_absolute_value=True))
                        v.wait_ge(ch["exp"].sem, expb0 + i + 1)
                        if i >= 1:
                            v.wait_ge(ch["s1"].sem, s1b0 + i)
                        v.tensor_scalar(out=ptout[:, 0:1024], in0=psgn2[:, 0:1024],
                                        scalar1=palph[:, 0:1], scalar2=1.0 / 1024,
                                        op0=ALU.mult, op1=ALU.mult).then_inc(ch["mx"].sem, 1)
                        if i >= 1:
                            v.wait_ge(ch["bwout"].sem, bwo0 + (i - 1) * 16)
                            pb = PS2B if ((i - 1) % 2 == 0 or NOALT) else PS3B
                            v.tensor_copy(out=psgn[:, 0:1024],
                                          in_=pb[:, 0:1024]).then_inc(ch["p2"].sem, 1)
                    if W1N >= 1:
                        v.wait_ge(ch["s1"].sem, s1b0 + W1N)
                        v.wait_ge(ch["bwout"].sem, bwo0 + (W1N - 1) * 16)
                        pb = PS2B if ((W1N - 1) % 2 == 0 or NOALT) else PS3B
                        v.tensor_copy(out=psgn[:, 0:1024],
                                      in_=pb[:, 0:1024]).then_inc(ch["p2"].sem, 1)
                    for i in range(W2N):
                        sm, val = wload_sem(W1N + i)
                        v.wait_ge(sm.sem, val)
                        v.wait_ge(ch["mx"].sem, mxb0 + W1N + i)
                        vedge(v, v.reduce_sum(out=palph[:, 0:1],
                                               in_=pstages[i % 2][:, :],
                                               axis=AX.X, apply_absolute_value=True))
                        v.wait_ge(ch["exp"].sem, expb0 + W1N + i + 1)
                        v.tensor_scalar(out=ptout[:, :], in0=psgn2[:, :],
                                        scalar1=palph[:, 0:1], scalar2=1.0 / 4096,
                                        op0=ALU.mult, op1=ALU.mult).then_inc(ch["mx"].sem, 1)
                        v.wait_ge(ch["bwout"].sem, bwo0 + (W1N + i) * 16)
                        for b4 in range(4):
                            idx = W1N + i * 4 + b4
                            v.wait_ge(ch["s1"].sem, sbase + idx + 1)
                            pb = PS2B if idx % 2 == 0 else PS3B
                            v.tensor_copy(out=psgn[:, b4 * 1024:(b4 + 1) * 1024],
                                          in_=pb[:, 0:1024]).then_inc(ch["p2"].sem, 1)
                    for i in range(WPN):
                        sm, val = wload_sem(W1N + W2N + i)
                        v.wait_ge(sm.sem, val)
                        v.wait_ge(ch["s1"].sem, s1b0 + W1N + W2N * 4 + i)
                        v.tensor_copy(out=ptout[:, 0:1024],
                                      in_=pstages[i % 2][:, 0:1024]).then_inc(ch["mx"].sem, 1)
                        idx = W1N + W2N * 4 + i
                        v.wait_ge(ch["s1"].sem, sbase + idx + 1)
                        v.wait_ge(ch["bwout"].sem, bwo0 + (W1N + W2N + i) * 16)
                        pb = PS2B if idx % 2 == 0 else PS3B
                        v.tensor_copy(out=psgn[:, 0:1024],
                                      in_=pb[:, 0:1024]).then_inc(ch["p2"].sem, 1)
                ch["mx"].n += W1N + W2N + WPN
                ch["p2"].n += W1N + W2N * 4 + WPN
                block.vector(_prep_w_v)

                def _prep_w_act(sc):
                    for i in range(W1N):
                        sm, val = wload_sem(i)
                        sc.wait_ge(sm.sem, val)
                        if i >= 1:
                            sc.wait_ge(ch["mx"].sem, mxb0 + i)
                        sc.activation(out=psgn2[:, 0:1024], in_=pstages[i % 2][:, 0:1024],
                                      func=AF.Sign).then_inc(ch["exp"].sem, 1)
                    for i in range(W2N):
                        sm, val = wload_sem(W1N + i)
                        sc.wait_ge(sm.sem, val)
                        sc.wait_ge(ch["mx"].sem, mxb0 + W1N + i)
                        sc.activation(out=psgn2[:, :], in_=pstages[i % 2][:, :],
                                      func=AF.Sign).then_inc(ch["exp"].sem, 1)
                ch["exp"].n += W1N + W2N
                block.scalar(_prep_w_act)

                def _prep_w_pe(t):
                    cnt = 0
                    for i in range(W1N):
                        t.wait_ge(ch["mx"].sem, mxb0 + i + 1)
                        if i >= 2 or NOALT:
                            t.wait_ge(ch["p2"].sem, p2b0 + max(0, i - (1 if not NOALT else 0)))
                        pb = PS2B if (i % 2 == 0 or NOALT) else PS3B
                        for dc in range(8):
                            inst = t.transpose(out=pb[:, dc * 128:(dc + 1) * 128],
                                               in_=ptout[:, dc * 128:(dc + 1) * 128],
                                               identity=identb[:, :])
                        inst.then_inc(ch["s1"].sem, 1)
                        cnt += 1
                    for i in range(W2N):
                        t.wait_ge(ch["mx"].sem, mxb0 + W1N + i + 1)
                        for b4 in range(4):
                            idx = i * 4 + b4
                            if W1N + idx >= 2:
                                t.wait_ge(ch["p2"].sem, p2b0 + W1N + idx - 1)
                            pb = PS2B if (32 + idx) % 2 == 0 else PS3B
                            for k in range(8):
                                ht = b4 * 8 + k
                                inst = t.transpose(out=pb[:, k * 128:(k + 1) * 128],
                                                   in_=ptout[:, ht * 128:(ht + 1) * 128],
                                                   identity=identb[:, :])
                            inst.then_inc(ch["s1"].sem, 1)
                            cnt += 1
                    for i in range(WPN):
                        t.wait_ge(ch["mx"].sem, mxb0 + W1N + W2N + i + 1)
                        idx = W1N + W2N * 4 + i
                        t.wait_ge(ch["p2"].sem, p2b0 + idx - 1)
                        pb = PS2B if idx % 2 == 0 else PS3B
                        for dc in range(8):
                            inst = t.transpose(out=pb[:, dc * 128:(dc + 1) * 128],
                                               in_=ptout[:, dc * 128:(dc + 1) * 128],
                                               identity=identb[:, :])
                        inst.then_inc(ch["s1"].sem, 1)
                        cnt += 1
                n_batches = W1N + W2N * 4 + WPN
                ch["s1"].n += n_batches
                block.tensor(_prep_w_pe)

                def _prep_w_out(g):
                    pbase = ch["p2"].n - n_batches
                    for i in range(W1N):
                        g.wait_ge(ch["p2"].sem, pbase + i + 1)
                        g.dma_start(out=bw1t_d[i], in_=psgn[:, 0:1024]).then_inc(ch["bwout"].sem, 16)
                    for i in range(W2N):
                        g.wait_ge(ch["p2"].sem, pbase + W1N + (i + 1) * 4)
                        g.dma_start(out=bw2t_d[i], in_=psgn[:, :]).then_inc(ch["bwout"].sem, 16)
                    for i in range(WPN):
                        g.wait_ge(ch["p2"].sem, pbase + W1N + W2N * 4 + i + 1)
                        g.dma_start(out=wpt_d[i], in_=psgn[:, 0:1024]).then_inc(ch["bwout"].sem, 16)
                ch["bwout"].n += (W1N + W2N + WPN) * 16
                block.gpsimd(_prep_w_out)

            if DBG:
                def _dbg_bw(s):
                    s.wait_ge(ch["bwout"].sem, ch["bwout"].n)
                    s.dma_start(out=dbg["bw1t"], in_=bw1t_d[0]).then_inc(ch["oout"].sem, 16)
                ch["oout"].n += 16
                block.sync(_dbg_bw)

        qA = EC(nc.sbuf_tensor("qA", [128, 2048], BF16))
        qB = EC(nc.sbuf_tensor("qB", [128, 2048], BF16))
        qC = EC(nc.sbuf_tensor("qC", [128, 2048], BF16))
        qD = EC(nc.sbuf_tensor("qD", [128, 2048], BF16))
        qE = EC(nc.sbuf_tensor("qE", [128, 2048], BF16))
        qT = EC(nc.sbuf_tensor("qT", [128, 2112], BF16))
        qsub = EC(nc.sbuf_tensor("qsub", [128, 512], BF16))
        pt0 = EC(nc.sbuf_tensor("pt0", [128, 1024], BF16))
        pt1 = EC(nc.sbuf_tensor("pt1", [128, 1024], BF16))
        subm = EC(nc.sbuf_tensor("subm", [128, 32], F32))
        diag = EC(nc.sbuf_tensor("diag", [128, 32], F32))
        mxc = EC(nc.sbuf_tensor("mxc", [128, 32], F32))
        tmpc = EC(nc.sbuf_tensor("tmpc", [128, 8], F32))
        rcz = EC(nc.sbuf_tensor("rcz", [128, 16], F32))
        cneg = EC(nc.sbuf_tensor("cneg", [128, 1], F32))
        xst0 = EC(nc.sbuf_tensor("xst0", [128, 1024], F32))
        xst1 = EC(nc.sbuf_tensor("xst1", [128, 1024], F32))
        sqscr = EC(nc.sbuf_tensor("sqscr", [128, 1024], F32))
        lns = EC(nc.sbuf_tensor("lns", [128, 8], F32))
        xnstage = EC(nc.sbuf_tensor("xnstage", [128, 1024], BF16))
        hsb = EC(nc.sbuf_tensor("hsb", [128, 16384], BF16))
        w1s0 = EC(nc.sbuf_tensor("w1s0", [128, 1024], BF16))
        w1s1 = EC(nc.sbuf_tensor("w1s1", [128, 1024], BF16))
        w2s0 = EC(nc.sbuf_tensor("w2s0", [128, 2048], BF16))
        w2s1 = EC(nc.sbuf_tensor("w2s1", [128, 2048], BF16))
        rtm = EC(nc.sbuf_tensor("rtm", [128, 2048], BF16))
        gscr = EC(nc.sbuf_tensor("gscr", [128, 512], F32))
        sscr = EC(nc.sbuf_tensor("sscr", [128, 512], F32))
        sgscr = EC(nc.sbuf_tensor("sgscr", [128, 512], F32))
        xrj0 = EC(nc.sbuf_tensor("xrj0", [128, 512], F32))
        xrj1 = EC(nc.sbuf_tensor("xrj1", [128, 512], F32))
        wps0 = EC(nc.sbuf_tensor("wps0", [128, 1024], BF16))
        wps1 = EC(nc.sbuf_tensor("wps1", [128, 1024], BF16))
        sgst0 = EC(nc.sbuf_tensor("sgst0", [128, 512], F32))
        sgst1 = EC(nc.sbuf_tensor("sgst1", [128, 512], F32))
        mxst = sqscr.bitcast(BF16)
        restm = rtm[:, :]

        prep_done = {"bwout": ch["bwout"].n, "p2": ch["p2"].n, "mrow": ch["mrow"].n,
                     "misc": ch["misc"].n}

        def _barrier(e):
            e.wait_ge(ch["bwout"].sem, prep_done["bwout"])
            e.wait_ge(ch["p2"].sem, prep_done["p2"])
        for engname in ["tensor", "vector", "scalar", "gpsimd", "sync"]:
            getattr(block, engname)(_barrier)

        def _post_prep(v):
            if BI_PREP >= 5:
                v.memset(qA[64:96, :], 0.0)
                v.memset(qB[64:96, :], 0.0)
                vedge(v, v.memset(qD[64:96, :], 0.0))
                vedge(v, v.memset(qE[64:96, :], 0.0))
                v.memset(qA[64:65, :], 1.0)
                v.memset(qD[64:65, :], 1.0)
                v.memset(qT[:, :].rearrange("p (s c) -> p s c", c=66)[:, :, 64:65], 1.0)
                v.memset(qT[:, :].rearrange("p (s c) -> p s c", c=66)[:, :, 65:66], 0.0)
                v.memset(lns[:, 7:8], 1e-5)
                v.memset(cneg[:, :], -32.0)
            v.memset(tmpc[:, 0:8], 0.0).then_inc(ch["qp"].sem, 1)
        ch["qp"].n += 1
        block.vector(_post_prep)



        def emit_ln(s):
            xstg = [xst0, xst1]

            lnP = [ch["lnxA"], ch["lnxB"]]
            lnb = [lnP[0].n, lnP[1].n]

            def _sync(sy):
                for tt in range(16):
                    if tt >= 2:
                        sy.wait_ge(ch["lnst"].sem, ch["lnst"].n + tt - 1)
                    sy.dma_start(out=xstg[tt % 2][:, :],
                                 in_=x_in[s, tt * 128:(tt + 1) * 128, :]
                                 ).then_inc(lnP[tt % 2].sem, 16)
            ch["lnxA"].n += 8 * 16
            ch["lnxB"].n += 8 * 16
            block.sync(_sync)

            lnt0 = ch["lnt"].n
            lnf0 = ch["lnf"].n
            lnst0 = ch["lnst"].n
            s1b = ch["s1"].n
            h_prev = ch["h"].n

            def _act(sc):
                for tt in range(16):
                    sc.wait_ge(lnP[tt % 2].sem, lnb[tt % 2] + (tt // 2 + 1) * 16)
                    if tt >= 1:
                        sc.wait_ge(ch["lnst"].sem, lnst0 + tt)
                    sc.activation(out=sqscr[:, :], in_=xstg[tt % 2][:, :], func=AF.Square,
                                  accum_out=lns[:, 1:2]).then_inc(ch["lnt"].sem, 1)
                    sc.wait_ge(ch["lnf"].sem, lnf0 + 2 * tt + 1)
                    sc.activation(out=lns[:, 5:6], in_=lns[:, 4:5], func=AF.Sqrt,
                                  bias=lns[:, 7:8], scale=1.0).then_inc(ch["lnt"].sem, 1)
            ch["lnt"].n += 32
            block.scalar(_act)

            def _vec(v):
                for tt in range(16):
                    v.wait_ge(lnP[tt % 2].sem, lnb[tt % 2] + (tt // 2 + 1) * 16)
                    vedge(v, v.reduce_sum(out=lns[:, 0:1], in_=xstg[tt % 2][:, :],
                                           axis=AX.X))
                    vedge(v, v.tensor_scalar(out=lns[:, 2:3], in0=lns[:, 0:1],
                                             scalar1=-1.0 / 1024,
                                             scalar2=None, op0=ALU.mult))
                    vedge(v, v.tensor_mul(out=lns[:, 3:4], in0=lns[:, 2:3],
                                          in1=lns[:, 2:3]))
                    v.wait_ge(ch["lnt"].sem, lnt0 + 2 * tt + 1)
                    v.scalar_tensor_tensor(out=lns[:, 4:5], in0=lns[:, 1:2],
                                           scalar=1.0 / 1024, in1=lns[:, 3:4],
                                           op0=ALU.mult, op1=ALU.subtract
                                           ).then_inc(ch["lnf"].sem, 1)
                    v.wait_ge(ch["lnt"].sem, lnt0 + 2 * tt + 2)
                    vedge(v, v.reciprocal(out=lns[:, 6:7], in_=lns[:, 5:6]))
                    if tt >= 1:
                        v.wait_ge(ch["s1"].sem, s1b + tt)
                    v.tensor_scalar(out=xnstage[:, :], in0=xstg[tt % 2][:, :],
                                    scalar1=lns[:, 2:3], scalar2=lns[:, 6:7],
                                    op0=ALU.add, op1=ALU.mult).then_inc(ch["lnst"].sem, 1)
                    if tt == 0:
                        v.wait_ge(ch["h"].sem, h_prev)
                    v.wait_ge(ch["s1"].sem, s1b + tt + 1)
                    v.tensor_copy(
                        out=xnf[:, :].rearrange("p (dc t) -> p dc t", t=2048)
                        [:, :, tt * 128:(tt + 1) * 128],
                        in_=PS1B[:, 1024:2048].rearrange("p (dc t) -> p dc t", t=128)
                        ).then_inc(ch["lnf"].sem, 1)
            ch["lnf"].n += 32
            ch["lnst"].n += 16
            block.vector(_vec)

            hs_prev = ch["hs"].n
            gys_prev = ch["gys"].n

            def _pe(t):
                t.wait_ge(ch["hs"].sem, hs_prev)
                t.wait_ge(ch["gys"].sem, gys_prev)
                for tt in range(16):
                    t.wait_ge(ch["lnst"].sem, lnst0 + tt + 1)
                    if tt >= 1:
                        t.wait_ge(ch["lnf"].sem, lnf0 + 2 * tt)
                    for dc in range(8):
                        inst = t.transpose(out=PS1B[:, 1024 + dc * 128:1024 + (dc + 1) * 128],
                                           in_=xnstage[:, dc * 128:(dc + 1) * 128],
                                           identity=identb[:, :])
                    inst.then_inc(ch["s1"].sem, 1)
            ch["s1"].n += 16
            block.tensor(_pe)

        pair_ctx = {}

        def emit_qproj_fm(s, p):
            qp0 = ch["qp"].n
            sev_prev = ch["sev"].n
            pair_ctx[p] = qp0

            def _fm_pe(t):
                t.wait_ge(ch["lnf"].sem, ch["lnf"].n)
                if p == 0:
                    t.wait_ge(ch["sev"].sem, sev_prev)
                for half in range(2):
                    if half == 1:
                        t.wait_ge(ch["qp"].sem, qp0 + 5)
                    for c in range(2):
                        t.matmul(out=PS2[0:64, c * 512:(c + 1) * 512],
                                 lhsT=wqt[0:64, p * 64:(p + 1) * 64],
                                 rhs=xnf[0:64, p * 2048 + half * 1024 + c * 512:
                                         p * 2048 + half * 1024 + (c + 1) * 512],
                                 start=True, stop=True, tile_position=(0, 0))
                        t.matmul(out=PS2[64:128, c * 512:(c + 1) * 512],
                                 lhsT=wqt[64:128, p * 64:(p + 1) * 64],
                                 rhs=xnf[64:128, p * 2048 + half * 1024 + c * 512:
                                         p * 2048 + half * 1024 + (c + 1) * 512],
                                 start=True, stop=True, tile_position=(64, 64))
                        inst = t.matmul(out=PS3[0:64, c * 512:(c + 1) * 512],
                                        lhsT=wqt[64:128, p * 64:(p + 1) * 64],
                                        rhs=xnf[64:128, p * 2048 + half * 1024 + c * 512:
                                                p * 2048 + half * 1024 + (c + 1) * 512],
                                        start=True, stop=True, tile_position=(64, 0))
                    inst.then_inc(ch["p2"].sem, 1)
                    pef(t, 0.003)
            p2q = ch["p2"].n
            ch["p2"].n += 2
            block.tensor(_fm_pe)

            def _fm_ev(sc):
                for half in range(2):
                    sc.wait_ge(ch["p2"].sem, p2q + half + 1)
                    hs = slice(half * 1024, (half + 1) * 1024)
                    sc.activation(out=qA[0:64, hs], in_=PS2[0:64, 0:1024],
                                  func=AF.Identity, bias=0.0, scale=1.0)
                    sc.activation(out=qC[64:128, hs], in_=PS2[64:128, 0:1024],
                                  func=AF.Identity, bias=0.0,
                                  scale=1.0).then_inc(ch["qp"].sem, 2)
                    sc.activation(out=qB[0:64, hs], in_=PS2[0:64, 0:1024],
                                  func=AF.Identity, bias=0.0, scale=1.0)
                    sc.activation(out=qD[0:64, hs], in_=PS3[0:64, 0:1024],
                                  func=AF.Identity, bias=0.0, scale=1.0)
                    sc.activation(out=qE[0:64, hs], in_=PS3[0:64, 0:1024],
                                  func=AF.Identity, bias=0.0,
                                  scale=1.0).then_inc(ch["qp"].sem, 3)
            ch["qp"].n += 10
            block.scalar(_fm_ev)

            def _fm_gather(v):
                v.wait_ge(ch["qp"].sem, qp0 + 7)
                v.tensor_copy(out=qsub[0:64, :],
                              in_=qA[0:64, :].rearrange("p (a b) -> p a b", b=4)
                              [:, :, 0])
                v.tensor_copy(out=qsub[64:128, :],
                              in_=qC[64:128, :].rearrange("p (a b) -> p a b", b=4)
                              [:, :, 0]).then_inc(ch["qg"].sem, 1)
                vef(v, 0.05)
            ch["qg"].n += 1
            block.vector(_fm_gather)

        def emit_attn_main(s, p):
            STG = BI_STAGE
            qp0 = pair_ctx[p]
            if fq["act"]:
                def _af(sc):
                    aef(sc, 1.0)
                block.scalar(_af)

            resf_prev = ch["resf"].n
            ost_prev = ch["ost"].n

            def _qproj_tm(t):
                for bb in range(2):
                    if bb == 0:
                        t.wait_ge(ch["resf"].sem, resf_prev)
                        if p == 0:
                            t.wait_ge(ch["ost"].sem, ost_prev)
                    t.wait_ge(ch["qp"].sem, qp0 + 10 + (0 if bb == 0 else 1))
                    for k in range(8):
                        nt = bb * 8 + k
                        t.transpose(out=PS0B[:, k * 64:(k + 1) * 64],
                                    in_=qA[0:64, nt * 128:(nt + 1) * 128],
                                    identity=identb[0:64, 0:64])
                        inst = t.transpose(out=PS1B[:, k * 64:(k + 1) * 64],
                                           in_=qC[64:128, nt * 128:(nt + 1) * 128],
                                           identity=identb[64:128, 64:128])
                    inst.then_inc(ch["p2"].sem, 1)
                    pef(t, 0.003)
            p2q = ch["p2"].n
            ch["p2"].n += 2
            block.tensor(_qproj_tm)

            def _qproj_ev(v):
                for bb in range(2 if QPN >= 3 else 0):
                    v.wait_ge(ch["p2"].sem, p2q + 1 + bb)
                    v.tensor_copy(
                        out=qT[:, :].rearrange("p (ss c) -> p ss c", c=66)
                        [:, bb * 8:(bb + 1) * 8, 0:64],
                        in_=PS0B[:, 0:512].rearrange("p (k e) -> p k e", e=64))
                    v.tensor_copy(
                        out=qT[:, :].rearrange("p (ss c) -> p ss c", c=66)
                        [:, 16 + bb * 8:16 + (bb + 1) * 8, 0:64],
                        in_=PS1B[:, 0:512].rearrange("p (k e) -> p k e", e=64)
                        ).then_inc(ch["qp"].sem, 1)
                if QPN >= 4:
                    v.wait_ge(ch["qp"].sem, qp0 + 12)
                    vedge(v, v.tensor_tensor(
                        out=sqscr[:, :].rearrange("p (ss e) -> p ss e", e=64),
                        in0=qT[:, :].rearrange("p (ss c) -> p ss c", c=66)[:, 0:16, 0:64],
                        in1=qT[:, :].rearrange("p (ss c) -> p ss c", c=66)[:, 0:16, 0:64],
                        op=ALU.mult))
                    vedge(v, v.reduce_sum(
                        out=diag[:, 0:16],
                        in_=sqscr[:, :].rearrange("p (ss e) -> p ss e", e=64), axis=AX.X))
                    vedge(v, v.tensor_tensor(
                        out=sqscr[:, :].rearrange("p (ss e) -> p ss e", e=64),
                        in0=qT[:, :].rearrange("p (ss c) -> p ss c", c=66)[:, 16:32, 0:64],
                        in1=qT[:, :].rearrange("p (ss c) -> p ss c", c=66)[:, 16:32, 0:64],
                        op=ALU.mult))
                    v.reduce_sum(out=diag[:, 16:32],
                                 in_=sqscr[:, :].rearrange("p (ss e) -> p ss e", e=64),
                                 axis=AX.X).then_inc(ch["qp"].sem, 1)
                    vef(v, 0.05)
            ch["qp"].n += 3
            block.vector(_qproj_ev)

            if STG < 3:
                return
            s10 = ch["s1"].n
            mx0 = ch["mx"].n

            qp_all = ch["qp"].n

            P1R = [PS0, PS2, PS3]

            qg_all = ch["qg"].n

            def _p1_t(t, nt):
                if nt == 0:
                    t.wait_ge(ch["qp"].sem, qp_all - 6)
                    t.wait_ge(ch["qg"].sem, qg_all)
                if nt == 2:
                    t.wait_ge(ch["qp"].sem, qp_all - 1)
                PSe = P1R[(nt + 1) % 3]
                if nt >= 3:
                    t.wait_ge(ch["mx"].sem, mx0 + (nt - 2) * 2)
                t.matmul(out=PSe[0:64, 0:512],
                         lhsT=qA[0:64, nt * 128:nt * 128 + 64],
                         rhs=qsub[0:64, :], start=True, stop=True,
                         tile_position=(0, 0))
                t.matmul(out=PSe[64:128, 0:512],
                         lhsT=qA[0:64, nt * 128 + 64:nt * 128 + 128],
                         rhs=qsub[0:64, :], start=True, stop=True,
                         tile_position=(0, 64))
                t.matmul(out=PSe[0:64, 512:1024],
                         lhsT=qC[64:128, nt * 128:nt * 128 + 64],
                         rhs=qsub[64:128, :], start=True, stop=True,
                         tile_position=(64, 0))
                mm = t.matmul(out=PSe[64:128, 512:1024],
                              lhsT=qC[64:128, nt * 128 + 64:nt * 128 + 128],
                              rhs=qsub[64:128, :], start=True, stop=True,
                              tile_position=(64, 64))
                mm.then_inc(ch["s1"].sem, 1)
                pef(t, 0.003)

            def _p1_vv(v, nt):
                PSe = P1R[(nt + 1) % 3]
                v.wait_ge(ch["s1"].sem, s10 + nt + 1)
                v.reduce_max(out=subm[:, nt:nt + 1], in_=PSe[:, 0:512], axis=AX.X)
                inst = v.reduce_max(out=subm[:, 16 + nt:17 + nt],
                                    in_=PSe[:, 512:1024], axis=AX.X)
                inst.then_inc(ch["mx"].sem, 2)
                if nt == 15:
                    v.wait_ge(ch["mx"].sem, mx0 + 32)
                    v.wait_ge(ch["qp"].sem, qp_all)
                    v.tensor_max(out=mxc[:, :], in0=subm[:, :],
                                 in1=diag[:, :]).then_inc(ch["mx"].sem, 1)

            for nt in range(16):
                block.tensor(lambda t, nt=nt: _p1_t(t, nt))
                block.vector(lambda v, nt=nt: _p1_vv(v, nt))
            ch["s1"].n += 16
            ch["mx"].n += 33

            if STG < 4:
                return
            mrow0 = ch["mrow"].n

            def _mx_pe(t):
                t.wait_ge(ch["mx"].sem, ch["mx"].n)
                for h in range(2):
                    for half in range(2):
                        PSx = PS2 if h == 0 else PS3
                        r = 64 * half
                        for c in range(8):
                            nt = half * 8 + c
                            inst = t.matmul(out=PSx[r:r + 1, c * 128:(c + 1) * 128],
                                            lhsT=mxc[:, h * 16 + nt:h * 16 + nt + 1],
                                            rhs=identf[:, :], start=(c in (0, 4)),
                                            stop=(c in (3, 7)),
                                            tile_position=(0, r))
                        inst.then_inc(ch["p2"].sem, 1)
                        pef(t, 0.005)
            ch["p2"].n += 4
            block.tensor(_mx_pe)

            p2m = ch["p2"].n - 4

            mxrow_prev = ch["mxrow"].n

            def _mx_v(v):
                v.wait_ge(ch["p2"].sem, p2m + 4)
                for i in range(4):
                    h, half = divmod(i, 2)
                    PSx = PS2 if h == 0 else PS3
                    r = 64 * half
                    if i == 0:
                        v.wait_ge(ch["mxrow"].sem, mxrow_prev)
                    if i == 2:
                        v.wait_ge(ch["mxrow"].sem, mxrow_prev + 16)
                    v.tensor_scalar(out=mxst[0:1, half * 1024:(half + 1) * 1024],
                                    in0=PSx[r:r + 1, 0:1024], scalar1=-1.0, scalar2=None,
                                    op0=ALU.mult).then_inc(ch["mrow"].sem, 1)
                vef(v, 0.03)
            ch["mrow"].n += 4
            block.vector(_mx_v)

            def _mx_g(g):
                g.wait_ge(ch["mrow"].sem, mrow0 + 2)
                g.dma_start(out=qB[64:65, :], in_=mxst[0:1, :]).then_inc(ch["mxrow"].sem, 16)
                g.wait_ge(ch["mrow"].sem, mrow0 + 4)
                g.dma_start(out=qE[64:65, :], in_=mxst[0:1, :]).then_inc(ch["mxrow"].sem, 16)
            ch["mxrow"].n += 32
            block.sync(_mx_g)

            if STG < 5:
                return
            slot_map = []
            for nt in range(16):
                if nt < 7:
                    slot_map.append((PS0, nt * 66))
                elif nt < 14:
                    slot_map.append((PS0, 512 + (nt - 7) * 66))
                else:
                    slot_map.append((PS1, (nt - 14) * 66))

            for h in range(2):
                qlhs = qA if h == 0 else qD
                qrhs = qB if h == 0 else qE
                p20 = ch["p2"].n
                exp0 = ch["exp"].n
                av0 = ch["av"].n

                def _p2_grp(t, grp, h=h, qlhs=qlhs, qrhs=qrhs, p20=p20,
                            exp0=exp0, av0=av0):
                    if grp == 0:
                        t.wait_ge(ch["mxrow"].sem, ch["mxrow"].n - 16 * (1 - h))
                        if h == 0:
                            t.wait_ge(ch["mx"].sem, ch["mx"].n)
                    for it in range(grp * 4, grp * 4 + 4):
                        half, J = divmod(it, 16)
                        PSa = PS2 if it % 2 == 0 else PS3
                        if it >= 2:
                            t.wait_ge(ch["exp"].sem, exp0 + it - 1)
                        t.matmul(out=PSa[:, 0:512],
                                 lhsT=qlhs[0:96, J * 128:(J + 1) * 128],
                                 rhs=qrhs[0:96, half * 1024:half * 1024 + 512],
                                 start=True, stop=True)
                        t.matmul(out=PSa[:, 512:1024],
                                 lhsT=qlhs[0:96, J * 128:(J + 1) * 128],
                                 rhs=qrhs[0:96, half * 1024 + 512:half * 1024 + 1024],
                                 start=True, stop=True).then_inc(ch["p2"].sem, 1)
                        pef(t, 0.0145)
                        if it >= 1:
                            Jp = (it - 1) % 16
                            halfp = (it - 1) // 16
                            t.wait_ge(ch["exp"].sem, exp0 + it)
                            if h == 1 and it == 1 and BI_STAGE >= 6:
                                t.wait_ge(ch["divd"].sem, ch["divd"].n)
                            ptb = pt0 if (it - 1) % 2 == 0 else pt1
                            for k in range(8):
                                nt = halfp * 8 + k
                                pst, col = slot_map[nt]
                                inst = t.matmul(
                                    out=pst[:, col:col + 65],
                                    lhsT=ptb[:, k * 128:(k + 1) * 128],
                                    rhs=qT[:, (h * 16 + Jp) * 66:(h * 16 + Jp) * 66 + 65],
                                    start=(Jp == 0 and nt in (0, 7, 8, 14)),
                                    stop=(Jp == 15 and nt in (6, 7, 13, 15)))
                            if Jp == 15:
                                inst.then_inc(ch["av"].sem, 1)
                    if grp == 7:
                        t.wait_ge(ch["exp"].sem, exp0 + 32)
                        ptb = pt1
                        for k in range(8):
                            nt = 8 + k
                            pst, col = slot_map[nt]
                            inst = t.matmul(
                                out=pst[:, col:col + 65],
                                lhsT=ptb[:, k * 128:(k + 1) * 128],
                                rhs=qT[:, (h * 16 + 15) * 66:(h * 16 + 15) * 66 + 65],
                                start=False, stop=(nt in (13, 15)))
                        inst.then_inc(ch["av"].sem, 1)
                        pef(t, 0.0145)

                for grp in range(8):
                    block.tensor(lambda t, grp=grp: _p2_grp(t, grp))
                    if fq["vec"]:
                        block.vector(lambda v: vef(v, 0.03))
                ch["p2"].n += 32
                ch["av"].n += 2

                def _p2_act(sc, p20=p20, exp0=exp0, av0=av0):
                    for half in range(2):
                        for J in range(16):
                            it = half * 16 + J
                            sc.wait_ge(ch["p2"].sem, p20 + it + 1)
                            if it >= 2:
                                sc.wait_ge(ch["av"].sem, av0)
                            PSa = PS2 if it % 2 == 0 else PS3
                            ptb = pt0 if it % 2 == 0 else pt1
                            sc.activation(out=ptb[:, :], in_=PSa[:, :], func=AF.Exp,
                                          bias=cneg[:, :], scale=8.0).then_inc(
                                              ch["exp"].sem, 1)
                ch["exp"].n += 32
                block.scalar(_p2_act)

                div0 = ch["divd"].n

                if STG < 6:
                    continue_div = False
                else:
                    continue_div = True

                def _div_v(v, h=h, av0=av0):
                    v.wait_ge(ch["av"].sem, av0 + 2)
                    v.tensor_copy(
                        out=diag[:, 0:7].rearrange("p (a c) -> p a c", c=1),
                        in_=PS0[:, 0:462].rearrange("p (a c) -> p a c", c=66)
                        [:, :, 64:65])
                    v.tensor_copy(
                        out=diag[:, 7:14].rearrange("p (a c) -> p a c", c=1),
                        in_=PS0[:, 512:974].rearrange("p (a c) -> p a c", c=66)
                        [:, :, 64:65])
                    inst = v.tensor_copy(
                        out=diag[:, 14:16].rearrange("p (a c) -> p a c", c=1),
                        in_=PS1[:, 0:132].rearrange("p (a c) -> p a c", c=66)
                        [:, :, 64:65])
                    vedge(v, inst)
                    vedge(v, v.reciprocal(out=rcz[:, 0:16], in_=diag[:, 0:16]))
                    for nt in range(16):
                        pst, col = slot_map[nt]
                        inst = v.tensor_scalar(
                            out=restm[:, nt * 128 + h * 64:nt * 128 + h * 64 + 64],
                            in0=pst[:, col:col + 64],
                            scalar1=rcz[:, nt:nt + 1], scalar2=None,
                            op0=ALU.mult)
                        if nt in (7, 15):
                            inst.then_inc(ch["divd"].sem, 1)
                    vef(v, 0.1)
                if continue_div:
                    ch["divd"].n += 2
                    block.vector(_div_v)

        def emit_attn_res(s, p):
            rst0 = ch["rest"].n
            divn = ch["divd"].n

            def _res_pe(t):
                t.wait_ge(ch["divd"].sem, divn)
                for bb in range(4):
                    if bb >= 2:
                        t.wait_ge(ch["resf"].sem, ch["resf"].n + bb - 1)
                    pb = PS0B[:, (bb % 2) * 1024:(bb % 2) * 1024 + 512]
                    for k in range(4):
                        nt = bb * 4 + k
                        inst = t.transpose(out=pb[:, k * 128:(k + 1) * 128],
                                           in_=restm[:, nt * 128:(nt + 1) * 128],
                                           identity=identb[:, :])
                    inst.then_inc(ch["rest"].sem, 1)
                    pef(t, 0.003)
            ch["rest"].n += 4
            block.tensor(_res_pe)

            def _res_v(v):
                for bb in range(4):
                    v.wait_ge(ch["rest"].sem, rst0 + bb + 1)
                    pb = PS0B[:, (bb % 2) * 1024:(bb % 2) * 1024 + 512]
                    v.tensor_copy(out=resf[:, p * 2048 + bb * 512:p * 2048 + (bb + 1) * 512],
                                  in_=pb[:, 0:512]).then_inc(ch["resf"].sem, 1)
                vef(v, 0.08)
            ch["resf"].n += 4
            block.vector(_res_v)

        seq_silu = {}
        cnt = {"wps": 0}

        from collections import deque
        fq = {"pe": deque(), "vec": deque(), "act": deque()}
        pace_d = {"pe": 0.0, "vec": 0.0, "act": 0.0}
        pace_t = {"pe": 0.0, "vec": 0.0, "act": 0.0}
        pair_budget = {"pe": 0.0, "vec": 0.0, "act": 0.0}

        emt = {"h": 0, "gy": 0, "hs": 0, "gys": 0, "gact": 0}

        def _fill(qn, eng, frac):
            pace_t[qn] += frac * pair_budget[qn]
            q = fq[qn]
            eager = qn == "vec"
            while q and (eager or pace_d[qn] + q[0][0] * 0.5 <= pace_t[qn]):
                c, fn, reqs, prov = q[0]
                if any(emt[k] < v for k, v in reqs):
                    break
                q.popleft()
                fn(eng)
                emt[prov] += 1
                pace_d[qn] += c

        def pef(t, frac):
            _fill("pe", t, frac)

        def vef(v, frac):
            _fill("vec", v, frac)

        def aef(sc, frac):
            _fill("act", sc, frac)

        def pace_reset():
            for qn in fq:
                pair_budget[qn] = sum(e[0] for e in fq[qn]) / 8.5
                pace_d[qn] = 0.0
                pace_t[qn] = 0.0

        def drain_all():
            qs = {"pe": block.tensor, "vec": block.vector, "act": block.scalar}
            while any(fq[qn] for qn in qs):
                progress = False
                for qn, emit in qs.items():
                    runnable = []
                    while fq[qn]:
                        c, fn, reqs, prov = fq[qn][0]
                        if any(emt[k] < v for k, v in reqs):
                            break
                        fq[qn].popleft()
                        runnable.append(fn)
                        emt[prov] += 1
                    if runnable:
                        progress = True

                        def _sec(e, runnable=runnable):
                            for fn in runnable:
                                fn(e)
                        emit(_sec)
                assert progress, "ffn queue drain deadlock"

        def plan_ffn_chunk(s, ci, defer_dma=None):
            tc, half = divmod(ci, 2)
            fb = PS1[:, 512:1024]
            w1P = [ch["lw1A"], ch["lw1B"]]
            w2P = [ch["lw2A"], ch["lw2B"]]
            M0 = ch["h"].n
            G0 = ch["gy"].n
            gy_now = ch["gy"].n
            gys_now = ch["gys"].n
            sg_now = ch["sg"].n
            gact0 = ch["gact"].n
            lnf_now = ch["lnf"].n

            w1_wait = []
            for k in range(16):
                par = (M0 + k) % 2
                w1P[par].n += 16
                w1_wait.append(w1P[par].n)
            w2_wait = []
            if half == 1:
                for dt in range(8):
                    w2P[0].n += 16
                    a = w2P[0].n
                    w2P[1].n += 16
                    w2_wait.append((a, w2P[1].n))

            yc_now = ch["yc"].n

            def _dma(g, M0=M0, G0=G0, tc=tc, half=half, ci=ci, yc_now=yc_now):
                for k in range(16):
                    M = M0 + k
                    if M >= 2:
                        g.wait_ge(ch["h"].sem, M - 1)
                    g.dma_start(out=(w1s0 if M % 2 == 0 else w1s1)[:, :],
                                in_=bw1t_d[half * 16 + k]).then_inc(w1P[M % 2].sem, 16)
                if half == 1:
                    for dt in range(8):
                        if G0 + dt >= 1:
                            g.wait_ge(ch["gyh"].sem, G0 + dt)
                        g.dma_start(out=w2s0[:, :],
                                    in_=bw2t_d[dt][:, 0:2048]).then_inc(w2P[0].sem, 16)
                        if G0 + dt >= 1:
                            g.wait_ge(ch["gy"].sem, G0 + dt)
                        g.dma_start(out=w2s1[:, :],
                                    in_=bw2t_d[dt][:, 2048:4096]).then_inc(w2P[1].sem, 16)
            if defer_dma is None:
                block.gpsimd(_dma)
            else:
                defer_dma.append(_dma)

            for k in range(16):
                def _h(t, k=k, M0=M0, tc=tc, half=half, ci=ci,
                       w1_wait=w1_wait, gys_now=gys_now, lnf_now=lnf_now):
                    M = M0 + k
                    par = M % 2
                    t.wait_ge(w1P[par].sem, w1_wait[k])
                    if ci == 0 and k == 0:
                        t.wait_ge(ch["lnf"].sem, lnf_now)
                    if half == 0 and k == 0:
                        t.wait_ge(ch["gys"].sem, gys_now)
                    if M >= 1:
                        t.wait_ge(ch["hs"].sem, M)
                    w1b = w1s0 if par == 0 else w1s1
                    for dc in range(8):
                        inst = t.matmul(
                            out=fb, lhsT=w1b[:, dc * 128:(dc + 1) * 128],
                            rhs=xnf[:, dc * 2048 + tc * 512:dc * 2048 + (tc + 1) * 512],
                            start=(dc == 0), stop=(dc == 7))
                    inst.then_inc(ch["h"].sem, 1)
                fq["pe"].append((1700, _h, [("hs", M0 + k)], "h"))
            if half == 1:
                for dt in range(8):
                    def _g(t, dt=dt, M0=M0, G0=G0, tc=tc,
                           w2_wait=w2_wait, gact0=gact0):
                        G = G0 + dt
                        t.wait_ge(w2P[0].sem, w2_wait[dt][0])
                        if dt == 0:
                            t.wait_ge(ch["gact"].sem, gact0 + 1)
                            t.wait_ge(ch["hs"].sem, M0 + 16)
                        else:
                            t.wait_ge(ch["gys"].sem, G)
                        for kk in range(32):
                            if kk == 16:
                                t.wait_ge(w2P[1].sem, w2_wait[dt][1])
                            w2b = w2s0 if kk < 16 else w2s1
                            inst = t.matmul(out=fb,
                                            lhsT=w2b[:, (kk % 16) * 128:(kk % 16 + 1) * 128],
                                            rhs=hsb[:, kk * 512:(kk + 1) * 512],
                                            start=(kk == 0), stop=(kk == 31))
                            if kk == 15:
                                inst.then_inc(ch["gyh"].sem, 1)
                        inst.then_inc(ch["gy"].sem, 1)
                    fq["pe"].append(
                        (6800, _g,
                         [("gact", gact0 + 1), ("hs", M0 + 16)] if dt == 0
                         else [("gys", G0 + dt)], "gy"))
            ch["h"].n += 16
            if half == 1:
                ch["gy"].n += 8
                ch["gyh"].n += 8

            for k in range(16):
                def _e(v, k=k, M0=M0, half=half, gy_now=gy_now):
                    M = M0 + k
                    v.wait_ge(ch["h"].sem, M + 1)
                    if half == 0 and k == 0:
                        v.wait_ge(ch["gy"].sem, gy_now)
                    ht = half * 16 + k
                    v.tensor_scalar(out=hsb[:, ht * 512:(ht + 1) * 512], in0=fb,
                                    scalar1=b1c[:, ht:ht + 1], scalar2=None,
                                    op0=ALU.add).then_inc(ch["hs"].sem, 1)
                fq["vec"].append((250, _e, [("h", M0 + k + 1)], "hs"))
            if half == 1:
                for dt in range(8):
                    def _ge(v, dt=dt, G0=G0, tc=tc, ci=ci, s=s, sg_now=sg_now):
                        G = G0 + dt
                        v.wait_ge(ch["gy"].sem, G + 1)
                        if s > 0 and ci == 1 and dt == 0:
                            v.wait_ge(ch["sg"].sem, sg_now)
                        v.tensor_scalar(out=gfm[:, tc * 4096 + dt * 512:
                                                 tc * 4096 + (dt + 1) * 512],
                                        in0=fb, scalar1=b2c[:, dt:dt + 1], scalar2=None,
                                        op0=ALU.add).then_inc(ch["gys"].sem, 1)
                    fq["vec"].append((250, _ge, [("gy", G0 + dt + 1)], "gys"))
            ch["hs"].n += 16
            if half == 1:
                ch["gys"].n += 8

            def _gelu(sc, M0=M0, half=half):
                sc.wait_ge(ch["hs"].sem, M0 + 16)
                sc.activation(out=hsb[:, half * 8192:(half + 1) * 8192],
                              in_=hsb[:, half * 8192:(half + 1) * 8192],
                              func=AF.Gelu, bias=0.0, scale=1.0).then_inc(
                    ch["gact"].sem, 1)
            fq["act"].append((7100, _gelu, [("hs", M0 + 16)], "gact"))
            ch["gact"].n += 1
            if half == 1:
                def _silu(sc, G0=G0, tc=tc):
                    sc.wait_ge(ch["gys"].sem, G0 + 8)
                    sc.activation(out=gfm[:, tc * 4096:(tc + 1) * 4096],
                                  in_=gfm[:, tc * 4096:(tc + 1) * 4096],
                                  func=AF.Silu, bias=0.0, scale=1.0).then_inc(
                        ch["gact"].sem, 1)
                fq["act"].append((3700, _silu, [("gys", G0 + 8)], "gact"))
                ch["gact"].n += 1
                seq_silu[(s, tc)] = ch["gact"].n

        def emit_combine_a(s):
            lwpP = [ch["lwpA"], ch["lwpB"]]
            ouP = [ch["ooutA"], ch["ooutB"]]
            resf_end = ch["resf"].n
            divd_end = ch["divd"].n
            exp_end = ch["exp"].n
            c = {"s": s, "Y0": ch["yc"].n, "W0": cnt["wps"], "sg0": ch["sg"].n,
                 "sev0": ch["sev"].n, "ost0": ch["ost"].n,
                 "silu_all": seq_silu[(s, 3)], "ou0": [ouP[0].n, ouP[1].n]}
            Y0, W0, sg0, ost0 = c["Y0"], c["W0"], c["sg0"], c["ost0"]
            h_now = ch["h"].n

            wps_wait = []
            for j in range(32):
                par = (W0 + j) % 2
                lwpP[par].n += 16
                wps_wait.append(lwpP[par].n)
            cnt["wps"] += 32
            lxrP = [ch["lxrA"], ch["lxrB"]]
            xres_ready = []
            for j in range(32):
                lxrP[j % 2].n += 16
                xres_ready.append(lxrP[j % 2].n)
            c["xres_ready"] = xres_ready
            c["ou_abs"] = [c["ou0"][j % 2] + 16 * (j // 2 + 1) for j in range(32)]
            YPS = [PS2[:, 0:512], PS2[:, 512:1024], PS3[:, 0:512], PS3[:, 512:1024]]
            c["YPS"] = YPS

            def _dma(g):
                for j in range(32):
                    dt = j % 8
                    Wg = W0 + j
                    if Wg >= 2:
                        g.wait_ge(ch["yc"].sem, Wg - 1)
                    g.dma_start(out=(wps0 if Wg % 2 == 0 else wps1)[:, :],
                                in_=wpt_d[dt]).then_inc(lwpP[Wg % 2].sem, 16)
            block.gpsimd(_dma)

            def _xld(sy, j):
                tc, dt = divmod(j, 8)
                sy.dma_start(
                    out=(xrj0 if j % 2 == 0 else xrj1)[:, :]
                    .rearrange("p (k d) -> p k d", d=128),
                    in_=x_in[s, tc * 512:(tc + 1) * 512, dt * 128:(dt + 1) * 128]
                    .rearrange("(k p) d -> p k d", p=128)
                    ).then_inc(lxrP[j % 2].sem, 16)
            c["xld"] = _xld

            def _sgT(t, j):
                Yg = Y0 + j
                t.wait_ge(ch["sg"].sem, sg0 + j + 1)
                if Yg >= 2:
                    t.wait_ge(ch["ost"].sem, Yg - 1)
                for k in range(4):
                    inst = t.transpose(
                        out=PS0[:, (j % 2) * 512 + k * 128:(j % 2) * 512 + (k + 1) * 128],
                        in_=sgscr[:, k * 128:(k + 1) * 128], identity=identf[:, :])
                inst.then_inc(ch["sgt"].sem, 1)

            def _pe(t):
                t.wait_ge(ch["divd"].sem, divd_end)
                t.wait_ge(ch["exp"].sem, exp_end)
                t.wait_ge(ch["resf"].sem, resf_end)
                for j in range(32):
                    tc, dt = divmod(j, 8)
                    Yg = Y0 + j
                    par = (W0 + j) % 2
                    t.wait_ge(lwpP[par].sem, wps_wait[j])
                    if Yg >= 4:
                        t.wait_ge(ch["sev"].sem, Yg - 3)
                    buf = wps0 if par == 0 else wps1
                    for dc in range(8):
                        inst = t.matmul(
                            out=YPS[j % 4],
                            lhsT=buf[:, dc * 128:(dc + 1) * 128],
                            rhs=resf[:, dc * 2048 + tc * 512:dc * 2048 + (tc + 1) * 512],
                            start=(dc == 0), stop=(dc == 7))
                    inst.then_inc(ch["yc"].sem, 1)
                    if j >= 1:
                        _sgT(t, j - 1)
                _sgT(t, 31)
            ch["yc"].n += 32
            ch["sgt"].n += 32
            block.tensor(_pe)

            def _act(sc):
                for j in range(32):
                    dt = j % 8
                    Yg = Y0 + j
                    sc.wait_ge(ch["yc"].sem, Yg + 1)
                    if Yg >= 2:
                        sc.wait_ge(ch["sg"].sem, Yg - 1)
                    sc.activation(out=(sscr if j % 2 == 0 else gscr)[:, :],
                                  in_=YPS[j % 4],
                                  func=AF.Identity, bias=bpc[:, dt:dt + 1],
                                  scale=1.0).then_inc(ch["sev"].sem, 1)
            ch["sev"].n += 32
            block.scalar(_act)
            return c

        def emit_combine_b(c):
            s = c["s"]
            ouP = [ch["ooutA"], ch["ooutB"]]
            lxrP = [ch["lxrA"], ch["lxrB"]]
            Y0, sg0, sev0, ost0 = c["Y0"], c["sg0"], c["sev0"], c["ost0"]

            def _vec(v):
                for j in range(32):
                    tc, dt = divmod(j, 8)
                    Yg = Y0 + j
                    v.wait_ge(ch["sev"].sem, sev0 + j + 1)
                    if j == 0:
                        v.wait_ge(ch["gact"].sem, c["silu_all"])
                    if Yg >= 1:
                        v.wait_ge(ch["sgt"].sem, Yg)
                    v.tensor_tensor(out=sgscr[:, :],
                                    in0=gfm[:, tc * 4096 + dt * 512:
                                            tc * 4096 + (dt + 1) * 512],
                                    in1=(sscr if j % 2 == 0 else gscr)[:, :],
                                    op=ALU.mult).then_inc(ch["sg"].sem, 1)
                    v.wait_ge(ch["sgt"].sem, Yg + 1)
                    v.wait_ge(lxrP[j % 2].sem, c["xres_ready"][j])
                    if j >= 2:
                        v.wait_ge(ouP[j % 2].sem, c["ou_abs"][j - 2])
                    else:
                        v.wait_ge(ouP[j % 2].sem, c["ou0"][j % 2])
                    stg = sgst0 if j % 2 == 0 else sgst1
                    v.tensor_tensor(
                        out=stg[:, :].rearrange("p (k d) -> p k d", d=128),
                        in0=PS0[:, (j % 2) * 512:(j % 2) * 512 + 512]
                        .rearrange("p (k d) -> p k d", d=128),
                        in1=(xrj0 if j % 2 == 0 else xrj1)[:, :]
                        .rearrange("p (k d) -> p k d", d=128),
                        op=ALU.add).then_inc(ch["ost"].sem, 1)
            ch["sg"].n += 32
            ch["ost"].n += 32
            block.vector(_vec)

            def _out(sy):
                sy.wait_ge(ch["ost"].sem, ost0)
                c["xld"](sy, 0)
                c["xld"](sy, 1)
                for j in range(32):
                    tc, dt = divmod(j, 8)
                    sy.wait_ge(ch["ost"].sem, ost0 + j + 1)
                    if j + 2 < 32:
                        c["xld"](sy, j + 2)
                    stg = sgst0 if j % 2 == 0 else sgst1
                    sy.dma_start(
                        out=out_d[s, tc * 512:(tc + 1) * 512,
                                  dt * 128:(dt + 1) * 128]
                        .rearrange("(k p) d -> p k d", p=128),
                        in_=stg[:, :].rearrange("p (k d) -> p k d", d=128)
                        ).then_inc(ouP[j % 2].sem, 16)
            ch["ooutA"].n += 16 * 16
            ch["ooutB"].n += 16 * 16
            block.sync(_out)

        ln_done = set()
        planned = set()
        for s in range(min(SEQ, BI_SEQ)):
            if BI_STAGE >= 1 and s not in ln_done:
                emit_ln(s)
            full = BI_STAGE >= 8 and BI_PAIR >= 8
            if full and s not in planned:
                for ci in range(8):
                    plan_ffn_chunk(s, ci)
                pace_reset()
                planned.add(s)
            if BI_STAGE >= 2:
                emit_qproj_fm(s, 0)
                for p in range(min(8, BI_PAIR)):
                    emit_attn_main(s, p)
                    if p + 1 < min(8, BI_PAIR):
                        emit_qproj_fm(s, p + 1)
                    emit_attn_res(s, p)
            if DBG and s == 0:
                def _dbg0(sy):
                    sy.wait_ge(ch["resf"].sem, ch["resf"].n)
                    sy.dma_start(out=dbg["xnf"], in_=xnf[:, :]).then_inc(ch["oout"].sem, 16)
                    sy.dma_start(out=dbg["resf"], in_=resf[:, :]).then_inc(ch["oout"].sem, 16)
                    sy.dma_start(out=dbg["qA"], in_=qA[:, :]).then_inc(ch["oout"].sem, 16)
                    sy.dma_start(out=dbg["qT"], in_=qT[:, :]).then_inc(ch["oout"].sem, 16)
                    sy.dma_start(out=dbg["mxc"], in_=mxc[:, :]).then_inc(ch["oout"].sem, 16)
                    sy.dma_start(out=dbg["pt"], in_=pt1[:, :]).then_inc(ch["oout"].sem, 16)
                ch["oout"].n += 96
                block.sync(_dbg0)
            if full:
                drain_all()
                boundary = BI_SEQ > SEQ - 1 and s + 1 < min(SEQ, BI_SEQ)
                if boundary:
                    emit_ln(s + 1)
                    ln_done.add(s + 1)
                    dma_defer = []
                    for ci in range(8):
                        plan_ffn_chunk(s + 1, ci, defer_dma=dma_defer)
                    pace_reset()
                    planned.add(s + 1)
                cctx = emit_combine_a(s)
                if boundary:
                    for f in dma_defer:
                        block.gpsimd(f)
                emit_combine_b(cctx)
                if boundary:
                    for _ in range(24):
                        block.tensor(lambda t: pef(t, 0.08))
                        block.vector(lambda v: vef(v, 0.07))
            if DBG and s == 0:
                def _dbg1(sy):
                    sy.wait_ge(ch["oout"].sem, ch["oout"].n)
                    sy.dma_start(out=dbg["hsb"], in_=hsb[:, :]).then_inc(ch["oout"].sem, 16)
                ch["oout"].n += 16
                block.sync(_dbg1)

        def _final(sy):
            sy.wait_ge(ch["ooutA"].sem, ch["ooutA"].n)
            sy.wait_ge(ch["ooutB"].sem, ch["ooutB"].n)
        block.sync(_final)

    return nc


def kernel(L0_attn_x, L0_attn_Wq, L0_attn_Wp, L0_attn_bp,
           L0_ffn_w1, L0_ffn_b1, L0_ffn_w2, L0_ffn_b2):
    x = np.ascontiguousarray(np.asarray(L0_attn_x, dtype=np.float32))
    wq = np.ascontiguousarray(np.asarray(L0_attn_Wq, dtype=np.float32))
    wp = np.ascontiguousarray(np.asarray(L0_attn_Wp, dtype=np.float32))
    bp = np.ascontiguousarray(np.asarray(L0_attn_bp, dtype=np.float32))
    w1 = np.ascontiguousarray(np.asarray(L0_ffn_w1, dtype=np.float32))
    b1 = np.ascontiguousarray(np.asarray(L0_ffn_b1, dtype=np.float32))
    w2 = np.ascontiguousarray(np.asarray(L0_ffn_w2, dtype=np.float32))
    b2 = np.ascontiguousarray(np.asarray(L0_ffn_b2, dtype=np.float32))

    nc = build_nc()
    in_maps = []
    for c in range(NCORE):
        in_maps.append({
            "x": np.ascontiguousarray(x[c * SEQ:(c + 1) * SEQ]),
            "wq": wq, "wp": wp, "bp": bp, "w1": w1, "b1": b1, "w2": w2, "b2": b2,
        })
    res = run_bass_kernel_spmd(nc, in_maps, core_ids=list(range(NCORE)))
    out = np.concatenate([np.asarray(r["out"]) for r in res.results], axis=0)
    if DBG:
        kernel.dbg = res.results
    return out
